# version 03
# baseline (speedup 1.0000x reference)
import numpy as np
import ml_dtypes
import concourse.bass as bass
import concourse.mybir as mybir
from concourse.bass_utils import run_bass_kernel_spmd
from contextlib import ExitStack

F32 = mybir.dt.float32
BF16 = mybir.dt.bfloat16
AF = mybir.ActivationFunctionType
ALU = mybir.AluOpType
AX = mybir.AxisListType

D = 1024
T = 2048
E = 2048
NST = 16
RK = 64
TC = 512
NCH = T // TC
SEP = 2
BLK = TC + SEP
SCW = NST * BLK
HEADS = 8
DH = 64
WIN = 512
INNER = 512
HID = 2730
HIDP = 2816
QKV_M = 2560

_TAPS = set()
_PHASE_LIMIT = 99
_FFN_LAYERS = 2
_FFN_WINDOWS = 99
_FFN_STOP = ""
_SERIAL_P3 = True


class S:

    def __init__(self, stack, nc, name):
        self.h = stack.enter_context(nc.semaphore(name))
        self.n = 0

    def inc(self, instr, k=1):
        instr.then_inc(self.h, k)
        self.n += k
        return self.n

    def dma(self, instr):
        return self.inc(instr, 16)

    def tick(self, eng, k=1):
        eng.sem_inc(self.h, k)
        self.n += k
        return self.n

    def wait(self, eng, at=None):
        eng.wait_ge(self.h, self.n if at is None else at)


def dmac(c, sem, eng, out, in_, **kw):
    n0 = c.nc.n_instructions()
    ins = eng.dma_start(out=out, in_=in_, **kw)
    d = c.nc.n_instructions() - n0
    return sem.inc(ins, 16 * d)


def apx(base, off, dims):
    return bass.AP(tensor=base.tensor, offset=base.offset + off,
                   ap=[list(d) for d in dims])


def sb_ap(t):
    return t.ap() if hasattr(t, "ap") else t


def _rotary_tables():
    half_idx = np.arange(0, DH, 2, dtype=np.float32)
    inv_freq = 1.0 / (10000.0 ** (half_idx / DH))
    tt = np.arange(T, dtype=np.float32)
    freqs = tt[:, None] * inv_freq[None, :]
    freqs = np.concatenate([freqs, freqs], axis=-1)
    base = (half_idx + 0.4 * DH) / (1.4 * DH)
    power = (tt - T // 2) / float(WIN // 2)
    scale = base[None, :] ** power[:, None]
    scale = np.concatenate([scale, scale], axis=-1)
    cos, sin = np.cos(freqs), np.sin(freqs)
    sign = np.where(np.arange(DH) < DH // 2, -1.0, 1.0).astype(np.float32)
    cs_q = (cos * scale).astype(np.float32)
    ss_q = (sin * scale * sign[None, :]).astype(np.float32)
    cs_k = (cos / scale).astype(np.float32)
    ss_k = (sin / scale * sign[None, :]).astype(np.float32)

    def fm(v):
        r = np.ascontiguousarray(v.T)
        return np.concatenate([r, r], axis=0)
    return fm(cs_q), fm(ss_q), fm(cs_k), fm(ss_k)


def _maskT_tiles():
    out = np.zeros((8, 128, WIN), np.float32)
    q = np.arange(WIN)
    for s in range(4):
        k_idx = (s * 128 + np.arange(128))[:, None]
        out[s] = (q[None, :] <= k_idx).astype(np.float32)
        out[4 + s] = (q[None, :] >= k_idx).astype(np.float32)
    return np.ascontiguousarray(out.reshape(8 * 128, WIN)).astype(ml_dtypes.bfloat16)


def _host_prep(inputs):
    x = inputs["L0_attn_x"]
    com = {}
    for pref in ("fwd", "bwd"):
        g = lambda n: np.ascontiguousarray(inputs[f"L0_attn_{pref}_{n}"])
        for n in ("in_w", "out_w", "xproj_w", "dt_w", "A_log"):
            com[f"{pref}_{n}"] = g(n)
        for n in ("conv_b", "dt_b", "Dp"):
            com[f"{pref}_{n}"] = np.ascontiguousarray(g(n).reshape(16, 128).T)
        com[f"{pref}_conv_w"] = np.ascontiguousarray(
            g("conv_w").reshape(16, 128, 4).transpose(1, 0, 2))
    perm = np.concatenate([np.arange(32) + 32, np.arange(32)])
    for l in range(2):
        wqkv = inputs["L0_ffn_wqkv"][l]
        wq, wk, wv = wqkv[:, 0:512], wqkv[:, 512:1024], wqkv[:, 1024:1536]
        pidx = (np.arange(INNER).reshape(HEADS, DH)[:, perm]).reshape(-1)
        com[f"l{l}_wqkv"] = np.ascontiguousarray(
            np.concatenate([wq, wq[:, pidx], wk, wk[:, pidx], wv], axis=1))
        com[f"l{l}_wo"] = np.ascontiguousarray(inputs["L0_ffn_wo"][l])
        w1 = inputs["L0_ffn_w1"][l]
        w1p = np.zeros((D, 2 * HIDP), np.float32)
        w1p[:, 0:HID] = w1[:, 0:HID]
        w1p[:, HIDP:HIDP + HID] = w1[:, HID:2 * HID]
        com[f"l{l}_w1p"] = w1p
        b1 = inputs["L0_ffn_b1"][l]
        b1p = np.zeros((2 * HIDP,), np.float32)
        b1p[0:HID] = b1[0:HID]
        b1p[HIDP:HIDP + HID] = b1[HID:2 * HID]
        com[f"l{l}_b1p"] = np.ascontiguousarray(b1p.reshape(44, 128).T)
        w2p = np.zeros((HIDP, D), np.float32)
        w2p[0:HID] = inputs["L0_ffn_w2"][l]
        com[f"l{l}_w2p"] = np.ascontiguousarray(w2p)
        for n in ("b2", "ln_g", "ln_b"):
            com[f"l{l}_{n}"] = np.ascontiguousarray(
                inputs[f"L0_ffn_{n}"][l].reshape(8, 128).T)
        for n in ("q_scale", "k_scale"):
            com[f"l{l}_{n}"] = np.ascontiguousarray(inputs[f"L0_ffn_{n}"][l])
    cs_q, ss_q, cs_k, ss_k = _rotary_tables()
    com["c_rot"] = np.ascontiguousarray(np.concatenate([cs_q, ss_q, cs_k, ss_k], axis=0))
    com["c_maskT"] = _maskT_tiles()
    ident = np.zeros((128, 128), np.float32)
    np.fill_diagonal(ident, 1.0)
    com["c_ident"] = ident.astype(ml_dtypes.bfloat16)
    com["c_ones"] = np.ones((128, 1), np.float32).astype(ml_dtypes.bfloat16)
    sel = np.zeros((128, 32), np.float32)
    sel[0:64, 0] = 1.0
    sel[64:128, 1] = 1.0
    com["c_sel2"] = sel.astype(ml_dtypes.bfloat16)
    com["c_sel2t"] = np.ascontiguousarray(sel.T).astype(ml_dtypes.bfloat16)
    scal = np.array([1e-5, -1.0, -0.5, 0.5, 2.0, -2.0, 64.0, 1.0 / 64], np.float32)
    com["c_scal"] = np.ascontiguousarray(np.broadcast_to(scal, (128, 8)))
    per_core = []
    for b in range(x.shape[0]):
        d = dict(com)
        d["x_t"] = np.ascontiguousarray(x[b].T)
        per_core.append(d)
    return per_core


class Ctx:
    pass


class BlkProxy:

    def __init__(self, c, blk):
        self._c = c
        self._blk = blk
        self.serial = False

    def _wrap(self, name):
        real = getattr(self._blk, name)
        c = self._c

        def deco(fn):
            if not self.serial:
                return real(fn)
            sems = [c.mm, c.vv, c.sv, c.gg, c.qW, c.qWb, c.qWc, c.qWd,
                    c.qG, c.qS, c.qV, c.qP, c.qY]
            marks = [(sm, sm.n) for sm in sems]

            def wrapped(eng):
                for sm, n in marks:
                    eng.wait_ge(sm.h, n)
                return fn(eng)
            return real(wrapped)
        return deco

    @property
    def sync(self):
        return self._wrap("sync")

    @property
    def vector(self):
        return self._wrap("vector")

    @property
    def scalar(self):
        return self._wrap("scalar")

    @property
    def tensor(self):
        return self._wrap("tensor")

    @property
    def gpsimd(self):
        return self._wrap("gpsimd")


def _tap_defs():
    return {
        "xln": ([D, T], BF16), "xln_rev": ([D, T], BF16),
        "fwd_xi": ([E, T], BF16), "fwd_z": ([E, T], BF16),
        "fwd_bc": ([32, T], F32), "fwd_yg": ([E, T], BF16),
        "bwd_yg": ([E, T], BF16), "x2": ([D, T], F32), "xf": ([D, T], F32),
        "fwd_delta": ([E, T], F32), "fwd_yr": ([E, T], F32),
        "l0_qrot": ([INNER, T], BF16), "l0_krot": ([INNER, T], BF16),
        "l0_ao": ([INNER, T], BF16), "l0_xa": ([D, T], F32),
        "l0_xin": ([D, T], BF16), "l0_xf": ([D, T], F32),
    }


def build_program():
    nc = bass.Bass("TRN2", target_bir_lowering=False)
    stack = ExitStack()
    c = Ctx()
    c.nc, c.stack = nc, stack
    di = {}

    def din(name, shape, dt=F32):
        di[name] = nc.dram_tensor(name, list(shape), dt, kind="ExternalInput").ap()
    din("x_t", [D, T])
    for pref in ("fwd", "bwd"):
        din(f"{pref}_in_w", [D, 2 * E])
        din(f"{pref}_out_w", [E, D])
        din(f"{pref}_xproj_w", [E, RK + 2 * NST])
        din(f"{pref}_dt_w", [RK, E])
        din(f"{pref}_conv_w", [128, 16, 4])
        din(f"{pref}_conv_b", [128, 16])
        din(f"{pref}_dt_b", [128, 16])
        din(f"{pref}_A_log", [E, NST])
        din(f"{pref}_Dp", [128, 16])
    for l in range(2):
        din(f"l{l}_wqkv", [D, QKV_M])
        din(f"l{l}_wo", [INNER, D])
        din(f"l{l}_w1p", [D, 2 * HIDP])
        din(f"l{l}_b1p", [128, 44])
        din(f"l{l}_w2p", [HIDP, D])
        din(f"l{l}_b2", [128, 8])
        din(f"l{l}_ln_g", [128, 8])
        din(f"l{l}_ln_b", [128, 8])
        din(f"l{l}_q_scale", [DH])
        din(f"l{l}_k_scale", [DH])
    din("c_rot", [4 * 128, T])
    din("c_maskT", [8 * 128, WIN], BF16)
    din("c_ident", [128, 128], BF16)
    din("c_ones", [128, 1], BF16)
    din("c_sel2", [128, 32], BF16)
    din("c_sel2t", [32, 128], BF16)
    din("c_scal", [128, 8])
    c.di = di

    c.out_t = nc.dram_tensor("out_t", [D, T], F32, kind="ExternalOutput").ap()
    c.taps = {}
    for tname, (shape, dt) in _tap_defs().items():
        if tname in _TAPS:
            c.taps[tname] = nc.dram_tensor("tap_" + tname, list(shape), dt,
                                           kind="ExternalOutput").ap()

    ds = {}

    def dscr(name, shape, dt):
        ds[name] = nc.dram_tensor("scr_" + name, list(shape), dt, kind="Internal").ap()
    dscr("xln", [D, T], BF16)
    dscr("xln_rev", [D, T], BF16)
    for pref in ("fwd", "bwd"):
        dscr(f"{pref}_bc", [32, T], F32)
        dscr(f"{pref}_yg", [E, T], BF16)
        if f"{pref}_delta" in _TAPS:
            dscr(f"{pref}_delta", [E, T], F32)
        if f"{pref}_yr" in _TAPS:
            dscr(f"{pref}_yr", [E, T], F32)
        if f"{pref}_xi" in _TAPS:
            dscr(f"{pref}_xi", [E, T], BF16)
        if f"{pref}_z" in _TAPS:
            dscr(f"{pref}_z", [E, T], BF16)
    dscr("x2", [D, T], F32)
    dscr("xf", [D, T], F32)
    for tn in ("l0_qrot", "l0_krot", "l0_ao", "l0_xa", "l0_xin", "l0_xf"):
        if tn in _TAPS:
            dscr(tn, list(_tap_defs()[tn][0]), _tap_defs()[tn][1])
    dscr("row", [64, T], F32)
    dscr("rmax", [HEADS * NCH * 4 * 128], F32)
    dscr("sumexp", [HEADS * NCH * WIN], F32)
    for pref in ("fwd", "bwd"):
        dscr(f"wb_{pref}_in_w", [D, 2 * E], BF16)
        dscr(f"wb_{pref}_out_w", [E, D], BF16)
    for l in range(2):
        dscr(f"wb_l{l}_w1p", [D, 2 * HIDP], BF16)
        dscr(f"wb_l{l}_w2p", [HIDP, D], BF16)
        dscr(f"wb_l{l}_wqkv", [D, QKV_M], BF16)
        dscr(f"wb_l{l}_wo", [INNER, D], BF16)
    c.ds = ds

    for s in ("qW", "qWb", "qWc", "qWd", "qG", "qS", "qV", "qP", "qY",
              "mm", "sv", "vv", "gg", "ws"):
        setattr(c, s, S(stack, nc, s))

    sb = {}

    def sbuf(name, shape, dt):
        sb[name] = stack.enter_context(nc.sbuf_tensor(name, list(shape), dt))
        return sb[name]
    c.sb, c.sbuf = sb, sbuf
    c.ps = [stack.enter_context(nc.psum_tensor(f"ps{i}", [128, 1024], F32))
            for i in range(4)]

    def psl(i):
        return apx(c.ps[i // 2].ap(), (i % 2) * 512, [[1024, 128], [1, 512]])
    c.psl = psl

    c.block = BlkProxy(c, stack.enter_context(nc.Block()))
    for i, val in enumerate([1e-5, -1.0, -0.5, 0.5, 2.0, -2.0, 64.0, 1.0 / 64]):
        pass

    phase0(c)
    if _PHASE_LIMIT >= 1:
        phase1(c)
    if _PHASE_LIMIT >= 2:
        phase2(c)
    if _PHASE_LIMIT >= 3:
        if _SERIAL_P3:
            c.block.serial = True
        phase3(c)
        c.block.serial = False
    if _PHASE_LIMIT >= 4:
        phase4(c)
    _finish(c)
    stack.close()
    return nc


def _emit_taps(c, names):
    todo = [n for n in names if n in c.taps]
    if not todo:
        return
    blk = c.block

    @blk.gpsimd
    def _(g):
        c.vv.wait(g)
        c.sv.wait(g)
        c.qS.wait(g)
        c.qW.wait(g)
        c.qWb.wait(g)
        c.qWc.wait(g)
        c.qWd.wait(g)
        c.qP.wait(g)
        c.qY.wait(g)
        c.qG.wait(g)
        for n in todo:
            dmac(c, c.qG, g, out=c.taps[n], in_=c.ds[n])


def phase0(c):
    nc, blk, sbuf = c.nc, c.block, c.sbuf

    c.ones_bf = sbuf("ones_bf", [128, 1], BF16)
    c.ones_f = sbuf("ones_f", [128, 1], F32)
    c.ident = sbuf("ident", [128, 128], BF16)
    c.sel2 = sbuf("sel2", [128, 32], BF16)
    c.sel2t = sbuf("sel2t", [32, 128], BF16)
    c.scal = sbuf("scal", [128, 8], F32)
    for i, val in enumerate([1e-5, -1.0, -0.5, 0.5, 2.0, -2.0, 64.0, 1.0 / 64]):
        nc.const_aps.register(F32, val, c.scal.ap()[:, i:i + 1])
    mp = {}
    for pref in ("fwd", "bwd"):
        mp[pref] = dict(
            conv_w=sbuf(f"{pref}_conv_w_sb", [128, 64], F32),
            conv_b=sbuf(f"{pref}_conv_b_sb", [128, 16], F32),
            dt_b=sbuf(f"{pref}_dt_b_sb", [128, 16], F32),
            Dp=sbuf(f"{pref}_Dp_sb", [128, 16], F32),
            A_neg=sbuf(f"{pref}_A_neg_sb", [128, 256], F32),
            xproj=sbuf(f"{pref}_xproj_sb", [128, 16 * 96], BF16),
            dt_w=sbuf(f"{pref}_dtw_sb", [64, E], BF16),
        )
    c.mp = mp

    p0 = ExitStack()
    x_f = p0.enter_context(nc.sbuf_tensor("p0_x", [128, 8 * T], F32))
    xsq = p0.enter_context(nc.sbuf_tensor("p0_xsq", [128, 8 * T], BF16))
    stat = p0.enter_context(nc.sbuf_tensor("p0_stat", [1, 4 * T], F32))
    rep = p0.enter_context(nc.sbuf_tensor("p0_rep", [128, 2 * T], F32))
    xln_sb = p0.enter_context(nc.sbuf_tensor("p0_xln", [128, 8 * T], BF16))

    @blk.sync
    def _(sync):
        dmac(c, c.qW, sync, out=sb_ap(x_f).rearrange("p (c t) -> p c t", c=8),
             in_=c.di["x_t"].rearrange("(c p) t -> p c t", p=128))
        dmac(c, c.qW, sync, out=c.ones_bf[:], in_=c.di["c_ones"])
        dmac(c, c.qW, sync, out=c.ident[:], in_=c.di["c_ident"])
        dmac(c, c.qW, sync, out=c.sel2[:], in_=c.di["c_sel2"])
        dmac(c, c.qW, sync, out=c.sel2t[:], in_=c.di["c_sel2t"])
        dmac(c, c.qW, sync, out=c.scal[:], in_=c.di["c_scal"])
        for pref in ("fwd", "bwd"):
            m = mp[pref]
            dmac(c, c.qW, sync,
                 out=sb_ap(m["conv_w"]).rearrange("p (c k) -> p c k", c=16),
                 in_=c.di[f"{pref}_conv_w"])
            dmac(c, c.qW, sync, out=m["conv_b"][:], in_=c.di[f"{pref}_conv_b"])
            dmac(c, c.qW, sync, out=m["dt_b"][:], in_=c.di[f"{pref}_dt_b"])
            dmac(c, c.qW, sync, out=m["Dp"][:], in_=c.di[f"{pref}_Dp"])
            dmac(c, c.qW, sync,
                 out=sb_ap(m["A_neg"]).rearrange("p (c n) -> p c n", c=16),
                 in_=c.di[f"{pref}_A_log"].rearrange("(c p) n -> p c n", p=128))
    p0_loads = c.qW.n

    @blk.gpsimd
    def _(g):
        for pref in ("fwd", "bwd"):
            m = mp[pref]
            dmac(c, c.qG, g,
                 out=sb_ap(m["xproj"]).rearrange("p (c r) -> p c r", c=16),
                 in_=c.di[f"{pref}_xproj_w"].rearrange("(c p) r -> p c r", p=128))
            dmac(c, c.qG, g, out=m["dt_w"][:], in_=c.di[f"{pref}_dt_w"])
    c.small_bf = c.qG.n

    @blk.vector
    def _(v):
        c.qW.wait(v, p0_loads)
        v.tensor_copy(c.ones_f[:], c.ones_bf[:])
        v.drain()
        c.vv.tick(v)
    onesf_ok = c.vv.n

    @blk.scalar
    def _(s):
        c.qW.wait(s, p0_loads)
        for pref in ("fwd", "bwd"):
            s.activation(mp[pref]["A_neg"][:], mp[pref]["A_neg"][:], AF.Exp,
                         bias=0.0, scale=1.0)
            s.drain()
        s.activation(xsq[:], x_f[:], AF.Square, bias=0.0, scale=1.0)
        s.drain()
        c.sv.tick(s)
    sq_ok = c.sv.n

    @blk.vector
    def _(v):
        c.sv.wait(v, sq_ok)
        for pref in ("fwd", "bwd"):
            v.tensor_scalar_mul(mp[pref]["A_neg"][:], mp[pref]["A_neg"][:], -1.0)
        v.drain()
        c.vv.tick(v)
    c.aneg_ok = c.vv.n

    @blk.tensor
    def _(t):
        c.vv.wait(t, onesf_ok)
        c.sv.wait(t, sq_ok)
        last = None
        for half in range(2):
            for tc in range(2):
                for k in range(8):
                    last = t.matmul(
                        apx(c.ps[half].ap(), tc * 512, [[1024, 1], [1, 512]]),
                        c.ones_f[:],
                        apx(sb_ap(x_f), k * T + half * 1024 + tc * 512,
                            [[8 * T, 128], [1, 512]]),
                        start=(k == 0), stop=(k == 7))
        for half in range(2):
            for tc in range(2):
                for k in range(8):
                    last = t.matmul(
                        apx(c.ps[2 + half].ap(), tc * 512, [[1024, 1], [1, 512]]),
                        c.ones_bf[:],
                        apx(sb_ap(xsq), k * T + half * 1024 + tc * 512,
                            [[8 * T, 128], [1, 512]]),
                        start=(k == 0), stop=(k == 7))
        c.mm.inc(last)
    stats_mm = c.mm.n

    MU, VAR, RSTD, MUR = 0, T, 2 * T, 3 * T

    def srow(off):
        return apx(sb_ap(stat), off, [[4 * T, 1], [1, T]])

    @blk.vector
    def _(v):
        c.mm.wait(v, stats_mm)
        for half in range(2):
            v.tensor_scalar_mul(apx(sb_ap(stat), MU + half * 1024, [[4 * T, 1], [1, 1024]]),
                                apx(c.ps[half].ap(), 0, [[1024, 1], [1, 1024]]), 1.0 / D)
            v.tensor_scalar_mul(apx(sb_ap(stat), VAR + half * 1024, [[4 * T, 1], [1, 1024]]),
                                apx(c.ps[2 + half].ap(), 0, [[1024, 1], [1, 1024]]), 1.0 / D)
        v.drain()
        v.tensor_mul(srow(RSTD), srow(MU), srow(MU))
        v.drain()
        v.tensor_sub(srow(VAR), srow(VAR), srow(RSTD))
        v.drain()
        c.vv.tick(v)
    var_ok = c.vv.n

    @blk.scalar
    def _(s):
        c.vv.wait(s, var_ok)
        s.activation(srow(RSTD), srow(VAR), AF.Ln, bias=1e-5, scale=1.0)
        s.drain()
        s.activation(srow(RSTD), srow(RSTD), AF.Exp, bias=0.0, scale=-0.5)
        s.drain()
        c.sv.tick(s)
    rstd_ok = c.sv.n

    @blk.vector
    def _(v):
        c.sv.wait(v, rstd_ok)
        v.tensor_mul(srow(MUR), srow(MU), srow(RSTD))
        v.drain()
        c.vv.tick(v)
    mur_ok = c.vv.n

    @blk.scalar
    def _(s):
        c.vv.wait(s, mur_ok)
        dmac(c, c.qS, s, out=apx(c.ds["row"], 0, [[T, 2], [1, T]]),
             in_=apx(sb_ap(stat), RSTD, [[4 * T, 1], [T, 2], [1, T]]))
    rows_stored = c.qS.n

    @blk.gpsimd
    def _(g):
        c.qS.wait(g, rows_stored)
        dmac(c, c.qG, g, out=sb_ap(rep).rearrange("p (c t) -> p c t", c=2),
             in_=apx(c.ds["row"], 0, [[0, 128], [T, 2], [1, T]]))
    rep_ok = c.qG.n

    @blk.vector
    def _(v):
        c.qG.wait(v, rep_ok)
        rrep = apx(sb_ap(rep), 0, [[2 * T, 128], [1, T]])
        mrep = apx(sb_ap(rep), T, [[2 * T, 128], [1, T]])
        for k in range(8):
            xk = apx(sb_ap(x_f), k * T, [[8 * T, 128], [1, T]])
            v.tensor_mul(xk, xk, rrep)
        v.drain()
        for k in range(8):
            xk = apx(sb_ap(x_f), k * T, [[8 * T, 128], [1, T]])
            ok = apx(sb_ap(xln_sb), k * T, [[8 * T, 128], [1, T]])
            v.tensor_sub(ok, xk, mrep)
        v.drain()
        for k in range(8):
            src = apx(sb_ap(xln_sb), k * T + (T - 1), [[8 * T, 128], [-1, T]])
            dst = apx(sb_ap(xsq), k * T, [[8 * T, 128], [1, T]])
            v.tensor_copy(dst, src)
        v.drain()
        c.vv.tick(v)
    xln_applied = c.vv.n

    @blk.scalar
    def _(s):
        c.vv.wait(s, xln_applied)
        dmac(c, c.qS, s, out=c.ds["xln"].rearrange("(c p) t -> p c t", p=128),
             in_=sb_ap(xln_sb).rearrange("p (c t) -> p c t", c=8))
        dmac(c, c.qS, s, out=c.ds["xln_rev"].rearrange("(c p) t -> p c t", p=128),
             in_=sb_ap(xsq).rearrange("p (c t) -> p c t", c=8))
    c.xln_ok = c.qS.n

    plan = []
    for pref in ("fwd", "bwd"):
        plan.append((c.di[f"{pref}_in_w"], c.ds[f"wb_{pref}_in_w"], D, 2 * E))
    for pref in ("fwd", "bwd"):
        plan.append((c.di[f"{pref}_out_w"], c.ds[f"wb_{pref}_out_w"], E, D))
    for l in range(2):
        plan.append((c.di[f"l{l}_wqkv"], c.ds[f"wb_l{l}_wqkv"], D, QKV_M))
        plan.append((c.di[f"l{l}_wo"], c.ds[f"wb_l{l}_wo"], INNER, D))
        plan.append((c.di[f"l{l}_w1p"], c.ds[f"wb_l{l}_w1p"], D, 2 * HIDP))
        plan.append((c.di[f"l{l}_w2p"], c.ds[f"wb_l{l}_w2p"], HIDP, D))
    c.precast = {}

    @blk.gpsimd
    def _(g):
        for src, dst, rws, cols in plan:
            nrt = rws // 128
            step = max(1, nrt // 4)
            for r0 in range(0, nrt, step):
                r1 = min(nrt, r0 + step)
                dmac(c, c.qP, g, out=dst[r0 * 128:r1 * 128, :],
                     in_=src[r0 * 128:r1 * 128, :])
            c.qP.wait(g)
            c.precast[dst.tensor.name] = c.gg.tick(g)

    p0.close()
    _emit_taps(c, ["xln", "xln_rev"])


def phase1(c):
    nc, blk, sbuf = c.nc, c.block, c.sbuf
    p1 = ExitStack()
    sb = lambda n, sh, dt: p1.enter_context(nc.sbuf_tensor(n, list(sh), dt))
    b = Ctx()
    b.xln = [sb("m_xln0", [128, 8 * TC], BF16), sb("m_xln1", [128, 8 * TC], BF16)]
    b.xiw = sb("m_xiw", [128, 16 * (TC + 3)], BF16)
    b.xis = sb("m_xis", [128, 16 * TC], BF16)
    b.zs = sb("m_zs", [128, 16 * TC], BF16)
    b.acc = sb("m_acc", [128, 2 * TC], F32)
    b.dt = sb("m_dt", [64, TC], BF16)
    b.bc = sb("m_bc", [64, TC], F32)
    b.brep = sb("m_brep", [128, NST * TC], BF16)
    b.crep = sb("m_crep", [128, NST * TC], BF16)
    b.delta = sb("m_delta", [128, 2 * TC], F32)
    b.sp = sb("m_sp", [128, 2 * TC], F32)
    b.du = sb("m_du", [128, 2 * TC], BF16)
    b.a = sb("m_a", [128, 2 * SCW], BF16)
    b.bt = sb("m_bt", [128, SCW], BF16)
    b.h = b.bt
    b.carry = sb("m_carry", [128, 16 * NST], BF16)
    b.yr = sb("m_yr", [128, TC], F32)
    b.yg = sb("m_yg", [128, 2 * TC], BF16)
    b.wring = sb("m_wring", [128, 32 * 128], BF16)

    @blk.vector
    def _(v):
        for sm in (c.qS, c.qW, c.qG, c.mm, c.sv):
            sm.wait(v)
        v.memset(b.a[:], 0.0)
        v.memset(b.bt[:], 0.0)
        v.drain()
        c.vv.tick(v)
    b.scan_zeroed = c.vv.n

    for pref in ("fwd", "bwd"):
        _mamba_dir(c, b, pref)
    p1.close()
    _emit_taps(c, ["fwd_xi", "fwd_z", "fwd_bc", "fwd_yg", "bwd_yg",
                   "fwd_delta", "fwd_yr"])


def _mamba_dir(c, b, pref):
    nc, blk = c.nc, c.block
    m = c.mp[pref]
    psl = c.psl
    xsrc = c.ds["xln"] if pref == "fwd" else c.ds["xln_rev"]
    wb_in = c.ds[f"wb_{pref}_in_w"]
    yg_dst = c.ds[f"{pref}_yg"]

    marks = [(c.mm, c.mm.n), (c.vv, c.vv.n), (c.sv, c.sv.n),
             (c.qW, c.qW.n), (c.qV, c.qV.n), (c.qS, c.qS.n),
             (c.qG, c.qG.n), (c.qY, c.qY.n)]

    @blk.vector
    def _(v):
        for s, n in marks:
            v.wait_ge(s.h, n)
        v.memset(b.carry[:], 0.0)
        v.memset(b.xiw[:], 0.0)
        v.drain()
        c.vv.tick(v)
    dir_zero = c.vv.n

    @blk.scalar
    def _(s_):
        for s, n in marks:
            s_.wait_ge(s.h, n)
        c.sv.tick(s_)

    @blk.tensor
    def _(t):
        c.vv.wait(t, dir_zero)

    @blk.sync
    def _(sync):
        c.vv.wait(sync, dir_zero)

    @blk.gpsimd
    def _(g):
        c.vv.wait(g, dir_zero)

    xln_ld = [None] * NCH
    stA_mm = {}
    evac_done = {}
    conv_done = [None] * 16
    silu_done = [None] * 16
    xproj_mm = [None] * NCH
    dt_ready = [None] * NCH
    bc_stored = [None] * NCH
    rep_ready = [None] * NCH
    dtp_mm = {}
    sp_done = {}
    delta_done = {}
    exps_done = {}
    scan_done = {}
    btt_done = {}
    ymult_done = {}
    gate_done = {}
    yasm_done = {}
    yg_dma = {}
    yred_done = {}
    ring_use = [None] * 32
    ring_load = [None] * 32

    precast_cnt = c.precast[wb_in.tensor.name]

    for ch in range(NCH):
        @blk.sync
        def _(sync, ch=ch):
            if ch >= 2:
                c.mm.wait(sync, xproj_mm[ch - 2])
            if ch >= 1:
                c.mm.wait(sync, stA_mm[(ch - 1, 28)])
            if ch == 0:
                c.qS.wait(sync, c.xln_ok)
            xln_ld[ch] = dmac(c, c.qW, sync,
                out=sb_ap(b.xln[ch % 2]).rearrange("p (k t) -> p k t", k=8),
                in_=apx(xsrc, ch * TC, [[T, 128], [128 * T, 8], [1, TC]]))

        for wave in range(8):
            @blk.sync
            def _(sync, ch=ch, wave=wave):
                if ch == 0 and wave == 0:
                    c.gg.wait(sync, precast_cnt)
                for mi in range(wave * 4, wave * 4 + 4):
                    for k in range(8):
                        slot = (mi * 8 + k) % 32
                        if ring_use[slot] is not None:
                            c.mm.wait(sync, ring_use[slot])
                        ring_load[slot] = dmac(c, c.qW, sync,
                            out=b.wring[:, slot * 128:(slot + 1) * 128],
                            in_=apx(wb_in, (k * 128) * (2 * E) + mi * 128,
                                    [[2 * E, 128], [1, 128]]))

            @blk.tensor
            def _(t, ch=ch, wave=wave):
                c.qW.wait(t)
                for mi in range(wave * 4, wave * 4 + 4):
                    slot_ps = mi % 4
                    prev = (ch, mi - 4) if mi >= 4 else (ch - 1, mi + 28)
                    if prev in evac_done:
                        eng, cnt = evac_done[prev]
                        (c.vv if eng == "vv" else c.sv).wait(t, cnt)
                    last = None
                    for k in range(8):
                        slot = (mi * 8 + k) % 32
                        last = t.matmul(
                            psl(slot_ps),
                            b.wring[:, slot * 128:(slot + 1) * 128],
                            apx(sb_ap(b.xln[ch % 2]), k * TC, [[8 * TC, 128], [1, TC]]),
                            start=(k == 0), stop=(k == 7))
                        ring_use[slot] = c.mm.n + 1
                    stA_mm[(ch, mi)] = c.mm.inc(last)

            if wave < 4:
                @blk.vector
                def _(v, ch=ch, wave=wave):
                    if wave == 0 and ch == 0:
                        c.vv.wait(v, dir_zero)
                    for mi in range(wave * 4, wave * 4 + 4):
                        c.mm.wait(v, stA_mm[(ch, mi)])
                        if ch > 0:
                            v.tensor_copy(
                                apx(sb_ap(b.xiw), mi * (TC + 3), [[16 * (TC + 3), 128], [1, 3]]),
                                apx(sb_ap(b.xiw), mi * (TC + 3) + TC, [[16 * (TC + 3), 128], [1, 3]]))
                            v.drain()
                        v.tensor_copy(
                            apx(sb_ap(b.xiw), mi * (TC + 3) + 3, [[16 * (TC + 3), 128], [1, TC]]),
                            psl(mi % 4))
                        v.drain()
                        evac_done[(ch, mi)] = ("vv", c.vv.tick(v))
            else:
                @blk.scalar
                def _(s, ch=ch, wave=wave):
                    if (f"{pref}_z" in c.taps or f"{pref}_xi" in c.taps) and ch > 0:
                        c.qG.wait(s)
                    for mi in range(wave * 4, wave * 4 + 4):
                        j = mi - 16
                        c.mm.wait(s, stA_mm[(ch, mi)])
                        if ch > 0:
                            c.vv.wait(s, gate_done[(ch - 1, j)])
                        ins = s.activation(
                            apx(sb_ap(b.zs), j * TC, [[16 * TC, 128], [1, TC]]),
                            psl(mi % 4), AF.Silu, bias=0.0, scale=1.0)
                        s.drain()
                        evac_done[(ch, mi)] = ("sv", c.sv.tick(s))

        for j in range(16):
            @blk.vector
            def _(v, ch=ch, j=j):
                base = j * (TC + 3)
                xw = lambda off: apx(sb_ap(b.xiw), base + off,
                                     [[16 * (TC + 3), 128], [1, TC]])
                accp = apx(sb_ap(b.acc), (j % 2) * TC, [[2 * TC, 128], [1, TC]])
                if silu_done[(j - 2) % 16] is not None:
                    c.sv.wait(v, silu_done[(j - 2) % 16])
                v.tensor_scalar(accp, xw(0), m["conv_w"][:, j * 4:j * 4 + 1],
                                m["conv_b"][:, j:j + 1], ALU.mult, ALU.add)
                v.drain()
                for k in range(1, 4):
                    v.scalar_tensor_tensor(accp, xw(k), m["conv_w"][:, j * 4 + k:j * 4 + k + 1],
                                           accp, ALU.mult, ALU.add)
                    v.drain()
                conv_done[j] = c.vv.tick(v)

            @blk.scalar
            def _(s, ch=ch, j=j):
                c.vv.wait(s, conv_done[j])
                if (f"{pref}_z" in c.taps or f"{pref}_xi" in c.taps) and ch > 0:
                    c.qG.wait(s)
                if ch > 0:
                    c.vv.wait(s, yasm_done[(ch - 1, j)])
                accp = apx(sb_ap(b.acc), (j % 2) * TC, [[2 * TC, 128], [1, TC]])
                ins = s.activation(apx(sb_ap(b.xis), j * TC, [[16 * TC, 128], [1, TC]]),
                                   accp, AF.Silu, bias=0.0, scale=1.0)
                s.drain()
                silu_done[j] = c.sv.tick(s)

        @blk.tensor
        def _(t, ch=ch):
            for j in range(16):
                c.sv.wait(t, silu_done[j])
            if ch > 0:
                c.vv.wait(t, dt_ready[ch - 1])
            last = None
            for k in range(16):
                last = t.matmul(
                    apx(c.ps[2].ap(), 0, [[1024, 96], [1, 512]]),
                    m["xproj"][:, k * 96:(k + 1) * 96],
                    apx(sb_ap(b.xis), k * TC, [[16 * TC, 128], [1, TC]]),
                    start=(k == 0), stop=(k == 15))
            xproj_mm[ch] = c.mm.inc(last)

        @blk.vector
        def _(v, ch=ch):
            c.mm.wait(v, xproj_mm[ch])
            if ch > 0:
                c.qG.wait(v)
            v.tensor_copy(b.dt[:], apx(c.ps[2].ap(), 0, [[1024, 64], [1, TC]]))
            v.tensor_copy(apx(sb_ap(b.bc), 0, [[TC, 32], [1, TC]]),
                          apx(c.ps[2].ap(), 64 * 1024, [[1024, 32], [1, TC]]))
            v.drain()
            dt_ready[ch] = c.vv.tick(v)

        @blk.gpsimd
        def _(g, ch=ch):
            c.vv.wait(g, dt_ready[ch])
            bc_stored[ch] = dmac(c, c.qG, g,
                out=apx(c.ds[f"{pref}_bc"], ch * TC, [[T, 32], [1, TC]]),
                in_=apx(sb_ap(b.bc), 0, [[TC, 32], [1, TC]]))
            c.qG.wait(g, bc_stored[ch])
            if ch > 0:
                c.vv.wait(g, btt_done[(ch - 1, 15)])
            dmac(c, c.qG, g,
                 out=sb_ap(b.brep).rearrange("p (n t) -> p n t", n=NST),
                 in_=apx(c.ds[f"{pref}_bc"], ch * TC, [[0, 128], [T, NST], [1, TC]]))
            if ch > 0:
                c.vv.wait(g, ymult_done[(ch - 1, 15)])
            rep_ready[ch] = dmac(c, c.qG, g,
                out=sb_ap(b.crep).rearrange("p (n t) -> p n t", n=NST),
                in_=apx(c.ds[f"{pref}_bc"], NST * T + ch * TC,
                        [[0, 128], [T, NST], [1, TC]]))

        for j in range(16):
            dslot = 5 if j % 2 == 0 else 6

            @blk.tensor
            def _(t, ch=ch, j=j, dslot=dslot):
                c.vv.wait(t, dt_ready[ch])
                prev = (ch, j - 2) if j >= 2 else ((ch - 1, j + 14) if ch > 0 else None)
                if prev is not None and prev in sp_done:
                    c.sv.wait(t, sp_done[prev])
                ins = t.matmul(psl(dslot), m["dt_w"][:, j * 128:(j + 1) * 128],
                               b.dt[:], start=True, stop=True)
                dtp_mm[(ch, j)] = c.mm.inc(ins)

            @blk.scalar
            def _(s, ch=ch, j=j, dslot=dslot):
                c.mm.wait(s, dtp_mm[(ch, j)])
                prev = (ch, j - 1) if j >= 1 else ((ch - 1, 15) if ch > 0 else None)
                if prev is not None and prev in delta_done:
                    c.vv.wait(s, delta_done[prev])
                sp0 = apx(sb_ap(b.sp), 0, [[2 * TC, 128], [1, TC]])
                sp1 = apx(sb_ap(b.sp), TC, [[2 * TC, 128], [1, TC]])
                dtb = m["dt_b"][:, j:j + 1]
                s.activation(sp0, psl(dslot), AF.Abs, bias=dtb, scale=1.0)
                s.drain()
                s.activation(sp1, sp0, AF.Exp, bias=0.0, scale=-1.0)
                s.drain()
                s.activation(sp0, sp1, AF.Ln, bias=1.0, scale=1.0)
                s.drain()
                s.activation(sp1, psl(dslot), AF.Relu, bias=dtb, scale=1.0)
                s.drain()
                sp_done[(ch, j)] = c.sv.tick(s)

            @blk.vector
            def _(v, ch=ch, j=j):
                c.sv.wait(v, sp_done[(ch, j)])
                prev = (ch, j - 2) if j >= 2 else ((ch - 1, j + 14) if ch > 0 else None)
                if prev is not None and prev in exps_done:
                    c.sv.wait(v, exps_done[prev])
                sp0 = apx(sb_ap(b.sp), 0, [[2 * TC, 128], [1, TC]])
                sp1 = apx(sb_ap(b.sp), TC, [[2 * TC, 128], [1, TC]])
                if f"{pref}_delta" in c.taps:
                    c.qG.wait(v)
                dj = apx(sb_ap(b.delta), (j % 2) * TC, [[2 * TC, 128], [1, TC]])
                v.tensor_add(dj, sp0, sp1)
                v.drain()
                delta_done[(ch, j)] = c.vv.tick(v)

            @blk.scalar
            def _(s, ch=ch, j=j):
                c.vv.wait(s, delta_done[(ch, j)])
                if ch == 0 and j == 0:
                    c.vv.wait(s, b.scan_zeroed)
                    c.vv.wait(s, c.aneg_ok)
                prev = (ch, j - 2) if j >= 2 else ((ch - 1, j + 14) if ch > 0 else None)
                if prev is not None and prev in yred_done:
                    c.vv.wait(s, yred_done[prev])
                dj = apx(sb_ap(b.delta), (j % 2) * TC, [[2 * TC, 128], [1, TC]])
                abase = (j % 2) * SCW
                last = None
                for n in range(NST):
                    last = s.activation(
                        apx(sb_ap(b.a), abase + n * BLK + SEP, [[2 * SCW, 128], [1, TC]]),
                        dj, AF.Exp, bias=0.0,
                        scale=m["A_neg"][:, j * NST + n:j * NST + n + 1])
                s.drain()
                exps_done[(ch, j)] = c.sv.tick(s)

            @blk.vector
            def _(v, ch=ch, j=j):
                dj = apx(sb_ap(b.delta), (j % 2) * TC, [[2 * TC, 128], [1, TC]])
                duj = apx(sb_ap(b.du), (j % 2) * TC, [[2 * TC, 128], [1, TC]])
                xisj = apx(sb_ap(b.xis), j * TC, [[16 * TC, 128], [1, TC]])
                v.tensor_mul(duj, dj, xisj)
                v.drain()
                v.tensor_copy(
                    apx(sb_ap(b.bt), 1, [[SCW, 128], [BLK, NST]]),
                    apx(sb_ap(b.carry), j * NST, [[16 * NST, 128], [1, NST]]))
                v.drain()
                c.qG.wait(v, rep_ready[ch])
                v.tensor_tensor(
                    apx(sb_ap(b.bt), SEP, [[SCW, 128], [BLK, NST], [1, TC]]),
                    apx(sb_ap(b.du), (j % 2) * TC, [[2 * TC, 128], [0, NST], [1, TC]]),
                    sb_ap(b.brep).rearrange("p (n t) -> p n t", n=NST),
                    ALU.mult)
                v.drain()
                btt_done[(ch, j)] = c.vv.tick(v)
                c.sv.wait(v, exps_done[(ch, j)])
                abase = (j % 2) * SCW
                v.tensor_tensor_scan(
                    sb_ap(b.bt)[:, :],
                    apx(sb_ap(b.a), abase, [[2 * SCW, 128], [1, SCW]]),
                    b.bt[:], 0.0, ALU.mult, ALU.add)
                v.drain()
                scan_done[(ch, j)] = c.vv.tick(v)
                v.tensor_copy(
                    apx(sb_ap(b.carry), j * NST, [[16 * NST, 128], [1, NST]]),
                    apx(sb_ap(b.bt), SEP + TC - 1, [[SCW, 128], [BLK, NST]]))
                v.drain()
                abase = (j % 2) * SCW
                v.tensor_tensor(
                    apx(sb_ap(b.a), abase + SEP, [[2 * SCW, 128], [BLK, NST], [1, TC]]),
                    apx(sb_ap(b.bt), SEP, [[SCW, 128], [BLK, NST], [1, TC]]),
                    sb_ap(b.crep).rearrange("p (n t) -> p n t", n=NST),
                    ALU.mult)
                v.drain()
                ymult_done[(ch, j)] = c.vv.tick(v)
                if f"{pref}_yr" in c.taps or f"{pref}_delta" in c.taps:
                    c.qG.wait(v)
                yrj = apx(sb_ap(b.yr), 0, [[TC, 128], [1, TC], [1, 1]])
                v.tensor_reduce(yrj,
                                apx(sb_ap(b.a), abase + SEP, [[2 * SCW, 128], [1, TC], [BLK, NST]]),
                                AX.X, ALU.add)
                v.drain()
                yred_done[(ch, j)] = c.vv.tick(v)
                yrf = apx(sb_ap(b.yr), 0, [[TC, 128], [1, TC]])
                v.scalar_tensor_tensor(yrf, xisj, m["Dp"][:, j:j + 1], yrf,
                                       ALU.mult, ALU.add)
                v.drain()
                yasm_done[(ch, j)] = c.vv.tick(v)
                prev = (ch, j - 2) if j >= 2 else (ch - 1, j + 14)
                if prev in yg_dma:
                    c.qY.wait(v)
                zsj = apx(sb_ap(b.zs), j * TC, [[16 * TC, 128], [1, TC]])
                if pref == "fwd":
                    ygj = apx(sb_ap(b.yg), (j % 2) * TC, [[2 * TC, 128], [1, TC]])
                else:
                    ygj = apx(sb_ap(b.yg), (j % 2) * TC + TC - 1, [[2 * TC, 128], [-1, TC]])
                v.tensor_mul(ygj, yrf, zsj)
                v.drain()
                gate_done[(ch, j)] = c.vv.tick(v)

            @blk.gpsimd
            def _(g, ch=ch, j=j):
                c.vv.wait(g, gate_done[(ch, j)])
                gcol = ch * TC if pref == "fwd" else T - (ch + 1) * TC
                yg_dma[(ch, j)] = dmac(c, c.qY, g,
                    out=apx(yg_dst, (j * 128) * T + gcol, [[T, 128], [1, TC]]),
                    in_=apx(sb_ap(b.yg), (j % 2) * TC, [[2 * TC, 128], [1, TC]]))
                if f"{pref}_delta" in c.taps:
                    dmac(c, c.qG, g,
                         out=apx(c.ds[f"{pref}_delta"], (j * 128) * T + ch * TC,
                                 [[T, 128], [1, TC]]),
                         in_=apx(sb_ap(b.delta), (j % 2) * TC, [[2 * TC, 128], [1, TC]]))
                if f"{pref}_yr" in c.taps:
                    yrf2 = apx(sb_ap(b.yr), 0, [[TC, 128], [1, TC]])
                    dmac(c, c.qG, g,
                         out=apx(c.ds[f"{pref}_yr"], (j * 128) * T + ch * TC,
                                 [[T, 128], [1, TC]]),
                         in_=yrf2)

        if f"{pref}_xi" in c.taps or f"{pref}_z" in c.taps:
            @blk.gpsimd
            def _(g, ch=ch):
                c.sv.wait(g)
                c.vv.wait(g)
                if f"{pref}_xi" in c.taps:
                    dmac(c, c.qG, g,
                         out=apx(c.ds[f"{pref}_xi"], ch * TC,
                                 [[T, 128], [128 * T, 16], [1, TC]]),
                         in_=sb_ap(b.xis).rearrange("p (j t) -> p j t", j=16))
                if f"{pref}_z" in c.taps:
                    dmac(c, c.qG, g,
                         out=apx(c.ds[f"{pref}_z"], ch * TC,
                                 [[T, 128], [128 * T, 16], [1, TC]]),
                         in_=sb_ap(b.zs).rearrange("p (j t) -> p j t", j=16))


def phase2(c):
    nc, blk, sbuf = c.nc, c.block, c.sbuf
    p2 = ExitStack()
    sb = lambda n, sh, dt: p2.enter_context(nc.sbuf_tensor(n, list(sh), dt))
    ygf = sb("p2_ygf", [128, 16 * TC], BF16)
    ygb = sb("p2_ygb", [128, 16 * TC], BF16)
    ao = sb("p2_ao", [128, 8 * TC], F32)
    aosq = sb("p2_aosq", [128, 8 * TC], BF16)
    xt = sb("p2_x", [128, 8 * TC], F32)
    xf0 = sb("p2_xf", [128, 8 * TC], F32)
    stat = sb("p2_stat", [1, 4 * TC], F32)
    rep = sb("p2_rep", [128, 2 * TC], F32)
    wring = sb("p2_wring", [128, 32 * 128], BF16)
    psl = c.psl

    marks = [(c.mm, c.mm.n), (c.vv, c.vv.n), (c.sv, c.sv.n), (c.qWb, c.qWb.n),
             (c.qV, c.qV.n), (c.qS, c.qS.n), (c.qG, c.qG.n), (c.qY, c.qY.n)]

    @blk.vector
    def _(v):
        for sm, n in marks:
            v.wait_ge(sm.h, n)
        c.vv.tick(v)
    p2_zero = c.vv.n

    for eng_name in ("scalar", "sync", "gpsimd", "tensor"):
        @getattr(blk, eng_name)
        def _(e):
            c.vv.wait(e, p2_zero)

    wb_f = c.ds["wb_fwd_out_w"]
    wb_b = c.ds["wb_bwd_out_w"]
    ring_load = [None] * 32
    ring_use = [None] * 32
    ow_mm = {}
    evacV = {}
    evacS = {}
    ygld = [None] * NCH
    s1_mm = [None] * NCH
    apply_done = [None] * NCH
    xf_done = [None] * NCH
    x2_stored = [None] * NCH
    MU, VA, RS, MR = 0, TC, 2 * TC, 3 * TC

    def srow(off, n=TC):
        return apx(sb_ap(stat), off, [[4 * TC, 1], [1, n]])

    for r in range(NCH):
        @blk.sync
        def _(sync, r=r):
            if r == 0:
                c.qY.wait(sync)
                c.gg.wait(sync, c.precast[wb_f.tensor.name])
                c.gg.wait(sync, c.precast[wb_b.tensor.name])
            else:
                c.mm.wait(sync, ow_mm[(r - 1, 7)])
                c.vv.wait(sync, xf_done[r - 1])
            for src, dstt in ((c.ds["fwd_yg"], ygf), (c.ds["bwd_yg"], ygb)):
                dmac(c, c.qV, sync,
                     out=sb_ap(dstt).rearrange("p (k t) -> p k t", k=16),
                     in_=apx(src, r * TC, [[T, 128], [128 * T, 16], [1, TC]]))
            ygld[r] = dmac(c, c.qV, sync,
                           out=sb_ap(xt).rearrange("p (k t) -> p k t", k=8),
                           in_=apx(c.di["x_t"], r * TC, [[T, 128], [128 * T, 8], [1, TC]]))

        for m in range(8):
            @blk.sync
            def _(sync, r=r, m=m):
                for di_, wsrc in enumerate((wb_f, wb_b)):
                    for k in range(16):
                        slot = (di_ * 16 + k) % 32
                        if ring_use[slot] is not None:
                            c.mm.wait(sync, ring_use[slot])
                        ring_load[slot] = dmac(
                            c, c.qWb, sync,
                            out=wring[:, slot * 128:(slot + 1) * 128],
                            in_=apx(wsrc, (k * 128) * D + m * 128,
                                    [[D, 128], [1, 128]]))

            @blk.tensor
            def _(t, r=r, m=m):
                c.qWb.wait(t)
                c.qV.wait(t, ygld[r])
                prev = (r, m - 4) if m >= 4 else (r - 1, m + 4)
                if prev in evacV:
                    c.vv.wait(t, evacV[prev])
                    c.sv.wait(t, evacS[prev])
                last = None
                i = 0
                for di_, ygt in enumerate((ygf, ygb)):
                    for k in range(16):
                        slot = (di_ * 16 + k) % 32
                        last = t.matmul(
                            psl(m % 4),
                            wring[:, slot * 128:(slot + 1) * 128],
                            apx(sb_ap(ygt), k * TC, [[16 * TC, 128], [1, TC]]),
                            start=(i == 0), stop=(i == 31))
                        ring_use[slot] = c.mm.n + 1
                        i += 1
                ow_mm[(r, m)] = c.mm.inc(last)

            @blk.vector
            def _(v, r=r, m=m):
                c.mm.wait(v, ow_mm[(r, m)])
                v.tensor_copy(apx(sb_ap(ao), m * TC, [[8 * TC, 128], [1, TC]]),
                              psl(m % 4))
                v.drain()
                evacV[(r, m)] = c.vv.tick(v)

            @blk.scalar
            def _(s, r=r, m=m):
                c.mm.wait(s, ow_mm[(r, m)])
                s.activation(apx(sb_ap(aosq), m * TC, [[8 * TC, 128], [1, TC]]),
                             psl(m % 4), AF.Square, bias=0.0, scale=1.0)
                s.drain()
                evacS[(r, m)] = c.sv.tick(s)

        @blk.tensor
        def _(t, r=r):
            for m in range(8):
                c.vv.wait(t, evacV[(r, m)])
                c.sv.wait(t, evacS[(r, m)])
            last = None
            for k in range(8):
                last = t.matmul(apx(c.ps[2].ap(), 0, [[1024, 1], [1, TC]]),
                                c.ones_f[:],
                                apx(sb_ap(ao), k * TC, [[8 * TC, 128], [1, TC]]),
                                start=(k == 0), stop=(k == 7))
            for k in range(8):
                last = t.matmul(apx(c.ps[2].ap(), 512, [[1024, 1], [1, TC]]),
                                c.ones_bf[:],
                                apx(sb_ap(aosq), k * TC, [[8 * TC, 128], [1, TC]]),
                                start=(k == 0), stop=(k == 7))
            s1_mm[r] = c.mm.inc(last)

        def emit_ln_rows(r, mm_cnt, which):
            @blk.vector
            def _(v, r=r, mm_cnt=mm_cnt):
                c.mm.wait(v, mm_cnt)
                v.tensor_scalar_mul(srow(MU), apx(c.ps[2].ap(), 0, [[1024, 1], [1, TC]]),
                                    1.0 / D)
                v.tensor_scalar_mul(srow(VA), apx(c.ps[2].ap(), 512, [[1024, 1], [1, TC]]),
                                    1.0 / D)
                v.drain()
                v.tensor_mul(srow(RS), srow(MU), srow(MU))
                v.drain()
                v.tensor_sub(srow(VA), srow(VA), srow(RS))
                v.drain()
                c.vv.tick(v)
            rows1 = c.vv.n

            @blk.scalar
            def _(s, rows1=rows1):
                c.vv.wait(s, rows1)
                s.activation(srow(RS), srow(VA), AF.Ln, bias=1e-5, scale=1.0)
                s.drain()
                s.activation(srow(RS), srow(RS), AF.Exp, bias=0.0, scale=-0.5)
                s.drain()
                c.sv.tick(s)
            rows2 = c.sv.n

            @blk.vector
            def _(v, rows2=rows2):
                c.sv.wait(v, rows2)
                v.tensor_mul(srow(MR), srow(MU), srow(RS))
                v.drain()
                c.vv.tick(v)
            rows3 = c.vv.n

            @blk.scalar
            def _(s, rows3=rows3):
                c.vv.wait(s, rows3)
                c.qG.wait(s)
                dmac(c, c.qS, s, out=apx(c.ds["row"], 0, [[TC, 2], [1, TC]]),
                     in_=apx(sb_ap(stat), RS, [[4 * TC, 1], [TC, 2], [1, TC]]))

            @blk.gpsimd
            def _(g):
                c.qS.wait(g)
                dmac(c, c.qG, g, out=sb_ap(rep).rearrange("p (a t) -> p a t", a=2),
                     in_=apx(c.ds["row"], 0, [[0, 128], [TC, 2], [1, TC]]))
            return c.qG.n

        rep1 = emit_ln_rows(r, s1_mm[r], "ao")

        @blk.vector
        def _(v, r=r, rep1=rep1):
            c.qG.wait(v, rep1)
            rrep = apx(sb_ap(rep), 0, [[2 * TC, 128], [1, TC]])
            mrep = apx(sb_ap(rep), TC, [[2 * TC, 128], [1, TC]])
            for k in range(8):
                aok = apx(sb_ap(ao), k * TC, [[8 * TC, 128], [1, TC]])
                v.tensor_mul(aok, aok, rrep)
            v.drain()
            for k in range(8):
                aok = apx(sb_ap(ao), k * TC, [[8 * TC, 128], [1, TC]])
                xk = apx(sb_ap(xt), k * TC, [[8 * TC, 128], [1, TC]])
                v.tensor_sub(aok, aok, mrep)
            v.drain()
            for k in range(8):
                aok = apx(sb_ap(ao), k * TC, [[8 * TC, 128], [1, TC]])
                xk = apx(sb_ap(xt), k * TC, [[8 * TC, 128], [1, TC]])
                v.tensor_add(xk, xk, aok)
            v.drain()
            apply_done[r] = c.vv.tick(v)

        @blk.scalar
        def _(s, r=r):
            c.vv.wait(s, apply_done[r])
            x2_stored[r] = dmac(
                c, c.qS, s,
                out=apx(c.ds["x2"], r * TC, [[T, 128], [128 * T, 8], [1, TC]]),
                in_=sb_ap(xt).rearrange("p (k t) -> p k t", k=8))
            s.activation(aosq[:], xt[:], AF.Square, bias=0.0, scale=1.0)
            s.drain()
            c.sv.tick(s)
        x2sq = c.sv.n

        @blk.tensor
        def _(t, r=r, x2sq=x2sq):
            c.sv.wait(t, x2sq)
            last = None
            for k in range(8):
                last = t.matmul(apx(c.ps[2].ap(), 0, [[1024, 1], [1, TC]]),
                                c.ones_f[:],
                                apx(sb_ap(xt), k * TC, [[8 * TC, 128], [1, TC]]),
                                start=(k == 0), stop=(k == 7))
            for k in range(8):
                last = t.matmul(apx(c.ps[2].ap(), 512, [[1024, 1], [1, TC]]),
                                c.ones_bf[:],
                                apx(sb_ap(aosq), k * TC, [[8 * TC, 128], [1, TC]]),
                                start=(k == 0), stop=(k == 7))
            c.mm.inc(last)
        rep2 = emit_ln_rows(r, c.mm.n, "x2")

        @blk.vector
        def _(v, r=r, rep2=rep2):
            c.qG.wait(v, rep2)
            rrep = apx(sb_ap(rep), 0, [[2 * TC, 128], [1, TC]])
            mrep = apx(sb_ap(rep), TC, [[2 * TC, 128], [1, TC]])
            for k in range(8):
                xk = apx(sb_ap(xt), k * TC, [[8 * TC, 128], [1, TC]])
                fk = apx(sb_ap(xf0), k * TC, [[8 * TC, 128], [1, TC]])
                v.tensor_sub(fk, xk, mrep)
            v.drain()
            for k in range(8):
                fk = apx(sb_ap(xf0), k * TC, [[8 * TC, 128], [1, TC]])
                v.tensor_mul(fk, fk, rrep)
            v.drain()
            xf_done[r] = c.vv.tick(v)

        @blk.scalar
        def _(s, r=r):
            c.vv.wait(s, xf_done[r])
            dmac(c, c.qS, s,
                 out=apx(c.ds["xf"], r * TC, [[T, 128], [128 * T, 8], [1, TC]]),
                 in_=sb_ap(xf0).rearrange("p (k t) -> p k t", k=8))
            s.drain()
            c.sv.tick(s)

    p2.close()
    _emit_taps(c, ["x2"])


def phase3(c):
    nc, blk, sbuf = c.nc, c.block, c.sbuf
    p3 = ExitStack()
    sb = lambda n, sh, dt: p3.enter_context(nc.sbuf_tensor(n, list(sh), dt))
    f = Ctx()
    f.xfw = sb("f_xfw", [128, 8 * WIN], F32)
    f.xfb = sb("f_xfb", [128, 8 * (WIN + 1)], BF16)
    f.xfb2 = sb("f_xfb2", [128, 8 * (WIN + 1)], BF16)
    f.sq = sb("f_sq", [128, 8 * WIN], BF16)
    f.qro = sb("f_qro", [128, 4 * WIN], BF16)
    f.kro = [sb("f_kro0", [128, 4 * WIN], BF16), sb("f_kro1", [128, 4 * WIN], BF16)]
    f.vT = [sb("f_vT0", [128, 16 * 130], BF16), sb("f_vT1", [128, 16 * 130], BF16)]
    f.tq = sb("f_tq", [128, 2 * WIN], F32)
    f.vs = sb("f_vs", [128, WIN], BF16)
    f.rot = sb("f_rot", [128, 4 * WIN], F32)
    f.maskT = sb("f_maskT", [128, 8 * WIN], BF16)
    f.qa = sb("f_qa", [65, WIN], BF16)
    f.ka = [sb("f_ka0", [65, WIN], BF16), sb("f_ka1", [65, WIN], BF16)]
    f.pT = sb("f_pT", [128, 8 * WIN], BF16)
    f.srep = sb("f_srep", [64, WIN], F32)
    f.serow = sb("f_serow", [1, WIN], F32)
    f.ato = sb("f_ato", [128, 4 * WIN], BF16)
    f.xin = sb("f_xin", [128, 8 * WIN], BF16)
    f.h1 = sb("f_h1", [128, 22 * WIN], BF16)
    f.gg = sb("f_gg", [128, 2 * WIN], BF16)
    f.wring = sb("f_wring", [128, 32 * 128], BF16)
    f.wo = sb("f_wo", [128, 4 * 1024], BF16)
    f.rstdb = sb("f_rstdb", [32, WIN], BF16)
    f.nrm = sb("f_nrm", [32, WIN], F32)
    f.rmneg = sb("f_rmneg", [128, 8], F32)
    f.rows = sb("f_rows", [1, 6 * (WIN + 1)], F32)
    f.rep = sb("f_rep", [128, 2 * WIN], F32)
    f.prevc1 = sb("f_prevc1", [128, 8], BF16)
    f.prevc2 = sb("f_prevc2", [128, 8], BF16)
    f.qs8 = sb("f_qs8", [128, 4], F32)
    f.b1c = sb("f_b1c", [128, 44], F32)
    f.b1cs = sb("f_b1cs", [128, 22], F32)
    f.gb = sb("f_gb", [128, 2 * WIN], F32)
    f.b2c = sb("f_b2c", [128, 8], F32)
    f.lng = sb("f_lng", [128, 8], F32)
    f.lnb = sb("f_lnb", [128, 8], F32)
    psl = c.psl

    marks = [(c.mm, c.mm.n), (c.vv, c.vv.n), (c.sv, c.sv.n), (c.qW, c.qW.n),
             (c.qWb, c.qWb.n), (c.qWc, c.qWc.n), (c.qWd, c.qWd.n),
             (c.qV, c.qV.n), (c.qS, c.qS.n), (c.qG, c.qG.n), (c.qY, c.qY.n)]

    @blk.vector
    def _(v):
        for sm, n in marks:
            v.wait_ge(sm.h, n)
        v.memset(f.ka[0][:], 0.0)
        v.memset(f.ka[1][:], 0.0)
        v.memset(f.vT[0][:], 0.0)
        v.memset(f.vT[1][:], 0.0)
        v.memset(f.rstdb[:], 0.0)
        v.memset(f.nrm[:], 0.0)
        v.drain()
        v.memset(f.ka[0][64:65, :], 1.0)
        v.memset(f.ka[1][64:65, :], 1.0)
        for i in range(2):
            v.memset(apx(sb_ap(f.vT[i]), 64, [[16 * 130, 128], [130, 16], [1, 1]]), 1.0)
            v.memset(apx(sb_ap(f.vT[i]), 129, [[16 * 130, 128], [130, 16], [1, 1]]), 1.0)
        v.drain()
        c.vv.tick(v)
    p3_zero = c.vv.n

    for eng_name in ("scalar", "sync", "gpsimd", "tensor"):
        @getattr(blk, eng_name)
        def _(e):
            c.vv.wait(e, p3_zero)

    @blk.sync
    def _(sync):
        dmac(c, c.qWb, sync, out=f.maskT.ap().rearrange("p (i t) -> p i t", i=8),
             in_=c.di["c_maskT"].rearrange("(i p) t -> p i t", p=128))
        c.qWb.wait(sync)
        c.maskT_ok = c.gg.tick(sync)

    for l in range(min(2, _FFN_LAYERS)):
        _ffn_layer(c, f, l)
    p3.close()
    _emit_taps(c, ["l0_qrot", "l0_krot", "l0_ao", "l0_xa", "l0_xin", "l0_xf", "xf"])


def _ffn_layer(c, f, l):
    nc, blk = c.nc, c.block
    psl = c.psl
    f.qw = c.qWc if l == 0 else c.qWd
    wqkv = c.ds[f"wb_l{l}_wqkv"]
    w1p = c.ds[f"wb_l{l}_w1p"]
    w2p = c.ds[f"wb_l{l}_w2p"]
    wo_src = c.ds[f"wb_l{l}_wo"]

    marks = [(c.mm, c.mm.n), (c.vv, c.vv.n), (c.sv, c.sv.n), (c.qW, f.qw.n),
             (c.qV, c.qV.n), (c.qS, c.qS.n), (c.qG, c.qG.n)]

    @blk.vector
    def _(v):
        for sm, n in marks:
            v.wait_ge(sm.h, n)
        v.memset(f.prevc1[:], 0.0)
        v.memset(f.prevc2[:], 0.0)
        v.memset(f.rows[:], 0.0)
        v.drain()
        c.vv.tick(v)
    l_zero = c.vv.n

    @blk.sync
    def _(sync):
        c.vv.wait(sync, l_zero)
        c.gg.wait(sync, c.precast[wo_src.tensor.name])
        dmac(c, f.qw, sync, out=f.wo.ap().rearrange("p (k m) -> p k m", k=4),
             in_=wo_src.rearrange("(k p) m -> p k m", p=128))
        for col, name in ((0, f"l{l}_q_scale"), (2, f"l{l}_k_scale")):
            src = c.di[name]
            for hb in range(2):
                dmac(c, f.qw, sync, out=f.qs8[hb * 64:hb * 64 + 64, col:col + 1],
                     in_=apx(src, 0, [[1, 64], [1, 1]]))
                dmac(c, f.qw, sync,
                     out=f.qs8[hb * 64:hb * 64 + 32, col + 1:col + 2],
                     in_=apx(src, 32, [[1, 32], [1, 1]]))
                dmac(c, f.qw, sync,
                     out=f.qs8[hb * 64 + 32:hb * 64 + 64, col + 1:col + 2],
                     in_=apx(src, 0, [[1, 32], [1, 1]]))
        dmac(c, f.qw, sync, out=f.b1c[:, 0:44], in_=c.di[f"l{l}_b1p"])
        dmac(c, f.qw, sync, out=f.b2c[:, 0:8], in_=c.di[f"l{l}_b2"])
        dmac(c, f.qw, sync, out=f.lng[:, 0:8], in_=c.di[f"l{l}_ln_g"])
        dmac(c, f.qw, sync, out=f.lnb[:, 0:8], in_=c.di[f"l{l}_ln_b"])
    l_small = f.qw.n

    @blk.vector
    def _(v):
        f.qw.wait(v, l_small)
        v.tensor_scalar_mul(f.qs8[:, 0:2], f.qs8[:, 0:2], 8.0)
        v.drain()
        v.tensor_scalar_mul(f.b1cs[:], f.b1c[:, 22:44], 0.70710678)
        v.drain()
        c.vv.tick(v)
    l_qs = c.vv.n

    for w in range(min(NCH, _FFN_WINDOWS)):
        _ffn_window(c, f, l, w, l_qs)


def _ffn_window(c, f, l, w, l_qs):
    nc, blk = c.nc, c.block
    psl = c.psl
    wqkv = c.ds[f"wb_l{l}_wqkv"]
    w1p = c.ds[f"wb_l{l}_w1p"]
    w2p = c.ds[f"wb_l{l}_w2p"]
    cur = w % 2
    prv = 1 - cur
    st = Ctx()

    @blk.sync
    def _(sync):
        if w > 0:
            c.vv.wait(sync, c._f_xfw_free)
        dmac(c, c.qV, sync,
             out=sb_ap(f.xfw).rearrange("p (k t) -> p k t", k=8),
             in_=apx(c.ds["xf"], w * WIN, [[T, 128], [128 * T, 8], [1, WIN]]))
        dmac(c, c.qV, sync,
             out=sb_ap(f.rot).rearrange("p (a t) -> p a t", a=4),
             in_=apx(c.di["c_rot"], w * WIN, [[T, 128], [128 * T, 4], [1, WIN]]))
    st.xf_ld = c.qV.n
    if _FFN_STOP == "loads":
        c._f_xfw_free = c.vv.n
        return

    @blk.vector
    def _(v):
        c.qV.wait(v, st.xf_ld)
        c.vv.wait(v, l_qs)
        for k in range(8):
            v.tensor_copy(
                apx(sb_ap(f.xfb), k * (WIN + 1), [[8 * (WIN + 1), 128], [1, 1]]),
                apx(sb_ap(f.prevc1), k, [[8, 128], [1, 1]]))
        v.drain()
        for k in range(8):
            v.tensor_copy(
                apx(sb_ap(f.xfb), k * (WIN + 1) + 1, [[8 * (WIN + 1), 128], [1, WIN]]),
                apx(sb_ap(f.xfw), k * WIN, [[8 * WIN, 128], [1, WIN]]))
        v.drain()
        for k in range(8):
            v.tensor_copy(
                apx(sb_ap(f.prevc1), k, [[8, 128], [1, 1]]),
                apx(sb_ap(f.xfw), k * WIN + WIN - 1, [[8 * WIN, 128], [1, 1]]))
        v.drain()
        st.xfb_ok = c.vv.tick(v)
    if _FFN_STOP == "xfb":
        c._f_xfw_free = c.vv.n
        return

    st.l_qs = l_qs
    _qkv_stage(c, f, l, w, st)
    if _FFN_STOP == "":
        _attn_stage(c, f, l, w, st)
        if _FFN_STOP != "attn":
            _wo_stage(c, f, l, w, st)
            if _FFN_STOP != "wo":
                _inner_stage(c, f, l, w, st)
            else:
                st.wo_done = c.vv.n
        else:
            st.wo_done = c.vv.n
    else:
        st.wo_done = c.vv.n

    if l == 0 and any(t in c.taps for t in ("l0_xa", "l0_xin", "l0_xf")):
        @blk.gpsimd
        def _(g):
            c.vv.wait(g)
            c.sv.wait(g)
            if "l0_xin" in c.taps:
                dmac(c, c.qG, g,
                     out=apx(c.ds["l0_xin"], w * WIN, [[T, 128], [128 * T, 8], [1, WIN]]),
                     in_=sb_ap(f.xin).rearrange("p (k t) -> p k t", k=8))
            if "l0_xf" in c.taps:
                dmac(c, c.qG, g,
                     out=apx(c.ds["l0_xf"], w * WIN, [[T, 128], [128 * T, 8], [1, WIN]]),
                     in_=sb_ap(f.xfw).rearrange("p (k t) -> p k t", k=8))

    @blk.scalar
    def _(s):
        c.vv.wait(s)
        st.xf_st = dmac(
            c, c.qS, s,
            out=apx(c.ds["xf"], w * WIN, [[T, 128], [128 * T, 8], [1, WIN]]),
            in_=sb_ap(f.xfw).rearrange("p (k t) -> p k t", k=8))
        s.drain()
        c.sv.tick(s)
    @blk.vector
    def _(v):
        c.qS.wait(v)
        c.qG.wait(v)
        c.vv.tick(v)
    c._f_xfw_free = c.vv.n


def _qkv_stage(c, f, l, w, st):
    nc, blk = c.nc, c.block
    psl = c.psl
    wqkv = c.ds[f"wb_l{l}_wqkv"]
    cur = w % 2
    morder = [0, 4, 1, 5, 2, 6, 3, 7, 8, 12, 9, 13, 10, 14, 11, 15,
              16, 17, 18, 19]
    ring_load = [None] * 32
    ring_use = [None] * 32
    mmc = {}
    freed = {}

    def rhs_ap(k):
        off = k * (WIN + 1) + (1 if k < 4 else 0)
        return apx(sb_ap(f.xfb), off, [[8 * (WIN + 1), 128], [1, WIN]])

    for wave in range(5):
        wm = morder[wave * 4:wave * 4 + 4]

        @blk.sync
        def _(sync, wm=wm, wave=wave):
            if wave == 0:
                c.mm.wait(sync)
                c.vv.wait(sync, st.l_qs)
                if w == 0:
                    c.gg.wait(sync, c.precast[wqkv.tensor.name])
            for i, m in enumerate(wm):
                for k in range(8):
                    slot = (i * 8 + k) % 32
                    if ring_use[slot] is not None:
                        c.mm.wait(sync, ring_use[slot])
                    ring_load[slot] = dmac(
                        c, f.qw, sync,
                        out=f.wring[:, slot * 128:(slot + 1) * 128],
                        in_=apx(wqkv, (k * 128) * QKV_M + m * 128,
                                [[QKV_M, 128], [1, 128]]))

        @blk.tensor
        def _(t, wm=wm, wave=wave):
            f.qw.wait(t)
            if wave == 0:
                c.vv.wait(t, st.xfb_ok)
            for i, m in enumerate(wm):
                sl = (wave * 4 + i) % 4
                for eng, cnt in freed.get(sl, []):
                    (c.vv if eng == "v" else c.sv).wait(t, cnt)
                last = None
                for k in range(8):
                    slot = (i * 8 + k) % 32
                    last = t.matmul(psl(sl), f.wring[:, slot * 128:(slot + 1) * 128],
                                    rhs_ap(k), start=(k == 0), stop=(k == 7))
                    ring_use[slot] = c.mm.n + 1
                mmc[m] = c.mm.inc(last)

        if _FFN_STOP == "qkvmm":
            continue
        if wave < 4:
            is_k = wm[0] >= 8
            for pi in range(2):
                j = wm[pi * 2] if not is_k else wm[pi * 2] - 8
                mq, mp = wm[pi * 2], wm[pi * 2 + 1]
                _norm_rotary_pair(c, f, l, w, st, j, is_k, mq, mp, mmc, freed, psl)
        else:
            if _FFN_STOP == "pairs":
                continue
            for i, m in enumerate(wm):
                _v_transpose(c, f, l, w, st, m - 16, m, mmc, freed, psl)

    st.qkv_done = (c.vv.n, c.sv.n)
    if l == 0 and ("l0_qrot" in c.taps or "l0_krot" in c.taps):
        @blk.gpsimd
        def _(g):
            c.vv.wait(g)
            c.sv.wait(g)
            if "l0_qrot" in c.taps:
                dmac(c, c.qG, g,
                     out=apx(c.ds["l0_qrot"], w * WIN, [[T, 128], [128 * T, 4], [1, WIN]]),
                     in_=sb_ap(f.qro).rearrange("p (j t) -> p j t", j=4))
            if "l0_krot" in c.taps:
                dmac(c, c.qG, g,
                     out=apx(c.ds["l0_krot"], w * WIN, [[T, 128], [128 * T, 4], [1, WIN]]),
                     in_=sb_ap(f.kro[cur]).rearrange("p (j t) -> p j t", j=4))


def _norm_rotary_pair(c, f, l, w, st, j, is_k, mq, mp, mmc, freed, psl):
    nc, blk = c.nc, c.block
    cur = w % 2
    slq = None
    morder = [0, 4, 1, 5, 2, 6, 3, 7, 8, 12, 9, 13, 10, 14, 11, 15, 16, 17, 18, 19]
    slq = morder.index(mq) % 4
    slp = morder.index(mp) % 4
    tq0 = apx(sb_ap(f.tq), 0, [[2 * WIN, 128], [1, WIN]])
    tq1 = apx(sb_ap(f.tq), WIN, [[2 * WIN, 128], [1, WIN]])
    sqt = apx(sb_ap(f.sq), 0, [[8 * WIN, 128], [1, WIN]])
    rows2 = apx(c.ps[2].ap(), 512, [[1024, 32], [1, WIN]])
    repp = apx(c.ps[3].ap(), 0, [[1024, 128], [1, WIN]])

    @blk.vector
    def _(v):
        c.mm.wait(v, mmc[mq])
        c.mm.wait(v, mmc[mp])
        v.tensor_copy(tq0, psl(slq))
        v.tensor_copy(tq1, psl(slp))
        v.drain()
        tick = c.vv.tick(v)
        freed[slp] = [("v", tick)]
        freed[slq] = [("v", tick)]

    @blk.scalar
    def _(s):
        c.mm.wait(s, mmc[mq])
        s.activation(sqt, psl(slq), AF.Square, bias=0.0, scale=1.0)
        s.drain()
        freed[slq].append(("s", c.sv.tick(s)))
    sq_ok = c.sv.n
    if _FFN_STOP == "nr1":
        return

    @blk.tensor
    def _(t):
        c.sv.wait(t, sq_ok)
        if getattr(f, "last_nrm", None) is not None:
            c.vv.wait(t, f.last_nrm)
        if getattr(f, "last_rep_used", None) is not None:
            c.vv.wait(t, f.last_rep_used)
        ins = t.matmul(rows2, c.sel2[:], sqt, start=True, stop=True)
        c.mm.inc(ins)
    ss_mm = c.mm.n

    @blk.vector
    def _(v):
        c.mm.wait(v, ss_mm)
        v.tensor_scalar_max(f.nrm[0:2, :], apx(c.ps[2].ap(), 512, [[1024, 2], [1, WIN]]),
                            1e-24)
        v.drain()
        c.vv.tick(v)
    nrm_ok0 = c.vv.n
    f.last_nrm = nrm_ok0

    @blk.scalar
    def _(s):
        c.vv.wait(s, nrm_ok0)
        s.activation(f.nrm[0:2, :], f.nrm[0:2, :], AF.Ln, bias=0.0, scale=1.0)
        s.drain()
        s.activation(f.nrm[0:2, :], f.nrm[0:2, :], AF.Exp, bias=0.0, scale=-0.5)
        s.drain()
        c.sv.tick(s)
    nrm_ok = c.sv.n
    if _FFN_STOP == "nr2":
        return

    @blk.vector
    def _(v):
        c.sv.wait(v, nrm_ok)
        v.tensor_copy(f.rstdb[0:2, :], f.nrm[0:2, :])
        v.drain()
        c.vv.tick(v)
    rstd_ok = c.vv.n
    if _FFN_STOP == "nr3":
        return

    @blk.tensor
    def _(t):
        c.vv.wait(t, rstd_ok)
        ins = t.matmul(repp, c.sel2t[:], f.rstdb[:], start=True, stop=True)
        c.mm.inc(ins)
    rep_mm = c.mm.n
    if _FFN_STOP == "nr4":
        @blk.vector
        def _(v):
            c.mm.wait(v, rep_mm)
            f.last_rep_used = c.vv.tick(v)
        return

    scol = 2 if is_k else 0
    rota = 2 if is_k else 0

    @blk.vector
    def _(v):
        c.mm.wait(v, rep_mm)
        c.qV.wait(v, st.xf_ld)
        v.scalar_tensor_tensor(tq0, tq0, f.qs8[:, scol:scol + 1], repp,
                               ALU.mult, ALU.mult)
        v.drain()
        v.scalar_tensor_tensor(tq1, tq1, f.qs8[:, scol + 1:scol + 2], repp,
                               ALU.mult, ALU.mult)
        v.drain()
        f.last_rep_used = c.vv.tick(v)
        cs = apx(sb_ap(f.rot), rota * WIN, [[4 * WIN, 128], [1, WIN]])
        ss = apx(sb_ap(f.rot), (rota + 1) * WIN, [[4 * WIN, 128], [1, WIN]])
        v.tensor_mul(tq0, tq0, cs)
        v.tensor_mul(tq1, tq1, ss)
        v.drain()
        dst = f.kro[cur] if is_k else f.qro
        v.tensor_add(apx(sb_ap(dst), j * WIN, [[4 * WIN, 128], [1, WIN]]),
                     tq0, tq1)
        v.drain()
        c.vv.tick(v)


def _v_transpose(c, f, l, w, st, j, m, mmc, freed, psl):
    nc, blk = c.nc, c.block
    cur = w % 2
    morder = [0, 4, 1, 5, 2, 6, 3, 7, 8, 12, 9, 13, 10, 14, 11, 15, 16, 17, 18, 19]
    sl = morder.index(m) % 4
    ptr128 = apx(c.ps[3].ap().bitcast(BF16), 1024, [[2048, 128], [1, 128]])

    @blk.vector
    def _(v):
        c.mm.wait(v, mmc[m])
        v.tensor_copy(f.vs[:], psl(sl))
        v.drain()
        freed[sl] = [("v", c.vv.tick(v))]
    vs_ok = c.vv.n

    for tsub in range(4):
        @blk.tensor
        def _(t, tsub=tsub):
            c.vv.wait(t, vs_ok)
            if tsub > 0:
                c.vv.wait(t)
            ins = t.transpose(ptr128, f.vs[:, tsub * 128:(tsub + 1) * 128],
                              c.ident[:])
            c.mm.inc(ins)
        tmm = c.mm.n

        @blk.vector
        def _(v, tsub=tsub):
            c.mm.wait(v, tmm)
            tile = (tsub * 4 + j) * 130
            v.tensor_copy(
                apx(sb_ap(f.vT[cur]), tile, [[16 * 130, 128], [65, 2], [1, 64]]),
                apx(ptr128, 0, [[2048, 128], [64, 2], [1, 64]]))
            v.drain()
            c.vv.tick(v)


def _attn_stage(c, f, l, w, st):
    nc, blk = c.nc, c.block
    psl = c.psl
    cur = w % 2
    prv = 1 - cur
    halves = ([0, 1] if w > 0 else [1])
    rmax = c.ds["rmax"]
    sume = c.ds["sumexp"]
    pf = {}
    st.ato_ticks = []

    for h in range(HEADS):
        jt = h // 2
        hoff = (h % 2) * 64
        poff = hoff * 4 * WIN

        def qro_lhs(qs_):
            return apx(sb_ap(f.qro), poff + jt * WIN + qs_ * 128,
                       [[4 * WIN, 64], [1, 128]])

        def kro_rhs(hb):
            buf = f.kro[cur] if hb == 1 else f.kro[prv]
            return apx(sb_ap(buf), poff + jt * WIN, [[4 * WIN, 64], [1, WIN]])

        for qs_ in range(4):
            @blk.tensor
            def _(t, h=h, qs_=qs_):
                if h == 0 and qs_ == 0:
                    c.vv.wait(t)
                    c.sv.wait(t)
                for sl in (0, 1):
                    if sl in pf:
                        eng, cnt = pf[sl]
                        (c.vv if eng == "v" else c.sv).wait(t, cnt)
                last = None
                for hi, hb in enumerate(halves):
                    last = t.matmul(psl(hi if len(halves) == 2 else 1),
                                    qro_lhs(qs_), kro_rhs(hb),
                                    start=True, stop=True)
                c.mm.inc(last)
            mm1 = c.mm.n

            @blk.vector
            def _(v, h=h, qs_=qs_, mm1=mm1):
                c.mm.wait(v, mm1)
                if qs_ == 0:
                    c.qS.wait(v)
                if len(halves) == 2:
                    v.reduce_max(f.rmneg[:, 6:7], psl(0), AX.X)
                    v.reduce_max(f.rmneg[:, 7:8], psl(1), AX.X)
                    v.drain()
                    v.tensor_max(f.rmneg[:, qs_:qs_ + 1], f.rmneg[:, 6:7],
                                 f.rmneg[:, 7:8])
                else:
                    v.reduce_max(f.rmneg[:, qs_:qs_ + 1], psl(1), AX.X)
                v.drain()
                v.tensor_scalar_mul(f.rmneg[:, qs_:qs_ + 1],
                                    f.rmneg[:, qs_:qs_ + 1], -1.0)
                v.drain()
                tick = c.vv.tick(v)
                pf[0] = ("v", tick)
                pf[1] = ("v", tick)

        @blk.scalar
        def _(s, h=h):
            c.vv.wait(s)
            for qs_ in range(4):
                dmac(c, c.qS, s,
                     out=apx(rmax, ((w * 8 + h) * 4 + qs_) * 128,
                             [[1, 128], [1, 1]]),
                     in_=f.rmneg[:, qs_:qs_ + 1])
        rm_st = c.qS.n

        @blk.vector
        def _(v, h=h):
            if getattr(f, "prev_simT", None) is not None:
                c.mm.wait(v, f.prev_simT)
            v.tensor_copy(f.qa[0:64, :],
                          apx(sb_ap(f.qro), poff + jt * WIN, [[4 * WIN, 64], [1, WIN]]))
            for hb in halves:
                buf = f.kro[cur] if hb == 1 else f.kro[prv]
                v.tensor_copy(f.ka[hb][0:64, :],
                              apx(sb_ap(buf), poff + jt * WIN, [[4 * WIN, 64], [1, WIN]]))
            v.drain()
            c.vv.tick(v)
        qa_rows = c.vv.n

        @blk.gpsimd
        def _(g, h=h):
            c.qS.wait(g, rm_st)
            c.vv.wait(g, qa_rows)
            if getattr(f, "prev_simT", None) is not None:
                c.mm.wait(g, f.prev_simT)
            dmac(c, c.qG, g, out=f.qa[64:65, :],
                 in_=apx(rmax, (w * 8 + h) * 512, [[512, 1], [1, 512]]))
        qa_ok = c.qG.n

        mask_ticks = []
        for hb in halves:
            for ks_ in range(4):
                sl = 4 + ((ks_ + hb * 4) % 2)
                @blk.tensor
                def _(t, h=h, hb=hb, ks_=ks_, sl=sl):
                    c.qG.wait(t, qa_ok)
                    if sl in pf:
                        eng, cnt = pf[sl]
                        (c.vv if eng == "v" else c.sv).wait(t, cnt)
                    ins = t.matmul(psl(sl), f.ka[hb][:, ks_ * 128:(ks_ + 1) * 128],
                                   f.qa[:], start=True, stop=True)
                    c.mm.inc(ins)
                mmT = c.mm.n

                @blk.scalar
                def _(s, hb=hb, ks_=ks_, sl=sl, mmT=mmT):
                    c.mm.wait(s, mmT)
                    if getattr(f, "prev_av", None) is not None:
                        c.mm.wait(s, f.prev_av)
                    idx = hb * 4 + ks_
                    s.activation(
                        apx(sb_ap(f.pT), idx * WIN, [[8 * WIN, 128], [1, WIN]]),
                        psl(sl), AF.Exp, bias=0.0, scale=1.0)
                    s.drain()
                    pf[sl] = ("s", c.sv.tick(s))
                exp_ok = c.sv.n

                @blk.vector
                def _(v, hb=hb, ks_=ks_, exp_ok=exp_ok, h=h):
                    c.sv.wait(v, exp_ok)
                    if h == 0 and not mask_ticks:
                        c.gg.wait(v, c.maskT_ok)
                    idx = hb * 4 + ks_
                    pt = apx(sb_ap(f.pT), idx * WIN, [[8 * WIN, 128], [1, WIN]])
                    mk = apx(sb_ap(f.maskT), idx * WIN, [[8 * WIN, 128], [1, WIN]])
                    v.tensor_mul(pt, pt, mk)
                    v.drain()
                    mask_ticks.append(c.vv.tick(v))

        f.prev_simT = c.mm.n

        @blk.tensor
        def _(t, h=h):
            c.vv.wait(t, mask_ticks[-1])
            if 2 in pf:
                eng, cnt = pf[2]
                (c.vv if eng == "v" else c.sv).wait(t, cnt)
            last = None
            first = True
            for hb in halves:
                buf = f.vT[cur] if hb == 1 else f.vT[prv]
                for ks_ in range(4):
                    idx = hb * 4 + ks_
                    lhsT = apx(sb_ap(buf), (ks_ * 4 + jt) * 130 + (h % 2) * 65,
                               [[16 * 130, 128], [1, 65]])
                    rhs = apx(sb_ap(f.pT), idx * WIN, [[8 * WIN, 128], [1, WIN]])
                    last = t.matmul(apx(c.ps[1].ap(), 0, [[1024, 65], [1, WIN]]),
                                    lhsT, rhs, start=first,
                                    stop=(hb == halves[-1] and ks_ == 3))
                    first = False
            c.mm.inc(last)
        av_mm = c.mm.n
        f.prev_av = av_mm

        @blk.vector
        def _(v, h=h, av_mm=av_mm):
            c.mm.wait(v, av_mm)
            v.tensor_copy(f.serow[:], apx(c.ps[1].ap(), 64 * 1024, [[1024, 1], [1, WIN]]))
            v.drain()
            c.vv.tick(v)
        se_cp = c.vv.n

        @blk.scalar
        def _(s, h=h, se_cp=se_cp):
            c.vv.wait(s, se_cp)
            dmac(c, c.qS, s,
                 out=apx(sume, (w * 8 + h) * 512, [[512, 1], [1, 512]]),
                 in_=f.serow[:])
        se_st = c.qS.n

        @blk.gpsimd
        def _(g, h=h, se_st=se_st):
            c.qS.wait(g, se_st)
            if getattr(f, "srep_used", None) is not None:
                c.vv.wait(g, f.srep_used)
            dmac(c, c.qG, g, out=f.srep[:],
                 in_=apx(sume, (w * 8 + h) * 512, [[0, 64], [1, 512]]))
        srep_ok = c.qG.n

        @blk.vector
        def _(v, h=h, srep_ok=srep_ok):
            c.qG.wait(v, srep_ok)
            v.reciprocal(f.srep[:], f.srep[:])
            v.drain()
            out = apx(sb_ap(f.ato), poff + jt * WIN, [[4 * WIN, 64], [1, WIN]])
            v.tensor_mul(out, apx(c.ps[1].ap(), 0, [[1024, 64], [1, WIN]]),
                         f.srep[:])
            v.drain()
            tick = c.vv.tick(v)
            f.srep_used = tick
            pf[2] = ("v", tick)
            st.ato_ticks.append(tick)


def _wo_stage(c, f, l, w, st):
    nc, blk = c.nc, c.block
    psl = c.psl
    pf = {}
    evt = {}

    for m in range(8):
        @blk.tensor
        def _(t, m=m):
            if m == 0:
                c.vv.wait(t, st.ato_ticks[-1])
            if m - 2 in evt:
                c.vv.wait(t, evt[m - 2])
            last = None
            for k in range(4):
                last = t.matmul(
                    psl(m % 2),
                    f.wo[:, k * 1024 + m * 128:k * 1024 + m * 128 + 128],
                    apx(sb_ap(f.ato), k * WIN, [[4 * WIN, 128], [1, WIN]]),
                    start=(k == 0), stop=(k == 3))
            c.mm.inc(last)
        mmw = c.mm.n

        @blk.vector
        def _(v, m=m, mmw=mmw):
            c.mm.wait(v, mmw)
            xk = apx(sb_ap(f.xfw), m * WIN, [[8 * WIN, 128], [1, WIN]])
            v.tensor_add(xk, xk, psl(m % 2))
            v.drain()
            evt[m] = c.vv.tick(v)

    st.wo_done = evt[7]
    if l == 0 and "l0_xa" in c.taps:
        @blk.gpsimd
        def _(g):
            c.vv.wait(g, st.wo_done)
            dmac(c, c.qG, g,
                 out=apx(c.ds["l0_xa"], w * WIN, [[T, 128], [128 * T, 8], [1, WIN]]),
                 in_=sb_ap(f.xfw).rearrange("p (k t) -> p k t", k=8))
        st.xa_tap = c.qG.n


def _inner_stage(c, f, l, w, st):
    nc, blk = c.nc, c.block
    psl = c.psl
    R_S1H, R_S2H = 4 * (WIN + 1), 5 * (WIN + 1)
    MU, VA, RS, MR = 0, WIN + 1, 2 * (WIN + 1), 3 * (WIN + 1)

    def srow(off, n=WIN):
        return apx(sb_ap(f.rows), off, [[6 * (WIN + 1), 1], [1, n]])

    @blk.vector
    def _(v):
        c.vv.wait(v, st.wo_done)
        if getattr(st, "xa_tap", None) is not None:
            c.qG.wait(v)
        for k in range(8):
            v.tensor_copy(
                apx(sb_ap(f.xfb2), k * (WIN + 1), [[8 * (WIN + 1), 128], [1, 1]]),
                apx(sb_ap(f.prevc2), k, [[8, 128], [1, 1]]))
        v.drain()
        for k in range(8):
            v.tensor_copy(
                apx(sb_ap(f.xfb2), k * (WIN + 1) + 1, [[8 * (WIN + 1), 128], [1, WIN]]),
                apx(sb_ap(f.xfw), k * WIN, [[8 * WIN, 128], [1, WIN]]))
        v.drain()
        for k in range(8):
            v.tensor_copy(
                apx(sb_ap(f.prevc2), k, [[8, 128], [1, 1]]),
                apx(sb_ap(f.xfw), k * WIN + WIN - 1, [[8 * WIN, 128], [1, 1]]))
        v.drain()
        c.vv.tick(v)
    xfb2_ok = c.vv.n
    if _FFN_STOP == "i0a":
        return

    @blk.scalar
    def _(s):
        c.vv.wait(s, xfb2_ok)
        s.activation(f.sq[:], f.xfw[:], AF.Square, bias=0.0, scale=1.0)
        s.drain()
        c.sv.tick(s)
    sq_ok = c.sv.n

    @blk.tensor
    def _(t):
        c.sv.wait(t, sq_ok)
        c.vv.wait(t)
        last = None
        for half in range(2):
            for k in range(4):
                kk = half * 4 + k
                last = t.matmul(
                    apx(c.ps[2].ap(), half * 512, [[1024, 1], [1, WIN]]),
                    c.ones_f[:],
                    apx(sb_ap(f.xfw), kk * WIN, [[8 * WIN, 128], [1, WIN]]),
                    start=(k == 0), stop=(k == 3))
        for half in range(2):
            for k in range(4):
                kk = half * 4 + k
                last = t.matmul(
                    apx(c.ps[3].ap(), half * 512, [[1024, 1], [1, WIN]]),
                    c.ones_bf[:],
                    apx(sb_ap(f.sq), kk * WIN, [[8 * WIN, 128], [1, WIN]]),
                    start=(k == 0), stop=(k == 3))
        c.mm.inc(last)
    st_mm = c.mm.n

    @blk.vector
    def _(v):
        c.mm.wait(v, st_mm)
        v.tensor_copy(srow(R_S1H, 1), srow(R_S1H + WIN, 1))
        v.tensor_copy(srow(R_S2H, 1), srow(R_S2H + WIN, 1))
        v.drain()
        v.tensor_copy(srow(R_S1H + 1), apx(c.ps[2].ap(), 512, [[1024, 1], [1, WIN]]))
        v.tensor_copy(srow(R_S2H + 1), apx(c.ps[3].ap(), 512, [[1024, 1], [1, WIN]]))
        v.drain()
        v.tensor_add(srow(MU), apx(c.ps[2].ap(), 0, [[1024, 1], [1, WIN]]),
                     srow(R_S1H))
        v.tensor_add(srow(VA), apx(c.ps[3].ap(), 0, [[1024, 1], [1, WIN]]),
                     srow(R_S2H))
        v.drain()
        v.tensor_scalar_mul(srow(MU), srow(MU), 1.0 / D)
        v.tensor_scalar_mul(srow(VA), srow(VA), 1.0 / D)
        v.drain()
        v.tensor_mul(srow(RS), srow(MU), srow(MU))
        v.drain()
        v.tensor_sub(srow(VA), srow(VA), srow(RS))
        v.drain()
        c.vv.tick(v)
    rows1 = c.vv.n
    if _FFN_STOP == "i0c":
        return

    @blk.scalar
    def _(s):
        c.vv.wait(s, rows1)
        s.activation(srow(RS), srow(VA), AF.Ln, bias=1e-5, scale=1.0)
        s.drain()
        s.activation(srow(RS), srow(RS), AF.Exp, bias=0.0, scale=-0.5)
        s.drain()
        c.sv.tick(s)
    rows2_ok = c.sv.n

    @blk.vector
    def _(v):
        c.sv.wait(v, rows2_ok)
        v.tensor_mul(srow(MR), srow(MU), srow(RS))
        v.drain()
        c.vv.tick(v)
    rows3 = c.vv.n

    @blk.scalar
    def _(s):
        c.vv.wait(s, rows3)
        c.qG.wait(s)
        dmac(c, c.qS, s, out=apx(c.ds["row"], 0, [[WIN + 1, 2], [1, WIN]]),
             in_=apx(sb_ap(f.rows), RS, [[6 * (WIN + 1), 1], [WIN + 1, 2], [1, WIN]]))

    @blk.gpsimd
    def _(g):
        c.qS.wait(g)
        dmac(c, c.qG, g, out=sb_ap(f.rep).rearrange("p (a t) -> p a t", a=2),
             in_=apx(c.ds["row"], 0, [[0, 128], [WIN + 1, 2], [1, WIN]]))
    rep_ok = c.qG.n
    if _FFN_STOP == "i0d":
        return

    @blk.vector
    def _(v):
        c.qG.wait(v, rep_ok)
        rrep = apx(sb_ap(f.rep), 0, [[2 * WIN, 128], [1, WIN]])
        mrep = apx(sb_ap(f.rep), WIN, [[2 * WIN, 128], [1, WIN]])
        for k in range(8):
            off = k * (WIN + 1) + (1 if k < 4 else 0)
            xsh = apx(sb_ap(f.xfb2), off, [[8 * (WIN + 1), 128], [1, WIN]])
            tq0 = apx(sb_ap(f.tq), 0, [[2 * WIN, 128], [1, WIN]])
            xk = apx(sb_ap(f.xin), k * WIN, [[8 * WIN, 128], [1, WIN]])
            v.tensor_sub(tq0, xsh, mrep)
            v.drain()
            v.tensor_mul(tq0, tq0, rrep)
            v.drain()
            v.tensor_scalar(xk, tq0, f.lng[:, k:k + 1], f.lnb[:, k:k + 1],
                            ALU.mult, ALU.add)
            v.drain()
        c.vv.tick(v)
    xin_ok = c.vv.n
    if _FFN_STOP == "i1":
        return

    w1p = c.ds[f"wb_l{l}_w1p"]
    ring_load = [None] * 32
    ring_use = [None] * 32
    pf = {}
    gg_tick = {}
    h1_tick = {}
    NM = HIDP // 128
    morder = []
    for j in range(NM):
        morder += [NM + j, j]
    for wave in range(11):
        wm = morder[wave * 4:wave * 4 + 4]

        @blk.sync
        def _(sync, wm=wm, wave=wave):
            if wave == 0:
                c.mm.wait(sync)
                if w == 0:
                    c.gg.wait(sync, c.precast[w1p.tensor.name])
            for i, m in enumerate(wm):
                for k in range(8):
                    slot = (i * 8 + k) % 32
                    if ring_use[slot] is not None:
                        c.mm.wait(sync, ring_use[slot])
                    ring_load[slot] = dmac(
                        c, f.qw, sync,
                        out=f.wring[:, slot * 128:(slot + 1) * 128],
                        in_=apx(w1p, (k * 128) * (2 * HIDP) + m * 128,
                                [[2 * HIDP, 128], [1, 128]]))

        mmcs = {}

        @blk.tensor
        def _(t, wm=wm, wave=wave):
            f.qw.wait(t)
            if wave == 0:
                c.vv.wait(t, xin_ok)
            for i, m in enumerate(wm):
                sl = (wave * 4 + i) % 4
                if sl in pf:
                    eng, cnt = pf[sl]
                    (c.vv if eng == "v" else c.sv).wait(t, cnt)
                last = None
                for k in range(8):
                    slot = (i * 8 + k) % 32
                    last = t.matmul(psl(sl), f.wring[:, slot * 128:(slot + 1) * 128],
                                    apx(sb_ap(f.xin), k * WIN, [[8 * WIN, 128], [1, WIN]]),
                                    start=(k == 0), stop=(k == 7))
                    ring_use[slot] = c.mm.n + 1
                mmcs[m] = c.mm.inc(last)

        for i, m in enumerate(wm):
            sl = (wave * 4 + i) % 4
            if m >= NM:
                j = m - NM

                @blk.scalar
                def _(sx, j=j, sl=sl, m=m, mmcs=mmcs):
                    c.mm.wait(sx, mmcs[m])
                    if getattr(f, f"gg_used_{j % 2}", None) is not None:
                        c.vv.wait(sx, getattr(f, f"gg_used_{j % 2}"))
                    sx.activation(
                        apx(sb_ap(f.gg), (j % 2) * WIN, [[2 * WIN, 128], [1, WIN]]),
                        psl(sl), AF.Erf, bias=f.b1cs[:, j:j + 1],
                        scale=0.70710678)
                    sx.drain()
                    pf[sl] = ("s", c.sv.tick(sx))
                gg_tick[j] = c.sv.n

                @blk.vector
                def _(vx, j=j, sl=sl, m=m, mmcs=mmcs):
                    c.mm.wait(vx, mmcs[m])
                    if getattr(f, f"gb_used_{j % 2}", None) is not None:
                        c.vv.wait(vx, getattr(f, f"gb_used_{j % 2}"))
                    vx.tensor_scalar(
                        apx(sb_ap(f.gb), (j % 2) * WIN, [[2 * WIN, 128], [1, WIN]]),
                        psl(sl), f.b1c[:, NM + j:NM + j + 1], 0.5,
                        ALU.add, ALU.mult)
                    vx.drain()
                    tick = c.vv.tick(vx)
                    pf[sl] = ("v", tick)
                gb_tick = {}
                gb_tick[j] = c.vv.n
                setattr(f, f"gbt_{j}", c.vv.n)
            else:
                j = m

                @blk.vector
                def _(vx, j=j, sl=sl, m=m, mmcs=mmcs):
                    c.mm.wait(vx, mmcs[m])
                    c.sv.wait(vx, gg_tick[j])
                    vx.scalar_tensor_tensor(
                        apx(sb_ap(f.gg), (j % 2) * WIN, [[2 * WIN, 128], [1, WIN]]),
                        apx(sb_ap(f.gg), (j % 2) * WIN, [[2 * WIN, 128], [1, WIN]]),
                        1.0,
                        apx(sb_ap(f.gb), (j % 2) * WIN, [[2 * WIN, 128], [1, WIN]]),
                        ALU.add, ALU.mult)
                    vx.drain()
                    vx.scalar_tensor_tensor(
                        apx(sb_ap(f.h1), j * WIN, [[22 * WIN, 128], [1, WIN]]),
                        psl(sl), f.b1c[:, j:j + 1],
                        apx(sb_ap(f.gg), (j % 2) * WIN, [[2 * WIN, 128], [1, WIN]]),
                        ALU.add, ALU.mult)
                    vx.drain()
                    tick = c.vv.tick(vx)
                    pf[sl] = ("v", tick)
                    setattr(f, f"gg_used_{j % 2}", tick)
                    setattr(f, f"gb_used_{j % 2}", tick)
                h1_tick[j] = c.vv.n

    if _FFN_STOP == "i2":
        return
    w2p = c.ds[f"wb_l{l}_w2p"]
    ring_load2 = [None] * 32
    ring_use2 = [None] * 32
    pf2 = {}
    w2mm = {}
    for m in range(8):
        @blk.sync
        def _(sync, m=m):
            if m == 0:
                c.mm.wait(sync)
                if w == 0:
                    c.gg.wait(sync, c.precast[w2p.tensor.name])
            else:
                c.mm.wait(sync, w2mm[m - 1])
            for k in range(22):
                slot = (m * 22 + k) % 32
                if ring_use2[slot] is not None:
                    c.mm.wait(sync, ring_use2[slot])
                ring_load2[slot] = dmac(
                    c, f.qw, sync,
                    out=f.wring[:, slot * 128:(slot + 1) * 128],
                    in_=apx(w2p, (k * 128) * D + m * 128, [[D, 128], [1, 128]]))

        @blk.tensor
        def _(t, m=m):
            f.qw.wait(t)
            if m == 0:
                c.vv.wait(t, h1_tick[NM - 1])
            sl = m % 2
            if sl in pf2:
                c.vv.wait(t, pf2[sl])
            last = None
            for k in range(22):
                slot = (m * 22 + k) % 32
                last = t.matmul(psl(sl), f.wring[:, slot * 128:(slot + 1) * 128],
                                apx(sb_ap(f.h1), k * WIN, [[22 * WIN, 128], [1, WIN]]),
                                start=(k == 0), stop=(k == 21))
                ring_use2[slot] = c.mm.n + 1
            c.mm.inc(last)
        mmw = c.mm.n
        w2mm[m] = mmw

        @blk.vector
        def _(v, m=m, mmw=mmw):
            c.mm.wait(v, mmw)
            if m == 0 and getattr(st, "xa_tap", None) is not None:
                c.qG.wait(v)
            xk = apx(sb_ap(f.xfw), m * WIN, [[8 * WIN, 128], [1, WIN]])
            v.scalar_tensor_tensor(xk, psl(m % 2), f.b2c[:, m:m + 1], xk,
                                   ALU.add, ALU.add)
            v.drain()
            pf2[m % 2] = c.vv.tick(v)


def phase4(c):
    nc, blk = c.nc, c.block
    p4 = ExitStack()
    sb = lambda n, sh, dt: p4.enter_context(nc.sbuf_tensor(n, list(sh), dt))
    xf = sb("p4_xf", [128, 8 * TC], F32)
    x2 = sb("p4_x2", [128, 8 * TC], F32)
    sq = sb("p4_sq", [128, 8 * TC], BF16)
    stat = sb("p4_stat", [1, 4 * TC], F32)
    rep = sb("p4_rep", [128, 2 * TC], F32)
    psl = c.psl
    MU, VA, RS, MR = 0, TC, 2 * TC, 3 * TC

    def srow(off):
        return apx(sb_ap(stat), off, [[4 * TC, 1], [1, TC]])

    marks = [(c.mm, c.mm.n), (c.vv, c.vv.n), (c.sv, c.sv.n), (c.qW, c.qW.n),
             (c.qWb, c.qWb.n), (c.qWc, c.qWc.n), (c.qWd, c.qWd.n),
             (c.qV, c.qV.n), (c.qS, c.qS.n), (c.qG, c.qG.n)]

    @blk.vector
    def _(v):
        for sm, n in marks:
            v.wait_ge(sm.h, n)
        c.vv.tick(v)
    p4_zero = c.vv.n

    for eng_name in ("scalar", "sync", "gpsimd", "tensor"):
        @getattr(blk, eng_name)
        def _(e):
            c.vv.wait(e, p4_zero)

    done = [None] * NCH
    for r in range(NCH):
        @blk.sync
        def _(sync, r=r):
            if r > 0:
                c.vv.wait(sync, done[r - 1])
            dmac(c, c.qV, sync,
                 out=sb_ap(xf).rearrange("p (k t) -> p k t", k=8),
                 in_=apx(c.ds["xf"], r * TC, [[T, 128], [128 * T, 8], [1, TC]]))
            dmac(c, c.qV, sync,
                 out=sb_ap(x2).rearrange("p (k t) -> p k t", k=8),
                 in_=apx(c.ds["x2"], r * TC, [[T, 128], [128 * T, 8], [1, TC]]))
        ld = c.qV.n

        @blk.scalar
        def _(s, r=r, ld=ld):
            c.qV.wait(s, ld)
            s.activation(sq[:], xf[:], AF.Square, bias=0.0, scale=1.0)
            s.drain()
            c.sv.tick(s)
        sq_ok = c.sv.n

        @blk.tensor
        def _(t, r=r, sq_ok=sq_ok):
            c.sv.wait(t, sq_ok)
            c.vv.wait(t)
            last = None
            for k in range(8):
                last = t.matmul(apx(c.ps[2].ap(), 0, [[1024, 1], [1, TC]]),
                                c.ones_f[:],
                                apx(sb_ap(xf), k * TC, [[8 * TC, 128], [1, TC]]),
                                start=(k == 0), stop=(k == 7))
            for k in range(8):
                last = t.matmul(apx(c.ps[2].ap(), 512, [[1024, 1], [1, TC]]),
                                c.ones_bf[:],
                                apx(sb_ap(sq), k * TC, [[8 * TC, 128], [1, TC]]),
                                start=(k == 0), stop=(k == 7))
            c.mm.inc(last)
        st_mm = c.mm.n

        @blk.vector
        def _(v, r=r, st_mm=st_mm):
            c.mm.wait(v, st_mm)
            v.tensor_scalar_mul(srow(MU), apx(c.ps[2].ap(), 0, [[1024, 1], [1, TC]]),
                                1.0 / D)
            v.tensor_scalar_mul(srow(VA), apx(c.ps[2].ap(), 512, [[1024, 1], [1, TC]]),
                                1.0 / D)
            v.drain()
            v.tensor_mul(srow(RS), srow(MU), srow(MU))
            v.drain()
            v.tensor_sub(srow(VA), srow(VA), srow(RS))
            v.drain()
            c.vv.tick(v)
        rows1 = c.vv.n

        @blk.scalar
        def _(s, rows1=rows1):
            c.vv.wait(s, rows1)
            s.activation(srow(RS), srow(VA), AF.Ln, bias=1e-5, scale=1.0)
            s.drain()
            s.activation(srow(RS), srow(RS), AF.Exp, bias=0.0, scale=-0.5)
            s.drain()
            c.sv.tick(s)
        rows2 = c.sv.n

        @blk.vector
        def _(v, rows2=rows2):
            c.sv.wait(v, rows2)
            v.tensor_mul(srow(MR), srow(MU), srow(RS))
            v.drain()
            c.vv.tick(v)
        rows3 = c.vv.n

        @blk.scalar
        def _(s, rows3=rows3):
            c.vv.wait(s, rows3)
            c.qG.wait(s)
            dmac(c, c.qS, s, out=apx(c.ds["row"], 0, [[TC, 2], [1, TC]]),
                 in_=apx(sb_ap(stat), RS, [[4 * TC, 1], [TC, 2], [1, TC]]))

        @blk.gpsimd
        def _(g):
            c.qS.wait(g)
            dmac(c, c.qG, g, out=sb_ap(rep).rearrange("p (a t) -> p a t", a=2),
                 in_=apx(c.ds["row"], 0, [[0, 128], [TC, 2], [1, TC]]))
        rep_ok = c.qG.n

        @blk.vector
        def _(v, r=r, rep_ok=rep_ok):
            c.qG.wait(v, rep_ok)
            rrep = apx(sb_ap(rep), 0, [[2 * TC, 128], [1, TC]])
            mrep = apx(sb_ap(rep), TC, [[2 * TC, 128], [1, TC]])
            for k in range(8):
                fk = apx(sb_ap(xf), k * TC, [[8 * TC, 128], [1, TC]])
                v.tensor_mul(fk, fk, rrep)
            v.drain()
            for k in range(8):
                fk = apx(sb_ap(xf), k * TC, [[8 * TC, 128], [1, TC]])
                v.tensor_sub(fk, fk, mrep)
            v.drain()
            for k in range(8):
                fk = apx(sb_ap(xf), k * TC, [[8 * TC, 128], [1, TC]])
                xk = apx(sb_ap(x2), k * TC, [[8 * TC, 128], [1, TC]])
                v.tensor_add(fk, fk, xk)
            v.drain()
            c.vv.tick(v)
        app = c.vv.n

        @blk.scalar
        def _(s, r=r, app=app):
            c.vv.wait(s, app)
            dmac(c, c.qS, s,
                 out=apx(c.out_t, r * TC, [[T, 128], [128 * T, 8], [1, TC]]),
                 in_=sb_ap(xf).rearrange("p (k t) -> p k t", k=8))
            s.drain()
            c.sv.tick(s)
        done[r] = c.vv.n

        @blk.vector
        def _(v, r=r):
            c.sv.wait(v)
            c.qS.wait(v)
            c.vv.tick(v)
        done[r] = c.vv.n

    p4.close()


def _finish(c):
    blk = c.block

    @blk.gpsimd
    def _(g):
        for sm in (c.qS, c.qG, c.qV, c.qW, c.qWb, c.qWc, c.qWd, c.qP, c.qY):
            g.wait_ge(sm.h, sm.n)
        c.gg.tick(g)

    @blk.vector
    def _(v):
        c.gg.wait(v)


def kernel(**inputs):
    per_core = _host_prep(inputs)
    nc = build_program()
    res = run_bass_kernel_spmd(nc, per_core, core_ids=list(range(len(per_core))))
    out = np.stack([np.ascontiguousarray(np.asarray(r["out_t"]).T)
                    for r in res.results])
    return out.astype(np.float32)


# version 04
# speedup vs baseline: 1.0917x; 1.0917x over previous
import numpy as np
import ml_dtypes
import concourse.bass as bass
import concourse.mybir as mybir
from concourse.bass_utils import run_bass_kernel_spmd
from contextlib import ExitStack

F32 = mybir.dt.float32
BF16 = mybir.dt.bfloat16
AF = mybir.ActivationFunctionType
ALU = mybir.AluOpType
AX = mybir.AxisListType

D = 1024
T = 2048
E = 2048
NST = 16
RK = 64
TC = 512
NCH = T // TC
SEP = 2
BLK = TC + SEP
SCW = NST * BLK
HEADS = 8
DH = 64
WIN = 512
INNER = 512
HID = 2730
HIDP = 2816
QKV_M = 2560

_TAPS = set()
_PHASE_LIMIT = 99
_FFN_LAYERS = 2
_FFN_WINDOWS = 99
_FFN_STOP = ""
_SERIAL_P3 = True


class S:

    def __init__(self, stack, nc, name):
        self.h = stack.enter_context(nc.semaphore(name))
        self.n = 0

    def inc(self, instr, k=1):
        instr.then_inc(self.h, k)
        self.n += k
        return self.n

    def dma(self, instr):
        return self.inc(instr, 16)

    def tick(self, eng, k=1):
        eng.sem_inc(self.h, k)
        self.n += k
        return self.n

    def wait(self, eng, at=None):
        eng.wait_ge(self.h, self.n if at is None else at)


def dmac(c, sem, eng, out, in_, **kw):
    n0 = c.nc.n_instructions()
    ins = eng.dma_start(out=out, in_=in_, **kw)
    d = c.nc.n_instructions() - n0
    return sem.inc(ins, 16 * d)


def apx(base, off, dims):
    return bass.AP(tensor=base.tensor, offset=base.offset + off,
                   ap=[list(d) for d in dims])


def sb_ap(t):
    return t.ap() if hasattr(t, "ap") else t


def _rotary_tables():
    half_idx = np.arange(0, DH, 2, dtype=np.float32)
    inv_freq = 1.0 / (10000.0 ** (half_idx / DH))
    tt = np.arange(T, dtype=np.float32)
    freqs = tt[:, None] * inv_freq[None, :]
    freqs = np.concatenate([freqs, freqs], axis=-1)
    base = (half_idx + 0.4 * DH) / (1.4 * DH)
    power = (tt - T // 2) / float(WIN // 2)
    scale = base[None, :] ** power[:, None]
    scale = np.concatenate([scale, scale], axis=-1)
    cos, sin = np.cos(freqs), np.sin(freqs)
    sign = np.where(np.arange(DH) < DH // 2, -1.0, 1.0).astype(np.float32)
    cs_q = (cos * scale).astype(np.float32)
    ss_q = (sin * scale * sign[None, :]).astype(np.float32)
    cs_k = (cos / scale).astype(np.float32)
    ss_k = (sin / scale * sign[None, :]).astype(np.float32)

    def fm(v):
        r = np.ascontiguousarray(v.T)
        return np.concatenate([r, r], axis=0)
    return fm(cs_q), fm(ss_q), fm(cs_k), fm(ss_k)


def _maskT_tiles():
    out = np.zeros((8, 128, WIN), np.float32)
    q = np.arange(WIN)
    for s in range(4):
        k_idx = (s * 128 + np.arange(128))[:, None]
        out[s] = (q[None, :] <= k_idx).astype(np.float32)
        out[4 + s] = (q[None, :] >= k_idx).astype(np.float32)
    return np.ascontiguousarray(out.reshape(8 * 128, WIN)).astype(ml_dtypes.bfloat16)


def _host_prep(inputs):
    x = inputs["L0_attn_x"]
    com = {}
    for pref in ("fwd", "bwd"):
        g = lambda n: np.ascontiguousarray(inputs[f"L0_attn_{pref}_{n}"])
        for n in ("in_w", "out_w", "xproj_w", "dt_w", "A_log"):
            com[f"{pref}_{n}"] = g(n)
        for n in ("conv_b", "dt_b", "Dp"):
            com[f"{pref}_{n}"] = np.ascontiguousarray(g(n).reshape(16, 128).T)
        com[f"{pref}_conv_w"] = np.ascontiguousarray(
            g("conv_w").reshape(16, 128, 4).transpose(1, 0, 2))
    perm = np.concatenate([np.arange(32) + 32, np.arange(32)])
    for l in range(2):
        wqkv = inputs["L0_ffn_wqkv"][l]
        wq, wk, wv = wqkv[:, 0:512], wqkv[:, 512:1024], wqkv[:, 1024:1536]
        pidx = (np.arange(INNER).reshape(HEADS, DH)[:, perm]).reshape(-1)
        com[f"l{l}_wqkv"] = np.ascontiguousarray(
            np.concatenate([wq, wq[:, pidx], wk, wk[:, pidx], wv], axis=1))
        com[f"l{l}_wo"] = np.ascontiguousarray(inputs["L0_ffn_wo"][l])
        w1 = inputs["L0_ffn_w1"][l]
        w1p = np.zeros((D, 2 * HIDP), np.float32)
        w1p[:, 0:HID] = w1[:, 0:HID]
        w1p[:, HIDP:HIDP + HID] = w1[:, HID:2 * HID]
        com[f"l{l}_w1p"] = w1p
        b1 = inputs["L0_ffn_b1"][l]
        b1p = np.zeros((2 * HIDP,), np.float32)
        b1p[0:HID] = b1[0:HID]
        b1p[HIDP:HIDP + HID] = b1[HID:2 * HID]
        com[f"l{l}_b1p"] = np.ascontiguousarray(b1p.reshape(44, 128).T)
        w2p = np.zeros((HIDP, D), np.float32)
        w2p[0:HID] = inputs["L0_ffn_w2"][l]
        com[f"l{l}_w2p"] = np.ascontiguousarray(w2p)
        for n in ("b2", "ln_g", "ln_b"):
            com[f"l{l}_{n}"] = np.ascontiguousarray(
                inputs[f"L0_ffn_{n}"][l].reshape(8, 128).T)
        for n in ("q_scale", "k_scale"):
            com[f"l{l}_{n}"] = np.ascontiguousarray(inputs[f"L0_ffn_{n}"][l])
    cs_q, ss_q, cs_k, ss_k = _rotary_tables()
    com["c_rot"] = np.ascontiguousarray(np.concatenate([cs_q, ss_q, cs_k, ss_k], axis=0))
    com["c_maskT"] = _maskT_tiles()
    ident = np.zeros((128, 128), np.float32)
    np.fill_diagonal(ident, 1.0)
    com["c_ident"] = ident.astype(ml_dtypes.bfloat16)
    com["c_ones"] = np.ones((128, 1), np.float32).astype(ml_dtypes.bfloat16)
    sel = np.zeros((128, 32), np.float32)
    sel[0:64, 0] = 1.0
    sel[64:128, 1] = 1.0
    com["c_sel2"] = sel.astype(ml_dtypes.bfloat16)
    com["c_sel2t"] = np.ascontiguousarray(sel.T).astype(ml_dtypes.bfloat16)
    scal = np.array([1e-5, -1.0, -0.5, 0.5, 2.0, -2.0, 64.0, 1.0 / 64], np.float32)
    com["c_scal"] = np.ascontiguousarray(np.broadcast_to(scal, (128, 8)))
    per_core = []
    for b in range(x.shape[0]):
        d = dict(com)
        d["x_t"] = np.ascontiguousarray(x[b].T)
        per_core.append(d)
    return per_core


class Ctx:
    pass


class BlkProxy:

    def __init__(self, c, blk):
        self._c = c
        self._blk = blk
        self.serial = False

    def fast(self):
        from contextlib import contextmanager

        @contextmanager
        def _cm():
            old = self.serial
            self.serial = False
            try:
                yield
            finally:
                self.serial = old
        return _cm()

    def _wrap(self, name):
        real = getattr(self._blk, name)
        c = self._c

        def deco(fn):
            if not self.serial:
                return real(fn)
            sems = [c.mm, c.vv, c.sv, c.gg, c.qW, c.qWb, c.qWc, c.qWd,
                    c.qG, c.qS, c.qV, c.qP, c.qY]
            marks = [(sm, sm.n) for sm in sems]

            def wrapped(eng):
                for sm, n in marks:
                    eng.wait_ge(sm.h, n)
                return fn(eng)
            return real(wrapped)
        return deco

    @property
    def sync(self):
        return self._wrap("sync")

    @property
    def vector(self):
        return self._wrap("vector")

    @property
    def scalar(self):
        return self._wrap("scalar")

    @property
    def tensor(self):
        return self._wrap("tensor")

    @property
    def gpsimd(self):
        return self._wrap("gpsimd")


def _tap_defs():
    return {
        "xln": ([D, T], BF16), "xln_rev": ([D, T], BF16),
        "fwd_xi": ([E, T], BF16), "fwd_z": ([E, T], BF16),
        "fwd_bc": ([32, T], F32), "fwd_yg": ([E, T], BF16),
        "bwd_yg": ([E, T], BF16), "x2": ([D, T], F32), "xf": ([D, T], F32),
        "fwd_delta": ([E, T], F32), "fwd_yr": ([E, T], F32),
        "l0_qrot": ([INNER, T], BF16), "l0_krot": ([INNER, T], BF16),
        "l0_ao": ([INNER, T], BF16), "l0_xa": ([D, T], F32),
        "l0_xin": ([D, T], BF16), "l0_xf": ([D, T], F32),
    }


def build_program():
    nc = bass.Bass("TRN2", target_bir_lowering=False)
    stack = ExitStack()
    c = Ctx()
    c.nc, c.stack = nc, stack
    di = {}

    def din(name, shape, dt=F32):
        di[name] = nc.dram_tensor(name, list(shape), dt, kind="ExternalInput").ap()
    din("x_t", [D, T])
    for pref in ("fwd", "bwd"):
        din(f"{pref}_in_w", [D, 2 * E])
        din(f"{pref}_out_w", [E, D])
        din(f"{pref}_xproj_w", [E, RK + 2 * NST])
        din(f"{pref}_dt_w", [RK, E])
        din(f"{pref}_conv_w", [128, 16, 4])
        din(f"{pref}_conv_b", [128, 16])
        din(f"{pref}_dt_b", [128, 16])
        din(f"{pref}_A_log", [E, NST])
        din(f"{pref}_Dp", [128, 16])
    for l in range(2):
        din(f"l{l}_wqkv", [D, QKV_M])
        din(f"l{l}_wo", [INNER, D])
        din(f"l{l}_w1p", [D, 2 * HIDP])
        din(f"l{l}_b1p", [128, 44])
        din(f"l{l}_w2p", [HIDP, D])
        din(f"l{l}_b2", [128, 8])
        din(f"l{l}_ln_g", [128, 8])
        din(f"l{l}_ln_b", [128, 8])
        din(f"l{l}_q_scale", [DH])
        din(f"l{l}_k_scale", [DH])
    din("c_rot", [4 * 128, T])
    din("c_maskT", [8 * 128, WIN], BF16)
    din("c_ident", [128, 128], BF16)
    din("c_ones", [128, 1], BF16)
    din("c_sel2", [128, 32], BF16)
    din("c_sel2t", [32, 128], BF16)
    din("c_scal", [128, 8])
    c.di = di

    c.out_t = nc.dram_tensor("out_t", [D, T], F32, kind="ExternalOutput").ap()
    c.taps = {}
    for tname, (shape, dt) in _tap_defs().items():
        if tname in _TAPS:
            c.taps[tname] = nc.dram_tensor("tap_" + tname, list(shape), dt,
                                           kind="ExternalOutput").ap()

    ds = {}

    def dscr(name, shape, dt):
        ds[name] = nc.dram_tensor("scr_" + name, list(shape), dt, kind="Internal").ap()
    dscr("xln", [D, T], BF16)
    dscr("xln_rev", [D, T], BF16)
    for pref in ("fwd", "bwd"):
        dscr(f"{pref}_bc", [32, T], F32)
        dscr(f"{pref}_yg", [E, T], BF16)
        if f"{pref}_delta" in _TAPS:
            dscr(f"{pref}_delta", [E, T], F32)
        if f"{pref}_yr" in _TAPS:
            dscr(f"{pref}_yr", [E, T], F32)
        if f"{pref}_xi" in _TAPS:
            dscr(f"{pref}_xi", [E, T], BF16)
        if f"{pref}_z" in _TAPS:
            dscr(f"{pref}_z", [E, T], BF16)
    dscr("x2", [D, T], F32)
    dscr("xf", [D, T], F32)
    for tn in ("l0_qrot", "l0_krot", "l0_ao", "l0_xa", "l0_xin", "l0_xf"):
        if tn in _TAPS:
            dscr(tn, list(_tap_defs()[tn][0]), _tap_defs()[tn][1])
    dscr("row", [64, T], F32)
    dscr("rmax", [HEADS * NCH * 4 * 128], F32)
    dscr("sumexp", [HEADS * NCH * WIN], F32)
    for pref in ("fwd", "bwd"):
        dscr(f"wb_{pref}_in_w", [D, 2 * E], BF16)
        dscr(f"wb_{pref}_out_w", [E, D], BF16)
    for l in range(2):
        dscr(f"wb_l{l}_w1p", [D, 2 * HIDP], BF16)
        dscr(f"wb_l{l}_w2p", [HIDP, D], BF16)
        dscr(f"wb_l{l}_wqkv", [D, QKV_M], BF16)
        dscr(f"wb_l{l}_wo", [INNER, D], BF16)
    c.ds = ds

    for s in ("qW", "qWb", "qWc", "qWd", "qG", "qS", "qV", "qP", "qY",
              "mm", "sv", "vv", "gg", "ws"):
        setattr(c, s, S(stack, nc, s))

    sb = {}

    def sbuf(name, shape, dt):
        sb[name] = stack.enter_context(nc.sbuf_tensor(name, list(shape), dt))
        return sb[name]
    c.sb, c.sbuf = sb, sbuf
    c.ps = [stack.enter_context(nc.psum_tensor(f"ps{i}", [128, 1024], F32))
            for i in range(4)]

    def psl(i):
        return apx(c.ps[i // 2].ap(), (i % 2) * 512, [[1024, 128], [1, 512]])
    c.psl = psl

    c.block = BlkProxy(c, stack.enter_context(nc.Block()))
    for i, val in enumerate([1e-5, -1.0, -0.5, 0.5, 2.0, -2.0, 64.0, 1.0 / 64]):
        pass

    phase0(c)
    if _PHASE_LIMIT >= 1:
        phase1(c)
    if _PHASE_LIMIT >= 2:
        phase2(c)
    if _PHASE_LIMIT >= 3:
        if _SERIAL_P3 is True:
            c.block.serial = True
        phase3(c)
        c.block.serial = False
    if _PHASE_LIMIT >= 4:
        phase4(c)
    _finish(c)
    stack.close()
    return nc


def _emit_taps(c, names):
    todo = [n for n in names if n in c.taps]
    if not todo:
        return
    blk = c.block

    @blk.gpsimd
    def _(g):
        c.vv.wait(g)
        c.sv.wait(g)
        c.qS.wait(g)
        c.qW.wait(g)
        c.qWb.wait(g)
        c.qWc.wait(g)
        c.qWd.wait(g)
        c.qP.wait(g)
        c.qY.wait(g)
        c.qG.wait(g)
        for n in todo:
            dmac(c, c.qG, g, out=c.taps[n], in_=c.ds[n])


def phase0(c):
    nc, blk, sbuf = c.nc, c.block, c.sbuf

    c.ones_bf = sbuf("ones_bf", [128, 1], BF16)
    c.ones_f = sbuf("ones_f", [128, 1], F32)
    c.ident = sbuf("ident", [128, 128], BF16)
    c.sel2 = sbuf("sel2", [128, 32], BF16)
    c.sel2t = sbuf("sel2t", [32, 128], BF16)
    c.scal = sbuf("scal", [128, 8], F32)
    for i, val in enumerate([1e-5, -1.0, -0.5, 0.5, 2.0, -2.0, 64.0, 1.0 / 64]):
        nc.const_aps.register(F32, val, c.scal.ap()[:, i:i + 1])
    mp = {}
    for pref in ("fwd", "bwd"):
        mp[pref] = dict(
            conv_w=sbuf(f"{pref}_conv_w_sb", [128, 64], F32),
            conv_b=sbuf(f"{pref}_conv_b_sb", [128, 16], F32),
            dt_b=sbuf(f"{pref}_dt_b_sb", [128, 16], F32),
            Dp=sbuf(f"{pref}_Dp_sb", [128, 16], F32),
            A_neg=sbuf(f"{pref}_A_neg_sb", [128, 256], F32),
            xproj=sbuf(f"{pref}_xproj_sb", [128, 16 * 96], BF16),
            dt_w=sbuf(f"{pref}_dtw_sb", [64, E], BF16),
        )
    c.mp = mp

    p0 = ExitStack()
    x_f = p0.enter_context(nc.sbuf_tensor("p0_x", [128, 8 * T], F32))
    xsq = p0.enter_context(nc.sbuf_tensor("p0_xsq", [128, 8 * T], BF16))
    stat = p0.enter_context(nc.sbuf_tensor("p0_stat", [1, 4 * T], F32))
    rep = p0.enter_context(nc.sbuf_tensor("p0_rep", [128, 2 * T], F32))
    xln_sb = p0.enter_context(nc.sbuf_tensor("p0_xln", [128, 8 * T], BF16))

    @blk.sync
    def _(sync):
        dmac(c, c.qW, sync, out=sb_ap(x_f).rearrange("p (c t) -> p c t", c=8),
             in_=c.di["x_t"].rearrange("(c p) t -> p c t", p=128))
        dmac(c, c.qW, sync, out=c.ones_bf[:], in_=c.di["c_ones"])
        dmac(c, c.qW, sync, out=c.ident[:], in_=c.di["c_ident"])
        dmac(c, c.qW, sync, out=c.sel2[:], in_=c.di["c_sel2"])
        dmac(c, c.qW, sync, out=c.sel2t[:], in_=c.di["c_sel2t"])
        dmac(c, c.qW, sync, out=c.scal[:], in_=c.di["c_scal"])
        for pref in ("fwd", "bwd"):
            m = mp[pref]
            dmac(c, c.qW, sync,
                 out=sb_ap(m["conv_w"]).rearrange("p (c k) -> p c k", c=16),
                 in_=c.di[f"{pref}_conv_w"])
            dmac(c, c.qW, sync, out=m["conv_b"][:], in_=c.di[f"{pref}_conv_b"])
            dmac(c, c.qW, sync, out=m["dt_b"][:], in_=c.di[f"{pref}_dt_b"])
            dmac(c, c.qW, sync, out=m["Dp"][:], in_=c.di[f"{pref}_Dp"])
            dmac(c, c.qW, sync,
                 out=sb_ap(m["A_neg"]).rearrange("p (c n) -> p c n", c=16),
                 in_=c.di[f"{pref}_A_log"].rearrange("(c p) n -> p c n", p=128))
    p0_loads = c.qW.n

    @blk.gpsimd
    def _(g):
        for pref in ("fwd", "bwd"):
            m = mp[pref]
            dmac(c, c.qG, g,
                 out=sb_ap(m["xproj"]).rearrange("p (c r) -> p c r", c=16),
                 in_=c.di[f"{pref}_xproj_w"].rearrange("(c p) r -> p c r", p=128))
            dmac(c, c.qG, g, out=m["dt_w"][:], in_=c.di[f"{pref}_dt_w"])
    c.small_bf = c.qG.n

    @blk.vector
    def _(v):
        c.qW.wait(v, p0_loads)
        v.tensor_copy(c.ones_f[:], c.ones_bf[:])
        v.drain()
        c.vv.tick(v)
    onesf_ok = c.vv.n

    @blk.scalar
    def _(s):
        c.qW.wait(s, p0_loads)
        for pref in ("fwd", "bwd"):
            s.activation(mp[pref]["A_neg"][:], mp[pref]["A_neg"][:], AF.Exp,
                         bias=0.0, scale=1.0)
            s.drain()
        s.activation(xsq[:], x_f[:], AF.Square, bias=0.0, scale=1.0)
        s.drain()
        c.sv.tick(s)
    sq_ok = c.sv.n

    @blk.vector
    def _(v):
        c.sv.wait(v, sq_ok)
        for pref in ("fwd", "bwd"):
            v.tensor_scalar_mul(mp[pref]["A_neg"][:], mp[pref]["A_neg"][:], -1.0)
        v.drain()
        c.vv.tick(v)
    c.aneg_ok = c.vv.n

    @blk.tensor
    def _(t):
        c.vv.wait(t, onesf_ok)
        c.sv.wait(t, sq_ok)
        last = None
        for half in range(2):
            for tc in range(2):
                for k in range(8):
                    last = t.matmul(
                        apx(c.ps[half].ap(), tc * 512, [[1024, 1], [1, 512]]),
                        c.ones_f[:],
                        apx(sb_ap(x_f), k * T + half * 1024 + tc * 512,
                            [[8 * T, 128], [1, 512]]),
                        start=(k == 0), stop=(k == 7))
        for half in range(2):
            for tc in range(2):
                for k in range(8):
                    last = t.matmul(
                        apx(c.ps[2 + half].ap(), tc * 512, [[1024, 1], [1, 512]]),
                        c.ones_bf[:],
                        apx(sb_ap(xsq), k * T + half * 1024 + tc * 512,
                            [[8 * T, 128], [1, 512]]),
                        start=(k == 0), stop=(k == 7))
        c.mm.inc(last)
    stats_mm = c.mm.n

    MU, VAR, RSTD, MUR = 0, T, 2 * T, 3 * T

    def srow(off):
        return apx(sb_ap(stat), off, [[4 * T, 1], [1, T]])

    @blk.vector
    def _(v):
        c.mm.wait(v, stats_mm)
        for half in range(2):
            v.tensor_scalar_mul(apx(sb_ap(stat), MU + half * 1024, [[4 * T, 1], [1, 1024]]),
                                apx(c.ps[half].ap(), 0, [[1024, 1], [1, 1024]]), 1.0 / D)
            v.tensor_scalar_mul(apx(sb_ap(stat), VAR + half * 1024, [[4 * T, 1], [1, 1024]]),
                                apx(c.ps[2 + half].ap(), 0, [[1024, 1], [1, 1024]]), 1.0 / D)
        v.drain()
        v.tensor_mul(srow(RSTD), srow(MU), srow(MU))
        v.drain()
        v.tensor_sub(srow(VAR), srow(VAR), srow(RSTD))
        v.drain()
        c.vv.tick(v)
    var_ok = c.vv.n

    @blk.scalar
    def _(s):
        c.vv.wait(s, var_ok)
        s.activation(srow(RSTD), srow(VAR), AF.Ln, bias=1e-5, scale=1.0)
        s.drain()
        s.activation(srow(RSTD), srow(RSTD), AF.Exp, bias=0.0, scale=-0.5)
        s.drain()
        c.sv.tick(s)
    rstd_ok = c.sv.n

    @blk.vector
    def _(v):
        c.sv.wait(v, rstd_ok)
        v.tensor_mul(srow(MUR), srow(MU), srow(RSTD))
        v.drain()
        c.vv.tick(v)
    mur_ok = c.vv.n

    @blk.scalar
    def _(s):
        c.vv.wait(s, mur_ok)
        dmac(c, c.qS, s, out=apx(c.ds["row"], 0, [[T, 2], [1, T]]),
             in_=apx(sb_ap(stat), RSTD, [[4 * T, 1], [T, 2], [1, T]]))
    rows_stored = c.qS.n

    @blk.gpsimd
    def _(g):
        c.qS.wait(g, rows_stored)
        dmac(c, c.qG, g, out=sb_ap(rep).rearrange("p (c t) -> p c t", c=2),
             in_=apx(c.ds["row"], 0, [[0, 128], [T, 2], [1, T]]))
    rep_ok = c.qG.n

    @blk.vector
    def _(v):
        c.qG.wait(v, rep_ok)
        rrep = apx(sb_ap(rep), 0, [[2 * T, 128], [1, T]])
        mrep = apx(sb_ap(rep), T, [[2 * T, 128], [1, T]])
        for k in range(8):
            xk = apx(sb_ap(x_f), k * T, [[8 * T, 128], [1, T]])
            v.tensor_mul(xk, xk, rrep)
        v.drain()
        for k in range(8):
            xk = apx(sb_ap(x_f), k * T, [[8 * T, 128], [1, T]])
            ok = apx(sb_ap(xln_sb), k * T, [[8 * T, 128], [1, T]])
            v.tensor_sub(ok, xk, mrep)
        v.drain()
        for k in range(8):
            src = apx(sb_ap(xln_sb), k * T + (T - 1), [[8 * T, 128], [-1, T]])
            dst = apx(sb_ap(xsq), k * T, [[8 * T, 128], [1, T]])
            v.tensor_copy(dst, src)
        v.drain()
        c.vv.tick(v)
    xln_applied = c.vv.n

    @blk.scalar
    def _(s):
        c.vv.wait(s, xln_applied)
        dmac(c, c.qS, s, out=c.ds["xln"].rearrange("(c p) t -> p c t", p=128),
             in_=sb_ap(xln_sb).rearrange("p (c t) -> p c t", c=8))
        dmac(c, c.qS, s, out=c.ds["xln_rev"].rearrange("(c p) t -> p c t", p=128),
             in_=sb_ap(xsq).rearrange("p (c t) -> p c t", c=8))
    c.xln_ok = c.qS.n

    plan = []
    for pref in ("fwd", "bwd"):
        plan.append((c.di[f"{pref}_in_w"], c.ds[f"wb_{pref}_in_w"], D, 2 * E))
    for pref in ("fwd", "bwd"):
        plan.append((c.di[f"{pref}_out_w"], c.ds[f"wb_{pref}_out_w"], E, D))
    for l in range(2):
        plan.append((c.di[f"l{l}_wqkv"], c.ds[f"wb_l{l}_wqkv"], D, QKV_M))
        plan.append((c.di[f"l{l}_wo"], c.ds[f"wb_l{l}_wo"], INNER, D))
        plan.append((c.di[f"l{l}_w1p"], c.ds[f"wb_l{l}_w1p"], D, 2 * HIDP))
        plan.append((c.di[f"l{l}_w2p"], c.ds[f"wb_l{l}_w2p"], HIDP, D))
    c.precast = {}

    @blk.gpsimd
    def _(g):
        for src, dst, rws, cols in plan:
            nrt = rws // 128
            step = max(1, nrt // 4)
            for r0 in range(0, nrt, step):
                r1 = min(nrt, r0 + step)
                dmac(c, c.qP, g, out=dst[r0 * 128:r1 * 128, :],
                     in_=src[r0 * 128:r1 * 128, :])
            c.qP.wait(g)
            c.precast[dst.tensor.name] = c.gg.tick(g)

    p0.close()
    _emit_taps(c, ["xln", "xln_rev"])


def phase1(c):
    nc, blk, sbuf = c.nc, c.block, c.sbuf
    p1 = ExitStack()
    sb = lambda n, sh, dt: p1.enter_context(nc.sbuf_tensor(n, list(sh), dt))
    b = Ctx()
    b.xln = [sb("m_xln0", [128, 8 * TC], BF16), sb("m_xln1", [128, 8 * TC], BF16)]
    b.xiw = sb("m_xiw", [128, 16 * (TC + 3)], BF16)
    b.xis = sb("m_xis", [128, 16 * TC], BF16)
    b.zs = sb("m_zs", [128, 16 * TC], BF16)
    b.acc = sb("m_acc", [128, 2 * TC], F32)
    b.dt = sb("m_dt", [64, TC], BF16)
    b.bc = sb("m_bc", [64, TC], F32)
    b.brep = sb("m_brep", [128, NST * TC], BF16)
    b.crep = sb("m_crep", [128, NST * TC], BF16)
    b.delta = sb("m_delta", [128, 2 * TC], F32)
    b.sp = sb("m_sp", [128, 2 * TC], F32)
    b.du = sb("m_du", [128, 2 * TC], BF16)
    b.a = sb("m_a", [128, 2 * SCW], BF16)
    b.bt = sb("m_bt", [128, SCW], BF16)
    b.h = b.bt
    b.carry = sb("m_carry", [128, 16 * NST], BF16)
    b.yr = sb("m_yr", [128, TC], F32)
    b.yg = sb("m_yg", [128, 2 * TC], BF16)
    b.wring = sb("m_wring", [128, 32 * 128], BF16)

    @blk.vector
    def _(v):
        for sm in (c.qS, c.qW, c.qG, c.mm, c.sv):
            sm.wait(v)
        v.memset(b.a[:], 0.0)
        v.memset(b.bt[:], 0.0)
        v.drain()
        c.vv.tick(v)
    b.scan_zeroed = c.vv.n

    for pref in ("fwd", "bwd"):
        _mamba_dir(c, b, pref)
    p1.close()
    _emit_taps(c, ["fwd_xi", "fwd_z", "fwd_bc", "fwd_yg", "bwd_yg",
                   "fwd_delta", "fwd_yr"])


def _mamba_dir(c, b, pref):
    nc, blk = c.nc, c.block
    m = c.mp[pref]
    psl = c.psl
    xsrc = c.ds["xln"] if pref == "fwd" else c.ds["xln_rev"]
    wb_in = c.ds[f"wb_{pref}_in_w"]
    yg_dst = c.ds[f"{pref}_yg"]

    marks = [(c.mm, c.mm.n), (c.vv, c.vv.n), (c.sv, c.sv.n),
             (c.qW, c.qW.n), (c.qV, c.qV.n), (c.qS, c.qS.n),
             (c.qG, c.qG.n), (c.qY, c.qY.n)]

    @blk.vector
    def _(v):
        for s, n in marks:
            v.wait_ge(s.h, n)
        v.memset(b.carry[:], 0.0)
        v.memset(b.xiw[:], 0.0)
        v.drain()
        c.vv.tick(v)
    dir_zero = c.vv.n

    @blk.scalar
    def _(s_):
        for s, n in marks:
            s_.wait_ge(s.h, n)
        c.sv.tick(s_)

    @blk.tensor
    def _(t):
        c.vv.wait(t, dir_zero)

    @blk.sync
    def _(sync):
        c.vv.wait(sync, dir_zero)

    @blk.gpsimd
    def _(g):
        c.vv.wait(g, dir_zero)

    xln_ld = [None] * NCH
    stA_mm = {}
    evac_done = {}
    conv_done = [None] * 16
    silu_done = [None] * 16
    xproj_mm = [None] * NCH
    dt_ready = [None] * NCH
    bc_stored = [None] * NCH
    rep_ready = [None] * NCH
    dtp_mm = {}
    sp_done = {}
    delta_done = {}
    exps_done = {}
    scan_done = {}
    btt_done = {}
    ymult_done = {}
    gate_done = {}
    yasm_done = {}
    yg_dma = {}
    yred_done = {}
    ring_use = [None] * 32
    ring_load = [None] * 32

    precast_cnt = c.precast[wb_in.tensor.name]

    for ch in range(NCH):
        @blk.sync
        def _(sync, ch=ch):
            if ch >= 2:
                c.mm.wait(sync, xproj_mm[ch - 2])
            if ch >= 1:
                c.mm.wait(sync, stA_mm[(ch - 1, 28)])
            if ch == 0:
                c.qS.wait(sync, c.xln_ok)
            xln_ld[ch] = dmac(c, c.qW, sync,
                out=sb_ap(b.xln[ch % 2]).rearrange("p (k t) -> p k t", k=8),
                in_=apx(xsrc, ch * TC, [[T, 128], [128 * T, 8], [1, TC]]))

        for wave in range(8):
            @blk.sync
            def _(sync, ch=ch, wave=wave):
                if ch == 0 and wave == 0:
                    c.gg.wait(sync, precast_cnt)
                for mi in range(wave * 4, wave * 4 + 4):
                    for k in range(8):
                        slot = (mi * 8 + k) % 32
                        if ring_use[slot] is not None:
                            c.mm.wait(sync, ring_use[slot])
                        ring_load[slot] = dmac(c, c.qW, sync,
                            out=b.wring[:, slot * 128:(slot + 1) * 128],
                            in_=apx(wb_in, (k * 128) * (2 * E) + mi * 128,
                                    [[2 * E, 128], [1, 128]]))

            @blk.tensor
            def _(t, ch=ch, wave=wave):
                c.qW.wait(t)
                for mi in range(wave * 4, wave * 4 + 4):
                    slot_ps = mi % 4
                    prev = (ch, mi - 4) if mi >= 4 else (ch - 1, mi + 28)
                    if prev in evac_done:
                        eng, cnt = evac_done[prev]
                        (c.vv if eng == "vv" else c.sv).wait(t, cnt)
                    last = None
                    for k in range(8):
                        slot = (mi * 8 + k) % 32
                        last = t.matmul(
                            psl(slot_ps),
                            b.wring[:, slot * 128:(slot + 1) * 128],
                            apx(sb_ap(b.xln[ch % 2]), k * TC, [[8 * TC, 128], [1, TC]]),
                            start=(k == 0), stop=(k == 7))
                        ring_use[slot] = c.mm.n + 1
                    stA_mm[(ch, mi)] = c.mm.inc(last)

            if wave < 4:
                @blk.vector
                def _(v, ch=ch, wave=wave):
                    if wave == 0 and ch == 0:
                        c.vv.wait(v, dir_zero)
                    for mi in range(wave * 4, wave * 4 + 4):
                        c.mm.wait(v, stA_mm[(ch, mi)])
                        if ch > 0:
                            v.tensor_copy(
                                apx(sb_ap(b.xiw), mi * (TC + 3), [[16 * (TC + 3), 128], [1, 3]]),
                                apx(sb_ap(b.xiw), mi * (TC + 3) + TC, [[16 * (TC + 3), 128], [1, 3]]))
                            v.drain()
                        v.tensor_copy(
                            apx(sb_ap(b.xiw), mi * (TC + 3) + 3, [[16 * (TC + 3), 128], [1, TC]]),
                            psl(mi % 4))
                        v.drain()
                        evac_done[(ch, mi)] = ("vv", c.vv.tick(v))
            else:
                @blk.scalar
                def _(s, ch=ch, wave=wave):
                    if (f"{pref}_z" in c.taps or f"{pref}_xi" in c.taps) and ch > 0:
                        c.qG.wait(s)
                    for mi in range(wave * 4, wave * 4 + 4):
                        j = mi - 16
                        c.mm.wait(s, stA_mm[(ch, mi)])
                        if ch > 0:
                            c.vv.wait(s, gate_done[(ch - 1, j)])
                        ins = s.activation(
                            apx(sb_ap(b.zs), j * TC, [[16 * TC, 128], [1, TC]]),
                            psl(mi % 4), AF.Silu, bias=0.0, scale=1.0)
                        s.drain()
                        evac_done[(ch, mi)] = ("sv", c.sv.tick(s))

        for j in range(16):
            @blk.vector
            def _(v, ch=ch, j=j):
                base = j * (TC + 3)
                xw = lambda off: apx(sb_ap(b.xiw), base + off,
                                     [[16 * (TC + 3), 128], [1, TC]])
                accp = apx(sb_ap(b.acc), (j % 2) * TC, [[2 * TC, 128], [1, TC]])
                if silu_done[(j - 2) % 16] is not None:
                    c.sv.wait(v, silu_done[(j - 2) % 16])
                v.tensor_scalar(accp, xw(0), m["conv_w"][:, j * 4:j * 4 + 1],
                                m["conv_b"][:, j:j + 1], ALU.mult, ALU.add)
                v.drain()
                for k in range(1, 4):
                    v.scalar_tensor_tensor(accp, xw(k), m["conv_w"][:, j * 4 + k:j * 4 + k + 1],
                                           accp, ALU.mult, ALU.add)
                    v.drain()
                conv_done[j] = c.vv.tick(v)

            @blk.scalar
            def _(s, ch=ch, j=j):
                c.vv.wait(s, conv_done[j])
                if (f"{pref}_z" in c.taps or f"{pref}_xi" in c.taps) and ch > 0:
                    c.qG.wait(s)
                if ch > 0:
                    c.vv.wait(s, yasm_done[(ch - 1, j)])
                accp = apx(sb_ap(b.acc), (j % 2) * TC, [[2 * TC, 128], [1, TC]])
                ins = s.activation(apx(sb_ap(b.xis), j * TC, [[16 * TC, 128], [1, TC]]),
                                   accp, AF.Silu, bias=0.0, scale=1.0)
                s.drain()
                silu_done[j] = c.sv.tick(s)

        @blk.tensor
        def _(t, ch=ch):
            for j in range(16):
                c.sv.wait(t, silu_done[j])
            if ch > 0:
                c.vv.wait(t, dt_ready[ch - 1])
            last = None
            for k in range(16):
                last = t.matmul(
                    apx(c.ps[2].ap(), 0, [[1024, 96], [1, 512]]),
                    m["xproj"][:, k * 96:(k + 1) * 96],
                    apx(sb_ap(b.xis), k * TC, [[16 * TC, 128], [1, TC]]),
                    start=(k == 0), stop=(k == 15))
            xproj_mm[ch] = c.mm.inc(last)

        @blk.vector
        def _(v, ch=ch):
            c.mm.wait(v, xproj_mm[ch])
            if ch > 0:
                c.qG.wait(v)
            v.tensor_copy(b.dt[:], apx(c.ps[2].ap(), 0, [[1024, 64], [1, TC]]))
            v.tensor_copy(apx(sb_ap(b.bc), 0, [[TC, 32], [1, TC]]),
                          apx(c.ps[2].ap(), 64 * 1024, [[1024, 32], [1, TC]]))
            v.drain()
            dt_ready[ch] = c.vv.tick(v)

        @blk.gpsimd
        def _(g, ch=ch):
            c.vv.wait(g, dt_ready[ch])
            bc_stored[ch] = dmac(c, c.qG, g,
                out=apx(c.ds[f"{pref}_bc"], ch * TC, [[T, 32], [1, TC]]),
                in_=apx(sb_ap(b.bc), 0, [[TC, 32], [1, TC]]))
            c.qG.wait(g, bc_stored[ch])
            if ch > 0:
                c.vv.wait(g, btt_done[(ch - 1, 15)])
            dmac(c, c.qG, g,
                 out=sb_ap(b.brep).rearrange("p (n t) -> p n t", n=NST),
                 in_=apx(c.ds[f"{pref}_bc"], ch * TC, [[0, 128], [T, NST], [1, TC]]))
            if ch > 0:
                c.vv.wait(g, ymult_done[(ch - 1, 15)])
            rep_ready[ch] = dmac(c, c.qG, g,
                out=sb_ap(b.crep).rearrange("p (n t) -> p n t", n=NST),
                in_=apx(c.ds[f"{pref}_bc"], NST * T + ch * TC,
                        [[0, 128], [T, NST], [1, TC]]))

        for j in range(16):
            dslot = 5 if j % 2 == 0 else 6

            @blk.tensor
            def _(t, ch=ch, j=j, dslot=dslot):
                c.vv.wait(t, dt_ready[ch])
                prev = (ch, j - 2) if j >= 2 else ((ch - 1, j + 14) if ch > 0 else None)
                if prev is not None and prev in sp_done:
                    c.sv.wait(t, sp_done[prev])
                ins = t.matmul(psl(dslot), m["dt_w"][:, j * 128:(j + 1) * 128],
                               b.dt[:], start=True, stop=True)
                dtp_mm[(ch, j)] = c.mm.inc(ins)

            @blk.scalar
            def _(s, ch=ch, j=j, dslot=dslot):
                c.mm.wait(s, dtp_mm[(ch, j)])
                prev = (ch, j - 1) if j >= 1 else ((ch - 1, 15) if ch > 0 else None)
                if prev is not None and prev in delta_done:
                    c.vv.wait(s, delta_done[prev])
                sp0 = apx(sb_ap(b.sp), 0, [[2 * TC, 128], [1, TC]])
                sp1 = apx(sb_ap(b.sp), TC, [[2 * TC, 128], [1, TC]])
                dtb = m["dt_b"][:, j:j + 1]
                s.activation(sp0, psl(dslot), AF.Abs, bias=dtb, scale=1.0)
                s.drain()
                s.activation(sp1, sp0, AF.Exp, bias=0.0, scale=-1.0)
                s.drain()
                s.activation(sp0, sp1, AF.Ln, bias=1.0, scale=1.0)
                s.drain()
                s.activation(sp1, psl(dslot), AF.Relu, bias=dtb, scale=1.0)
                s.drain()
                sp_done[(ch, j)] = c.sv.tick(s)

            @blk.vector
            def _(v, ch=ch, j=j):
                c.sv.wait(v, sp_done[(ch, j)])
                prev = (ch, j - 2) if j >= 2 else ((ch - 1, j + 14) if ch > 0 else None)
                if prev is not None and prev in exps_done:
                    c.sv.wait(v, exps_done[prev])
                sp0 = apx(sb_ap(b.sp), 0, [[2 * TC, 128], [1, TC]])
                sp1 = apx(sb_ap(b.sp), TC, [[2 * TC, 128], [1, TC]])
                if f"{pref}_delta" in c.taps:
                    c.qG.wait(v)
                dj = apx(sb_ap(b.delta), (j % 2) * TC, [[2 * TC, 128], [1, TC]])
                v.tensor_add(dj, sp0, sp1)
                v.drain()
                delta_done[(ch, j)] = c.vv.tick(v)

            @blk.scalar
            def _(s, ch=ch, j=j):
                c.vv.wait(s, delta_done[(ch, j)])
                if ch == 0 and j == 0:
                    c.vv.wait(s, b.scan_zeroed)
                    c.vv.wait(s, c.aneg_ok)
                prev = (ch, j - 2) if j >= 2 else ((ch - 1, j + 14) if ch > 0 else None)
                if prev is not None and prev in yred_done:
                    c.vv.wait(s, yred_done[prev])
                dj = apx(sb_ap(b.delta), (j % 2) * TC, [[2 * TC, 128], [1, TC]])
                abase = (j % 2) * SCW
                last = None
                for n in range(NST):
                    last = s.activation(
                        apx(sb_ap(b.a), abase + n * BLK + SEP, [[2 * SCW, 128], [1, TC]]),
                        dj, AF.Exp, bias=0.0,
                        scale=m["A_neg"][:, j * NST + n:j * NST + n + 1])
                s.drain()
                exps_done[(ch, j)] = c.sv.tick(s)

            @blk.vector
            def _(v, ch=ch, j=j):
                dj = apx(sb_ap(b.delta), (j % 2) * TC, [[2 * TC, 128], [1, TC]])
                duj = apx(sb_ap(b.du), (j % 2) * TC, [[2 * TC, 128], [1, TC]])
                xisj = apx(sb_ap(b.xis), j * TC, [[16 * TC, 128], [1, TC]])
                v.tensor_mul(duj, dj, xisj)
                v.drain()
                v.tensor_copy(
                    apx(sb_ap(b.bt), 1, [[SCW, 128], [BLK, NST]]),
                    apx(sb_ap(b.carry), j * NST, [[16 * NST, 128], [1, NST]]))
                v.drain()
                c.qG.wait(v, rep_ready[ch])
                v.tensor_tensor(
                    apx(sb_ap(b.bt), SEP, [[SCW, 128], [BLK, NST], [1, TC]]),
                    apx(sb_ap(b.du), (j % 2) * TC, [[2 * TC, 128], [0, NST], [1, TC]]),
                    sb_ap(b.brep).rearrange("p (n t) -> p n t", n=NST),
                    ALU.mult)
                v.drain()
                btt_done[(ch, j)] = c.vv.tick(v)
                c.sv.wait(v, exps_done[(ch, j)])
                abase = (j % 2) * SCW
                v.tensor_tensor_scan(
                    sb_ap(b.bt)[:, :],
                    apx(sb_ap(b.a), abase, [[2 * SCW, 128], [1, SCW]]),
                    b.bt[:], 0.0, ALU.mult, ALU.add)
                v.drain()
                scan_done[(ch, j)] = c.vv.tick(v)
                v.tensor_copy(
                    apx(sb_ap(b.carry), j * NST, [[16 * NST, 128], [1, NST]]),
                    apx(sb_ap(b.bt), SEP + TC - 1, [[SCW, 128], [BLK, NST]]))
                v.drain()
                abase = (j % 2) * SCW
                v.tensor_tensor(
                    apx(sb_ap(b.a), abase + SEP, [[2 * SCW, 128], [BLK, NST], [1, TC]]),
                    apx(sb_ap(b.bt), SEP, [[SCW, 128], [BLK, NST], [1, TC]]),
                    sb_ap(b.crep).rearrange("p (n t) -> p n t", n=NST),
                    ALU.mult)
                v.drain()
                ymult_done[(ch, j)] = c.vv.tick(v)
                if f"{pref}_yr" in c.taps or f"{pref}_delta" in c.taps:
                    c.qG.wait(v)
                yrj = apx(sb_ap(b.yr), 0, [[TC, 128], [1, TC], [1, 1]])
                v.tensor_reduce(yrj,
                                apx(sb_ap(b.a), abase + SEP, [[2 * SCW, 128], [1, TC], [BLK, NST]]),
                                AX.X, ALU.add)
                v.drain()
                yred_done[(ch, j)] = c.vv.tick(v)
                yrf = apx(sb_ap(b.yr), 0, [[TC, 128], [1, TC]])
                v.scalar_tensor_tensor(yrf, xisj, m["Dp"][:, j:j + 1], yrf,
                                       ALU.mult, ALU.add)
                v.drain()
                yasm_done[(ch, j)] = c.vv.tick(v)
                prev = (ch, j - 2) if j >= 2 else (ch - 1, j + 14)
                if prev in yg_dma:
                    c.qY.wait(v)
                zsj = apx(sb_ap(b.zs), j * TC, [[16 * TC, 128], [1, TC]])
                if pref == "fwd":
                    ygj = apx(sb_ap(b.yg), (j % 2) * TC, [[2 * TC, 128], [1, TC]])
                else:
                    ygj = apx(sb_ap(b.yg), (j % 2) * TC + TC - 1, [[2 * TC, 128], [-1, TC]])
                v.tensor_mul(ygj, yrf, zsj)
                v.drain()
                gate_done[(ch, j)] = c.vv.tick(v)

            @blk.gpsimd
            def _(g, ch=ch, j=j):
                c.vv.wait(g, gate_done[(ch, j)])
                gcol = ch * TC if pref == "fwd" else T - (ch + 1) * TC
                yg_dma[(ch, j)] = dmac(c, c.qY, g,
                    out=apx(yg_dst, (j * 128) * T + gcol, [[T, 128], [1, TC]]),
                    in_=apx(sb_ap(b.yg), (j % 2) * TC, [[2 * TC, 128], [1, TC]]))
                if f"{pref}_delta" in c.taps:
                    dmac(c, c.qG, g,
                         out=apx(c.ds[f"{pref}_delta"], (j * 128) * T + ch * TC,
                                 [[T, 128], [1, TC]]),
                         in_=apx(sb_ap(b.delta), (j % 2) * TC, [[2 * TC, 128], [1, TC]]))
                if f"{pref}_yr" in c.taps:
                    yrf2 = apx(sb_ap(b.yr), 0, [[TC, 128], [1, TC]])
                    dmac(c, c.qG, g,
                         out=apx(c.ds[f"{pref}_yr"], (j * 128) * T + ch * TC,
                                 [[T, 128], [1, TC]]),
                         in_=yrf2)

        if f"{pref}_xi" in c.taps or f"{pref}_z" in c.taps:
            @blk.gpsimd
            def _(g, ch=ch):
                c.sv.wait(g)
                c.vv.wait(g)
                if f"{pref}_xi" in c.taps:
                    dmac(c, c.qG, g,
                         out=apx(c.ds[f"{pref}_xi"], ch * TC,
                                 [[T, 128], [128 * T, 16], [1, TC]]),
                         in_=sb_ap(b.xis).rearrange("p (j t) -> p j t", j=16))
                if f"{pref}_z" in c.taps:
                    dmac(c, c.qG, g,
                         out=apx(c.ds[f"{pref}_z"], ch * TC,
                                 [[T, 128], [128 * T, 16], [1, TC]]),
                         in_=sb_ap(b.zs).rearrange("p (j t) -> p j t", j=16))


def phase2(c):
    nc, blk, sbuf = c.nc, c.block, c.sbuf
    p2 = ExitStack()
    sb = lambda n, sh, dt: p2.enter_context(nc.sbuf_tensor(n, list(sh), dt))
    ygf = sb("p2_ygf", [128, 16 * TC], BF16)
    ygb = sb("p2_ygb", [128, 16 * TC], BF16)
    ao = sb("p2_ao", [128, 8 * TC], F32)
    aosq = sb("p2_aosq", [128, 8 * TC], BF16)
    xt = sb("p2_x", [128, 8 * TC], F32)
    xf0 = sb("p2_xf", [128, 8 * TC], F32)
    stat = sb("p2_stat", [1, 4 * TC], F32)
    rep = sb("p2_rep", [128, 2 * TC], F32)
    wring = sb("p2_wring", [128, 32 * 128], BF16)
    psl = c.psl

    marks = [(c.mm, c.mm.n), (c.vv, c.vv.n), (c.sv, c.sv.n), (c.qWb, c.qWb.n),
             (c.qV, c.qV.n), (c.qS, c.qS.n), (c.qG, c.qG.n), (c.qY, c.qY.n)]

    @blk.vector
    def _(v):
        for sm, n in marks:
            v.wait_ge(sm.h, n)
        c.vv.tick(v)
    p2_zero = c.vv.n

    for eng_name in ("scalar", "sync", "gpsimd", "tensor"):
        @getattr(blk, eng_name)
        def _(e):
            c.vv.wait(e, p2_zero)

    wb_f = c.ds["wb_fwd_out_w"]
    wb_b = c.ds["wb_bwd_out_w"]
    ring_load = [None] * 32
    ring_use = [None] * 32
    ow_mm = {}
    evacV = {}
    evacS = {}
    ygld = [None] * NCH
    s1_mm = [None] * NCH
    apply_done = [None] * NCH
    xf_done = [None] * NCH
    x2_stored = [None] * NCH
    MU, VA, RS, MR = 0, TC, 2 * TC, 3 * TC

    def srow(off, n=TC):
        return apx(sb_ap(stat), off, [[4 * TC, 1], [1, n]])

    for r in range(NCH):
        @blk.sync
        def _(sync, r=r):
            if r == 0:
                c.qY.wait(sync)
                c.gg.wait(sync, c.precast[wb_f.tensor.name])
                c.gg.wait(sync, c.precast[wb_b.tensor.name])
            else:
                c.mm.wait(sync, ow_mm[(r - 1, 7)])
                c.vv.wait(sync, xf_done[r - 1])
            for src, dstt in ((c.ds["fwd_yg"], ygf), (c.ds["bwd_yg"], ygb)):
                dmac(c, c.qV, sync,
                     out=sb_ap(dstt).rearrange("p (k t) -> p k t", k=16),
                     in_=apx(src, r * TC, [[T, 128], [128 * T, 16], [1, TC]]))
            ygld[r] = dmac(c, c.qV, sync,
                           out=sb_ap(xt).rearrange("p (k t) -> p k t", k=8),
                           in_=apx(c.di["x_t"], r * TC, [[T, 128], [128 * T, 8], [1, TC]]))

        for m in range(8):
            @blk.sync
            def _(sync, r=r, m=m):
                for di_, wsrc in enumerate((wb_f, wb_b)):
                    for k in range(16):
                        slot = (di_ * 16 + k) % 32
                        if ring_use[slot] is not None:
                            c.mm.wait(sync, ring_use[slot])
                        ring_load[slot] = dmac(
                            c, c.qWb, sync,
                            out=wring[:, slot * 128:(slot + 1) * 128],
                            in_=apx(wsrc, (k * 128) * D + m * 128,
                                    [[D, 128], [1, 128]]))

            @blk.tensor
            def _(t, r=r, m=m):
                c.qWb.wait(t)
                c.qV.wait(t, ygld[r])
                prev = (r, m - 4) if m >= 4 else (r - 1, m + 4)
                if prev in evacV:
                    c.vv.wait(t, evacV[prev])
                    c.sv.wait(t, evacS[prev])
                last = None
                i = 0
                for di_, ygt in enumerate((ygf, ygb)):
                    for k in range(16):
                        slot = (di_ * 16 + k) % 32
                        last = t.matmul(
                            psl(m % 4),
                            wring[:, slot * 128:(slot + 1) * 128],
                            apx(sb_ap(ygt), k * TC, [[16 * TC, 128], [1, TC]]),
                            start=(i == 0), stop=(i == 31))
                        ring_use[slot] = c.mm.n + 1
                        i += 1
                ow_mm[(r, m)] = c.mm.inc(last)

            @blk.vector
            def _(v, r=r, m=m):
                c.mm.wait(v, ow_mm[(r, m)])
                v.tensor_copy(apx(sb_ap(ao), m * TC, [[8 * TC, 128], [1, TC]]),
                              psl(m % 4))
                v.drain()
                evacV[(r, m)] = c.vv.tick(v)

            @blk.scalar
            def _(s, r=r, m=m):
                c.mm.wait(s, ow_mm[(r, m)])
                s.activation(apx(sb_ap(aosq), m * TC, [[8 * TC, 128], [1, TC]]),
                             psl(m % 4), AF.Square, bias=0.0, scale=1.0)
                s.drain()
                evacS[(r, m)] = c.sv.tick(s)

        @blk.tensor
        def _(t, r=r):
            for m in range(8):
                c.vv.wait(t, evacV[(r, m)])
                c.sv.wait(t, evacS[(r, m)])
            last = None
            for k in range(8):
                last = t.matmul(apx(c.ps[2].ap(), 0, [[1024, 1], [1, TC]]),
                                c.ones_f[:],
                                apx(sb_ap(ao), k * TC, [[8 * TC, 128], [1, TC]]),
                                start=(k == 0), stop=(k == 7))
            for k in range(8):
                last = t.matmul(apx(c.ps[2].ap(), 512, [[1024, 1], [1, TC]]),
                                c.ones_bf[:],
                                apx(sb_ap(aosq), k * TC, [[8 * TC, 128], [1, TC]]),
                                start=(k == 0), stop=(k == 7))
            s1_mm[r] = c.mm.inc(last)

        def emit_ln_rows(r, mm_cnt, which):
            @blk.vector
            def _(v, r=r, mm_cnt=mm_cnt):
                c.mm.wait(v, mm_cnt)
                v.tensor_scalar_mul(srow(MU), apx(c.ps[2].ap(), 0, [[1024, 1], [1, TC]]),
                                    1.0 / D)
                v.tensor_scalar_mul(srow(VA), apx(c.ps[2].ap(), 512, [[1024, 1], [1, TC]]),
                                    1.0 / D)
                v.drain()
                v.tensor_mul(srow(RS), srow(MU), srow(MU))
                v.drain()
                v.tensor_sub(srow(VA), srow(VA), srow(RS))
                v.drain()
                c.vv.tick(v)
            rows1 = c.vv.n

            @blk.scalar
            def _(s, rows1=rows1):
                c.vv.wait(s, rows1)
                s.activation(srow(RS), srow(VA), AF.Ln, bias=1e-5, scale=1.0)
                s.drain()
                s.activation(srow(RS), srow(RS), AF.Exp, bias=0.0, scale=-0.5)
                s.drain()
                c.sv.tick(s)
            rows2 = c.sv.n

            @blk.vector
            def _(v, rows2=rows2):
                c.sv.wait(v, rows2)
                v.tensor_mul(srow(MR), srow(MU), srow(RS))
                v.drain()
                c.vv.tick(v)
            rows3 = c.vv.n

            @blk.scalar
            def _(s, rows3=rows3):
                c.vv.wait(s, rows3)
                c.qG.wait(s)
                dmac(c, c.qS, s, out=apx(c.ds["row"], 0, [[TC, 2], [1, TC]]),
                     in_=apx(sb_ap(stat), RS, [[4 * TC, 1], [TC, 2], [1, TC]]))

            @blk.gpsimd
            def _(g):
                c.qS.wait(g)
                dmac(c, c.qG, g, out=sb_ap(rep).rearrange("p (a t) -> p a t", a=2),
                     in_=apx(c.ds["row"], 0, [[0, 128], [TC, 2], [1, TC]]))
            return c.qG.n

        rep1 = emit_ln_rows(r, s1_mm[r], "ao")

        @blk.vector
        def _(v, r=r, rep1=rep1):
            c.qG.wait(v, rep1)
            rrep = apx(sb_ap(rep), 0, [[2 * TC, 128], [1, TC]])
            mrep = apx(sb_ap(rep), TC, [[2 * TC, 128], [1, TC]])
            for k in range(8):
                aok = apx(sb_ap(ao), k * TC, [[8 * TC, 128], [1, TC]])
                v.tensor_mul(aok, aok, rrep)
            v.drain()
            for k in range(8):
                aok = apx(sb_ap(ao), k * TC, [[8 * TC, 128], [1, TC]])
                xk = apx(sb_ap(xt), k * TC, [[8 * TC, 128], [1, TC]])
                v.tensor_sub(aok, aok, mrep)
            v.drain()
            for k in range(8):
                aok = apx(sb_ap(ao), k * TC, [[8 * TC, 128], [1, TC]])
                xk = apx(sb_ap(xt), k * TC, [[8 * TC, 128], [1, TC]])
                v.tensor_add(xk, xk, aok)
            v.drain()
            apply_done[r] = c.vv.tick(v)

        @blk.scalar
        def _(s, r=r):
            c.vv.wait(s, apply_done[r])
            x2_stored[r] = dmac(
                c, c.qS, s,
                out=apx(c.ds["x2"], r * TC, [[T, 128], [128 * T, 8], [1, TC]]),
                in_=sb_ap(xt).rearrange("p (k t) -> p k t", k=8))
            s.activation(aosq[:], xt[:], AF.Square, bias=0.0, scale=1.0)
            s.drain()
            c.sv.tick(s)
        x2sq = c.sv.n

        @blk.tensor
        def _(t, r=r, x2sq=x2sq):
            c.sv.wait(t, x2sq)
            last = None
            for k in range(8):
                last = t.matmul(apx(c.ps[2].ap(), 0, [[1024, 1], [1, TC]]),
                                c.ones_f[:],
                                apx(sb_ap(xt), k * TC, [[8 * TC, 128], [1, TC]]),
                                start=(k == 0), stop=(k == 7))
            for k in range(8):
                last = t.matmul(apx(c.ps[2].ap(), 512, [[1024, 1], [1, TC]]),
                                c.ones_bf[:],
                                apx(sb_ap(aosq), k * TC, [[8 * TC, 128], [1, TC]]),
                                start=(k == 0), stop=(k == 7))
            c.mm.inc(last)
        rep2 = emit_ln_rows(r, c.mm.n, "x2")

        @blk.vector
        def _(v, r=r, rep2=rep2):
            c.qG.wait(v, rep2)
            rrep = apx(sb_ap(rep), 0, [[2 * TC, 128], [1, TC]])
            mrep = apx(sb_ap(rep), TC, [[2 * TC, 128], [1, TC]])
            for k in range(8):
                xk = apx(sb_ap(xt), k * TC, [[8 * TC, 128], [1, TC]])
                fk = apx(sb_ap(xf0), k * TC, [[8 * TC, 128], [1, TC]])
                v.tensor_sub(fk, xk, mrep)
            v.drain()
            for k in range(8):
                fk = apx(sb_ap(xf0), k * TC, [[8 * TC, 128], [1, TC]])
                v.tensor_mul(fk, fk, rrep)
            v.drain()
            xf_done[r] = c.vv.tick(v)

        @blk.scalar
        def _(s, r=r):
            c.vv.wait(s, xf_done[r])
            dmac(c, c.qS, s,
                 out=apx(c.ds["xf"], r * TC, [[T, 128], [128 * T, 8], [1, TC]]),
                 in_=sb_ap(xf0).rearrange("p (k t) -> p k t", k=8))
            s.drain()
            c.sv.tick(s)

    p2.close()
    _emit_taps(c, ["x2"])


def phase3(c):
    nc, blk, sbuf = c.nc, c.block, c.sbuf
    p3 = ExitStack()
    sb = lambda n, sh, dt: p3.enter_context(nc.sbuf_tensor(n, list(sh), dt))
    f = Ctx()
    f.xfw = sb("f_xfw", [128, 8 * WIN], F32)
    f.xfb = sb("f_xfb", [128, 8 * (WIN + 1)], BF16)
    f.xfb2 = sb("f_xfb2", [128, 8 * (WIN + 1)], BF16)
    f.sq = sb("f_sq", [128, 8 * WIN], BF16)
    f.qro = sb("f_qro", [128, 4 * WIN], BF16)
    f.kro = [sb("f_kro0", [128, 4 * WIN], BF16), sb("f_kro1", [128, 4 * WIN], BF16)]
    f.vT = [sb("f_vT0", [128, 16 * 130], BF16), sb("f_vT1", [128, 16 * 130], BF16)]
    f.tq = sb("f_tq", [128, 2 * WIN], F32)
    f.vs = sb("f_vs", [128, WIN], BF16)
    f.rot = sb("f_rot", [128, 4 * WIN], F32)
    f.maskT = sb("f_maskT", [128, 8 * WIN], BF16)
    f.qa = sb("f_qa", [65, WIN], BF16)
    f.ka = [sb("f_ka0", [65, WIN], BF16), sb("f_ka1", [65, WIN], BF16)]
    f.pT = sb("f_pT", [128, 8 * WIN], BF16)
    f.srep = sb("f_srep", [64, WIN], F32)
    f.serow = sb("f_serow", [1, WIN], F32)
    f.ato = sb("f_ato", [128, 4 * WIN], BF16)
    f.xin = sb("f_xin", [128, 8 * WIN], BF16)
    f.h1 = sb("f_h1", [128, 22 * WIN], BF16)
    f.gg = sb("f_gg", [128, 2 * WIN], BF16)
    f.wring = sb("f_wring", [128, 32 * 128], BF16)
    f.wo = sb("f_wo", [128, 4 * 1024], BF16)
    f.rstdb = sb("f_rstdb", [32, WIN], BF16)
    f.nrm = sb("f_nrm", [32, WIN], F32)
    f.rmneg = sb("f_rmneg", [128, 8], F32)
    f.rows = sb("f_rows", [1, 6 * (WIN + 1)], F32)
    f.rep = sb("f_rep", [128, 2 * WIN], F32)
    f.prevc1 = sb("f_prevc1", [128, 8], BF16)
    f.prevc2 = sb("f_prevc2", [128, 8], BF16)
    f.qs8 = sb("f_qs8", [128, 4], F32)
    f.b1c = sb("f_b1c", [128, 44], F32)
    f.b1cs = sb("f_b1cs", [128, 22], F32)
    f.gb = sb("f_gb", [128, 2 * WIN], F32)
    f.b2c = sb("f_b2c", [128, 8], F32)
    f.lng = sb("f_lng", [128, 8], F32)
    f.lnb = sb("f_lnb", [128, 8], F32)
    psl = c.psl

    marks = [(c.mm, c.mm.n), (c.vv, c.vv.n), (c.sv, c.sv.n), (c.qW, c.qW.n),
             (c.qWb, c.qWb.n), (c.qWc, c.qWc.n), (c.qWd, c.qWd.n),
             (c.qV, c.qV.n), (c.qS, c.qS.n), (c.qG, c.qG.n), (c.qY, c.qY.n)]

    @blk.vector
    def _(v):
        for sm, n in marks:
            v.wait_ge(sm.h, n)
        v.memset(f.ka[0][:], 0.0)
        v.memset(f.ka[1][:], 0.0)
        v.memset(f.vT[0][:], 0.0)
        v.memset(f.vT[1][:], 0.0)
        v.memset(f.rstdb[:], 0.0)
        v.memset(f.nrm[:], 0.0)
        v.drain()
        v.memset(f.ka[0][64:65, :], 1.0)
        v.memset(f.ka[1][64:65, :], 1.0)
        for i in range(2):
            v.memset(apx(sb_ap(f.vT[i]), 64, [[16 * 130, 128], [130, 16], [1, 1]]), 1.0)
            v.memset(apx(sb_ap(f.vT[i]), 129, [[16 * 130, 128], [130, 16], [1, 1]]), 1.0)
        v.drain()
        c.vv.tick(v)
    p3_zero = c.vv.n

    for eng_name in ("scalar", "sync", "gpsimd", "tensor"):
        @getattr(blk, eng_name)
        def _(e):
            c.vv.wait(e, p3_zero)

    @blk.sync
    def _(sync):
        dmac(c, c.qWb, sync, out=f.maskT.ap().rearrange("p (i t) -> p i t", i=8),
             in_=c.di["c_maskT"].rearrange("(i p) t -> p i t", p=128))
        c.qWb.wait(sync)
        c.maskT_ok = c.gg.tick(sync)

    for l in range(min(2, _FFN_LAYERS)):
        _ffn_layer(c, f, l)
    p3.close()
    _emit_taps(c, ["l0_qrot", "l0_krot", "l0_ao", "l0_xa", "l0_xin", "l0_xf", "xf"])


def _ffn_layer(c, f, l):
    nc, blk = c.nc, c.block
    psl = c.psl
    f.qw = c.qWc if l == 0 else c.qWd
    wqkv = c.ds[f"wb_l{l}_wqkv"]
    w1p = c.ds[f"wb_l{l}_w1p"]
    w2p = c.ds[f"wb_l{l}_w2p"]
    wo_src = c.ds[f"wb_l{l}_wo"]

    marks = [(c.mm, c.mm.n), (c.vv, c.vv.n), (c.sv, c.sv.n), (c.qW, f.qw.n),
             (c.qV, c.qV.n), (c.qS, c.qS.n), (c.qG, c.qG.n)]

    @blk.vector
    def _(v):
        for sm, n in marks:
            v.wait_ge(sm.h, n)
        v.memset(f.prevc1[:], 0.0)
        v.memset(f.prevc2[:], 0.0)
        v.memset(f.rows[:], 0.0)
        v.drain()
        c.vv.tick(v)
    l_zero = c.vv.n

    @blk.sync
    def _(sync):
        c.vv.wait(sync, l_zero)
        c.gg.wait(sync, c.precast[wo_src.tensor.name])
        dmac(c, f.qw, sync, out=f.wo.ap().rearrange("p (k m) -> p k m", k=4),
             in_=wo_src.rearrange("(k p) m -> p k m", p=128))
        for col, name in ((0, f"l{l}_q_scale"), (2, f"l{l}_k_scale")):
            src = c.di[name]
            for hb in range(2):
                dmac(c, f.qw, sync, out=f.qs8[hb * 64:hb * 64 + 64, col:col + 1],
                     in_=apx(src, 0, [[1, 64], [1, 1]]))
                dmac(c, f.qw, sync,
                     out=f.qs8[hb * 64:hb * 64 + 32, col + 1:col + 2],
                     in_=apx(src, 32, [[1, 32], [1, 1]]))
                dmac(c, f.qw, sync,
                     out=f.qs8[hb * 64 + 32:hb * 64 + 64, col + 1:col + 2],
                     in_=apx(src, 0, [[1, 32], [1, 1]]))
        dmac(c, f.qw, sync, out=f.b1c[:, 0:44], in_=c.di[f"l{l}_b1p"])
        dmac(c, f.qw, sync, out=f.b2c[:, 0:8], in_=c.di[f"l{l}_b2"])
        dmac(c, f.qw, sync, out=f.lng[:, 0:8], in_=c.di[f"l{l}_ln_g"])
        dmac(c, f.qw, sync, out=f.lnb[:, 0:8], in_=c.di[f"l{l}_ln_b"])
    l_small = f.qw.n

    @blk.vector
    def _(v):
        f.qw.wait(v, l_small)
        v.tensor_scalar_mul(f.qs8[:, 0:2], f.qs8[:, 0:2], 8.0)
        v.drain()
        v.tensor_scalar_mul(f.b1cs[:], f.b1c[:, 22:44], 0.70710678)
        v.drain()
        c.vv.tick(v)
    l_qs = c.vv.n

    for w in range(min(NCH, _FFN_WINDOWS)):
        _ffn_window(c, f, l, w, l_qs)


def _ffn_window(c, f, l, w, l_qs):
    nc, blk = c.nc, c.block
    psl = c.psl
    wqkv = c.ds[f"wb_l{l}_wqkv"]
    w1p = c.ds[f"wb_l{l}_w1p"]
    w2p = c.ds[f"wb_l{l}_w2p"]
    cur = w % 2
    prv = 1 - cur
    st = Ctx()

    @blk.sync
    def _(sync):
        if w > 0:
            c.vv.wait(sync, c._f_xfw_free)
        dmac(c, c.qV, sync,
             out=sb_ap(f.xfw).rearrange("p (k t) -> p k t", k=8),
             in_=apx(c.ds["xf"], w * WIN, [[T, 128], [128 * T, 8], [1, WIN]]))
        dmac(c, c.qV, sync,
             out=sb_ap(f.rot).rearrange("p (a t) -> p a t", a=4),
             in_=apx(c.di["c_rot"], w * WIN, [[T, 128], [128 * T, 4], [1, WIN]]))
    st.xf_ld = c.qV.n
    if _FFN_STOP == "loads":
        c._f_xfw_free = c.vv.n
        return

    @blk.vector
    def _(v):
        c.qV.wait(v, st.xf_ld)
        c.vv.wait(v, l_qs)
        for k in range(8):
            v.tensor_copy(
                apx(sb_ap(f.xfb), k * (WIN + 1), [[8 * (WIN + 1), 128], [1, 1]]),
                apx(sb_ap(f.prevc1), k, [[8, 128], [1, 1]]))
        v.drain()
        for k in range(8):
            v.tensor_copy(
                apx(sb_ap(f.xfb), k * (WIN + 1) + 1, [[8 * (WIN + 1), 128], [1, WIN]]),
                apx(sb_ap(f.xfw), k * WIN, [[8 * WIN, 128], [1, WIN]]))
        v.drain()
        for k in range(8):
            v.tensor_copy(
                apx(sb_ap(f.prevc1), k, [[8, 128], [1, 1]]),
                apx(sb_ap(f.xfw), k * WIN + WIN - 1, [[8 * WIN, 128], [1, 1]]))
        v.drain()
        st.xfb_ok = c.vv.tick(v)
    if _FFN_STOP == "xfb":
        c._f_xfw_free = c.vv.n
        return

    st.l_qs = l_qs
    if _SERIAL_P3 == "qkv":
        c.block.serial = True
    _qkv_stage(c, f, l, w, st)
    if _SERIAL_P3 == "qkv":
        c.block.serial = False
    if _FFN_STOP == "":
        if _SERIAL_P3 == "attn":
            c.block.serial = True
        _attn_stage(c, f, l, w, st)
        if _SERIAL_P3 == "attn":
            c.block.serial = False
        if _FFN_STOP != "attn":
            _wo_stage(c, f, l, w, st)
            if _FFN_STOP != "wo":
                if _SERIAL_P3 == "inner":
                    c.block.serial = True
                _inner_stage(c, f, l, w, st)
                if _SERIAL_P3 == "inner":
                    c.block.serial = False
            else:
                st.wo_done = c.vv.n
        else:
            st.wo_done = c.vv.n
    else:
        st.wo_done = c.vv.n

    if l == 0 and any(t in c.taps for t in ("l0_xa", "l0_xin", "l0_xf")):
        @blk.gpsimd
        def _(g):
            c.vv.wait(g)
            c.sv.wait(g)
            if "l0_xin" in c.taps:
                dmac(c, c.qG, g,
                     out=apx(c.ds["l0_xin"], w * WIN, [[T, 128], [128 * T, 8], [1, WIN]]),
                     in_=sb_ap(f.xin).rearrange("p (k t) -> p k t", k=8))
            if "l0_xf" in c.taps:
                dmac(c, c.qG, g,
                     out=apx(c.ds["l0_xf"], w * WIN, [[T, 128], [128 * T, 8], [1, WIN]]),
                     in_=sb_ap(f.xfw).rearrange("p (k t) -> p k t", k=8))

    @blk.scalar
    def _(s):
        c.vv.wait(s)
        st.xf_st = dmac(
            c, c.qS, s,
            out=apx(c.ds["xf"], w * WIN, [[T, 128], [128 * T, 8], [1, WIN]]),
            in_=sb_ap(f.xfw).rearrange("p (k t) -> p k t", k=8))
        s.drain()
        c.sv.tick(s)
    @blk.vector
    def _(v):
        c.qS.wait(v)
        c.qG.wait(v)
        c.vv.tick(v)
    c._f_xfw_free = c.vv.n


def _qkv_stage(c, f, l, w, st):
    nc, blk = c.nc, c.block
    psl = c.psl
    wqkv = c.ds[f"wb_l{l}_wqkv"]
    cur = w % 2
    morder = [0, 4, 1, 5, 2, 6, 3, 7, 8, 12, 9, 13, 10, 14, 11, 15,
              16, 17, 18, 19]
    ring_load = [None] * 32
    ring_use = [None] * 32
    mmc = {}
    freed = {}

    def rhs_ap(k):
        off = k * (WIN + 1) + (1 if k < 4 else 0)
        return apx(sb_ap(f.xfb), off, [[8 * (WIN + 1), 128], [1, WIN]])

    for wave in range(5):
        wm = morder[wave * 4:wave * 4 + 4]

        @blk.sync
        def _(sync, wm=wm, wave=wave):
            if wave == 0:
                c.mm.wait(sync)
                c.vv.wait(sync, st.l_qs)
                if w == 0:
                    c.gg.wait(sync, c.precast[wqkv.tensor.name])
            for i, m in enumerate(wm):
                for k in range(8):
                    slot = (i * 8 + k) % 32
                    if ring_use[slot] is not None:
                        c.mm.wait(sync, ring_use[slot])
                    ring_load[slot] = dmac(
                        c, f.qw, sync,
                        out=f.wring[:, slot * 128:(slot + 1) * 128],
                        in_=apx(wqkv, (k * 128) * QKV_M + m * 128,
                                [[QKV_M, 128], [1, 128]]))

        @blk.tensor
        def _(t, wm=wm, wave=wave):
            f.qw.wait(t)
            if wave == 0:
                c.vv.wait(t, st.xfb_ok)
            for i, m in enumerate(wm):
                sl = (wave * 4 + i) % 4
                for eng, cnt in freed.get(sl, []):
                    (c.vv if eng == "v" else c.sv).wait(t, cnt)
                last = None
                for k in range(8):
                    slot = (i * 8 + k) % 32
                    last = t.matmul(psl(sl), f.wring[:, slot * 128:(slot + 1) * 128],
                                    rhs_ap(k), start=(k == 0), stop=(k == 7))
                    ring_use[slot] = c.mm.n + 1
                mmc[m] = c.mm.inc(last)

        if _FFN_STOP == "qkvmm":
            continue
        if wave < 4:
            is_k = wm[0] >= 8
            for pi in range(2):
                j = wm[pi * 2] if not is_k else wm[pi * 2] - 8
                mq, mp = wm[pi * 2], wm[pi * 2 + 1]
                _norm_rotary_pair(c, f, l, w, st, j, is_k, mq, mp, mmc, freed, psl)
        else:
            if _FFN_STOP == "pairs":
                continue
            for i, m in enumerate(wm):
                _v_transpose(c, f, l, w, st, m - 16, m, mmc, freed, psl)

    st.qkv_done = (c.vv.n, c.sv.n)
    if l == 0 and ("l0_qrot" in c.taps or "l0_krot" in c.taps):
        @blk.gpsimd
        def _(g):
            c.vv.wait(g)
            c.sv.wait(g)
            if "l0_qrot" in c.taps:
                dmac(c, c.qG, g,
                     out=apx(c.ds["l0_qrot"], w * WIN, [[T, 128], [128 * T, 4], [1, WIN]]),
                     in_=sb_ap(f.qro).rearrange("p (j t) -> p j t", j=4))
            if "l0_krot" in c.taps:
                dmac(c, c.qG, g,
                     out=apx(c.ds["l0_krot"], w * WIN, [[T, 128], [128 * T, 4], [1, WIN]]),
                     in_=sb_ap(f.kro[cur]).rearrange("p (j t) -> p j t", j=4))


def _norm_rotary_pair(c, f, l, w, st, j, is_k, mq, mp, mmc, freed, psl):
    nc, blk = c.nc, c.block
    cur = w % 2
    slq = None
    morder = [0, 4, 1, 5, 2, 6, 3, 7, 8, 12, 9, 13, 10, 14, 11, 15, 16, 17, 18, 19]
    slq = morder.index(mq) % 4
    slp = morder.index(mp) % 4
    tq0 = apx(sb_ap(f.tq), 0, [[2 * WIN, 128], [1, WIN]])
    tq1 = apx(sb_ap(f.tq), WIN, [[2 * WIN, 128], [1, WIN]])
    sqt = apx(sb_ap(f.sq), 0, [[8 * WIN, 128], [1, WIN]])
    rows2 = apx(c.ps[2].ap(), 512, [[1024, 32], [1, WIN]])
    repp = apx(c.ps[3].ap(), 0, [[1024, 128], [1, WIN]])

    @blk.vector
    def _(v):
        c.mm.wait(v, mmc[mq])
        c.mm.wait(v, mmc[mp])
        v.tensor_copy(tq0, psl(slq))
        v.tensor_copy(tq1, psl(slp))
        v.drain()
        tick = c.vv.tick(v)
        freed[slp] = [("v", tick)]
        freed[slq] = [("v", tick)]

    @blk.scalar
    def _(s):
        c.mm.wait(s, mmc[mq])
        s.activation(sqt, psl(slq), AF.Square, bias=0.0, scale=1.0)
        s.drain()
        freed[slq].append(("s", c.sv.tick(s)))
    sq_ok = c.sv.n
    if _FFN_STOP == "nr1":
        return

    @blk.tensor
    def _(t):
        c.sv.wait(t, sq_ok)
        if getattr(f, "last_nrm", None) is not None:
            c.vv.wait(t, f.last_nrm)
        if getattr(f, "last_rep_used", None) is not None:
            c.vv.wait(t, f.last_rep_used)
        ins = t.matmul(rows2, c.sel2[:], sqt, start=True, stop=True)
        c.mm.inc(ins)
    ss_mm = c.mm.n

    @blk.vector
    def _(v):
        c.mm.wait(v, ss_mm)
        v.tensor_scalar_max(f.nrm[0:2, :], apx(c.ps[2].ap(), 512, [[1024, 2], [1, WIN]]),
                            1e-24)
        v.drain()
        c.vv.tick(v)
    nrm_ok0 = c.vv.n
    f.last_nrm = nrm_ok0

    @blk.scalar
    def _(s):
        c.vv.wait(s, nrm_ok0)
        s.activation(f.nrm[0:2, :], f.nrm[0:2, :], AF.Ln, bias=0.0, scale=1.0)
        s.drain()
        s.activation(f.nrm[0:2, :], f.nrm[0:2, :], AF.Exp, bias=0.0, scale=-0.5)
        s.drain()
        c.sv.tick(s)
    nrm_ok = c.sv.n
    if _FFN_STOP == "nr2":
        return

    @blk.vector
    def _(v):
        c.sv.wait(v, nrm_ok)
        v.tensor_copy(f.rstdb[0:2, :], f.nrm[0:2, :])
        v.drain()
        c.vv.tick(v)
    rstd_ok = c.vv.n
    if _FFN_STOP == "nr3":
        return

    @blk.tensor
    def _(t):
        c.vv.wait(t, rstd_ok)
        ins = t.matmul(repp, c.sel2t[:], f.rstdb[:], start=True, stop=True)
        c.mm.inc(ins)
    rep_mm = c.mm.n
    if _FFN_STOP == "nr4":
        @blk.vector
        def _(v):
            c.mm.wait(v, rep_mm)
            f.last_rep_used = c.vv.tick(v)
        return

    scol = 2 if is_k else 0
    rota = 2 if is_k else 0

    @blk.vector
    def _(v):
        c.mm.wait(v, rep_mm)
        c.qV.wait(v, st.xf_ld)
        v.scalar_tensor_tensor(tq0, tq0, f.qs8[:, scol:scol + 1], repp,
                               ALU.mult, ALU.mult)
        v.drain()
        v.scalar_tensor_tensor(tq1, tq1, f.qs8[:, scol + 1:scol + 2], repp,
                               ALU.mult, ALU.mult)
        v.drain()
        f.last_rep_used = c.vv.tick(v)
        cs = apx(sb_ap(f.rot), rota * WIN, [[4 * WIN, 128], [1, WIN]])
        ss = apx(sb_ap(f.rot), (rota + 1) * WIN, [[4 * WIN, 128], [1, WIN]])
        v.tensor_mul(tq0, tq0, cs)
        v.tensor_mul(tq1, tq1, ss)
        v.drain()
        dst = f.kro[cur] if is_k else f.qro
        v.tensor_add(apx(sb_ap(dst), j * WIN, [[4 * WIN, 128], [1, WIN]]),
                     tq0, tq1)
        v.drain()
        c.vv.tick(v)


def _v_transpose(c, f, l, w, st, j, m, mmc, freed, psl):
    nc, blk = c.nc, c.block
    cur = w % 2
    morder = [0, 4, 1, 5, 2, 6, 3, 7, 8, 12, 9, 13, 10, 14, 11, 15, 16, 17, 18, 19]
    sl = morder.index(m) % 4
    ptr128 = apx(c.ps[3].ap().bitcast(BF16), 1024, [[2048, 128], [1, 128]])

    @blk.vector
    def _(v):
        c.mm.wait(v, mmc[m])
        v.tensor_copy(f.vs[:], psl(sl))
        v.drain()
        freed[sl] = [("v", c.vv.tick(v))]
    vs_ok = c.vv.n

    for tsub in range(4):
        @blk.tensor
        def _(t, tsub=tsub):
            c.vv.wait(t, vs_ok)
            if tsub > 0:
                c.vv.wait(t)
            ins = t.transpose(ptr128, f.vs[:, tsub * 128:(tsub + 1) * 128],
                              c.ident[:])
            c.mm.inc(ins)
        tmm = c.mm.n

        @blk.vector
        def _(v, tsub=tsub):
            c.mm.wait(v, tmm)
            tile = (tsub * 4 + j) * 130
            v.tensor_copy(
                apx(sb_ap(f.vT[cur]), tile, [[16 * 130, 128], [65, 2], [1, 64]]),
                apx(ptr128, 0, [[2048, 128], [64, 2], [1, 64]]))
            v.drain()
            c.vv.tick(v)


def _attn_stage(c, f, l, w, st):
    nc, blk = c.nc, c.block
    psl = c.psl
    cur = w % 2
    prv = 1 - cur
    halves = ([0, 1] if w > 0 else [1])
    rmax = c.ds["rmax"]
    sume = c.ds["sumexp"]
    pf = {}
    st.ato_ticks = []

    for h in range(HEADS):
        jt = h // 2
        hoff = (h % 2) * 64
        poff = hoff * 4 * WIN

        def qro_lhs(qs_):
            return apx(sb_ap(f.qro), poff + jt * WIN + qs_ * 128,
                       [[4 * WIN, 64], [1, 128]])

        def kro_rhs(hb):
            buf = f.kro[cur] if hb == 1 else f.kro[prv]
            return apx(sb_ap(buf), poff + jt * WIN, [[4 * WIN, 64], [1, WIN]])

        for qs_ in range(4):
            @blk.tensor
            def _(t, h=h, qs_=qs_):
                if h == 0 and qs_ == 0:
                    c.vv.wait(t)
                    c.sv.wait(t)
                for sl in (0, 1):
                    if sl in pf:
                        eng, cnt = pf[sl]
                        (c.vv if eng == "v" else c.sv).wait(t, cnt)
                last = None
                for hi, hb in enumerate(halves):
                    last = t.matmul(psl(hi if len(halves) == 2 else 1),
                                    qro_lhs(qs_), kro_rhs(hb),
                                    start=True, stop=True)
                c.mm.inc(last)
            mm1 = c.mm.n

            @blk.vector
            def _(v, h=h, qs_=qs_, mm1=mm1):
                c.mm.wait(v, mm1)
                if qs_ == 0:
                    c.qS.wait(v)
                if len(halves) == 2:
                    v.reduce_max(f.rmneg[:, 6:7], psl(0), AX.X)
                    v.reduce_max(f.rmneg[:, 7:8], psl(1), AX.X)
                    v.drain()
                    v.tensor_max(f.rmneg[:, qs_:qs_ + 1], f.rmneg[:, 6:7],
                                 f.rmneg[:, 7:8])
                else:
                    v.reduce_max(f.rmneg[:, qs_:qs_ + 1], psl(1), AX.X)
                v.drain()
                v.tensor_scalar_mul(f.rmneg[:, qs_:qs_ + 1],
                                    f.rmneg[:, qs_:qs_ + 1], -1.0)
                v.drain()
                tick = c.vv.tick(v)
                pf[0] = ("v", tick)
                pf[1] = ("v", tick)

        @blk.scalar
        def _(s, h=h):
            c.vv.wait(s)
            for qs_ in range(4):
                dmac(c, c.qS, s,
                     out=apx(rmax, ((w * 8 + h) * 4 + qs_) * 128,
                             [[1, 128], [1, 1]]),
                     in_=f.rmneg[:, qs_:qs_ + 1])
        rm_st = c.qS.n

        @blk.vector
        def _(v, h=h):
            if getattr(f, "prev_simT", None) is not None:
                c.mm.wait(v, f.prev_simT)
            v.tensor_copy(f.qa[0:64, :],
                          apx(sb_ap(f.qro), poff + jt * WIN, [[4 * WIN, 64], [1, WIN]]))
            for hb in halves:
                buf = f.kro[cur] if hb == 1 else f.kro[prv]
                v.tensor_copy(f.ka[hb][0:64, :],
                              apx(sb_ap(buf), poff + jt * WIN, [[4 * WIN, 64], [1, WIN]]))
            v.drain()
            c.vv.tick(v)
        qa_rows = c.vv.n

        @blk.gpsimd
        def _(g, h=h):
            c.qS.wait(g, rm_st)
            c.vv.wait(g, qa_rows)
            if getattr(f, "prev_simT", None) is not None:
                c.mm.wait(g, f.prev_simT)
            dmac(c, c.qG, g, out=f.qa[64:65, :],
                 in_=apx(rmax, (w * 8 + h) * 512, [[512, 1], [1, 512]]))
        qa_ok = c.qG.n

        mask_ticks = []
        for hb in halves:
            for ks_ in range(4):
                sl = 4 + ((ks_ + hb * 4) % 2)
                @blk.tensor
                def _(t, h=h, hb=hb, ks_=ks_, sl=sl):
                    c.qG.wait(t, qa_ok)
                    if sl in pf:
                        eng, cnt = pf[sl]
                        (c.vv if eng == "v" else c.sv).wait(t, cnt)
                    ins = t.matmul(psl(sl), f.ka[hb][:, ks_ * 128:(ks_ + 1) * 128],
                                   f.qa[:], start=True, stop=True)
                    c.mm.inc(ins)
                mmT = c.mm.n

                @blk.scalar
                def _(s, hb=hb, ks_=ks_, sl=sl, mmT=mmT):
                    c.mm.wait(s, mmT)
                    if getattr(f, "prev_av", None) is not None:
                        c.mm.wait(s, f.prev_av)
                    idx = hb * 4 + ks_
                    s.activation(
                        apx(sb_ap(f.pT), idx * WIN, [[8 * WIN, 128], [1, WIN]]),
                        psl(sl), AF.Exp, bias=0.0, scale=1.0)
                    s.drain()
                    pf[sl] = ("s", c.sv.tick(s))
                exp_ok = c.sv.n

                @blk.vector
                def _(v, hb=hb, ks_=ks_, exp_ok=exp_ok, h=h):
                    c.sv.wait(v, exp_ok)
                    if h == 0 and not mask_ticks:
                        c.gg.wait(v, c.maskT_ok)
                    idx = hb * 4 + ks_
                    pt = apx(sb_ap(f.pT), idx * WIN, [[8 * WIN, 128], [1, WIN]])
                    mk = apx(sb_ap(f.maskT), idx * WIN, [[8 * WIN, 128], [1, WIN]])
                    v.tensor_mul(pt, pt, mk)
                    v.drain()
                    mask_ticks.append(c.vv.tick(v))

        f.prev_simT = c.mm.n

        @blk.tensor
        def _(t, h=h):
            c.vv.wait(t, mask_ticks[-1])
            if 2 in pf:
                eng, cnt = pf[2]
                (c.vv if eng == "v" else c.sv).wait(t, cnt)
            last = None
            first = True
            for hb in halves:
                buf = f.vT[cur] if hb == 1 else f.vT[prv]
                for ks_ in range(4):
                    idx = hb * 4 + ks_
                    lhsT = apx(sb_ap(buf), (ks_ * 4 + jt) * 130 + (h % 2) * 65,
                               [[16 * 130, 128], [1, 65]])
                    rhs = apx(sb_ap(f.pT), idx * WIN, [[8 * WIN, 128], [1, WIN]])
                    last = t.matmul(apx(c.ps[1].ap(), 0, [[1024, 65], [1, WIN]]),
                                    lhsT, rhs, start=first,
                                    stop=(hb == halves[-1] and ks_ == 3))
                    first = False
            c.mm.inc(last)
        av_mm = c.mm.n
        f.prev_av = av_mm

        @blk.vector
        def _(v, h=h, av_mm=av_mm):
            c.mm.wait(v, av_mm)
            v.tensor_copy(f.serow[:], apx(c.ps[1].ap(), 64 * 1024, [[1024, 1], [1, WIN]]))
            v.drain()
            c.vv.tick(v)
        se_cp = c.vv.n

        @blk.scalar
        def _(s, h=h, se_cp=se_cp):
            c.vv.wait(s, se_cp)
            dmac(c, c.qS, s,
                 out=apx(sume, (w * 8 + h) * 512, [[512, 1], [1, 512]]),
                 in_=f.serow[:])
        se_st = c.qS.n

        @blk.gpsimd
        def _(g, h=h, se_st=se_st):
            c.qS.wait(g, se_st)
            if getattr(f, "srep_used", None) is not None:
                c.vv.wait(g, f.srep_used)
            dmac(c, c.qG, g, out=f.srep[:],
                 in_=apx(sume, (w * 8 + h) * 512, [[0, 64], [1, 512]]))
        srep_ok = c.qG.n

        @blk.vector
        def _(v, h=h, srep_ok=srep_ok):
            c.qG.wait(v, srep_ok)
            v.reciprocal(f.srep[:], f.srep[:])
            v.drain()
            out = apx(sb_ap(f.ato), poff + jt * WIN, [[4 * WIN, 64], [1, WIN]])
            v.tensor_mul(out, apx(c.ps[1].ap(), 0, [[1024, 64], [1, WIN]]),
                         f.srep[:])
            v.drain()
            tick = c.vv.tick(v)
            f.srep_used = tick
            pf[2] = ("v", tick)
            st.ato_ticks.append(tick)


def _wo_stage(c, f, l, w, st):
    nc, blk = c.nc, c.block
    psl = c.psl
    pf = {}
    evt = {}

    for m in range(8):
        @blk.tensor
        def _(t, m=m):
            if m == 0:
                c.vv.wait(t, st.ato_ticks[-1])
            if m - 2 in evt:
                c.vv.wait(t, evt[m - 2])
            last = None
            for k in range(4):
                last = t.matmul(
                    psl(m % 2),
                    f.wo[:, k * 1024 + m * 128:k * 1024 + m * 128 + 128],
                    apx(sb_ap(f.ato), k * WIN, [[4 * WIN, 128], [1, WIN]]),
                    start=(k == 0), stop=(k == 3))
            c.mm.inc(last)
        mmw = c.mm.n

        @blk.vector
        def _(v, m=m, mmw=mmw):
            c.mm.wait(v, mmw)
            xk = apx(sb_ap(f.xfw), m * WIN, [[8 * WIN, 128], [1, WIN]])
            v.tensor_add(xk, xk, psl(m % 2))
            v.drain()
            evt[m] = c.vv.tick(v)

    st.wo_done = evt[7]
    if l == 0 and "l0_xa" in c.taps:
        @blk.gpsimd
        def _(g):
            c.vv.wait(g, st.wo_done)
            dmac(c, c.qG, g,
                 out=apx(c.ds["l0_xa"], w * WIN, [[T, 128], [128 * T, 8], [1, WIN]]),
                 in_=sb_ap(f.xfw).rearrange("p (k t) -> p k t", k=8))
        st.xa_tap = c.qG.n


def _inner_stage(c, f, l, w, st):
    nc, blk = c.nc, c.block
    psl = c.psl
    R_S1H, R_S2H = 4 * (WIN + 1), 5 * (WIN + 1)
    MU, VA, RS, MR = 0, WIN + 1, 2 * (WIN + 1), 3 * (WIN + 1)

    def srow(off, n=WIN):
        return apx(sb_ap(f.rows), off, [[6 * (WIN + 1), 1], [1, n]])

    @blk.vector
    def _(v):
        c.vv.wait(v, st.wo_done)
        if getattr(st, "xa_tap", None) is not None:
            c.qG.wait(v)
        for k in range(8):
            v.tensor_copy(
                apx(sb_ap(f.xfb2), k * (WIN + 1), [[8 * (WIN + 1), 128], [1, 1]]),
                apx(sb_ap(f.prevc2), k, [[8, 128], [1, 1]]))
        v.drain()
        for k in range(8):
            v.tensor_copy(
                apx(sb_ap(f.xfb2), k * (WIN + 1) + 1, [[8 * (WIN + 1), 128], [1, WIN]]),
                apx(sb_ap(f.xfw), k * WIN, [[8 * WIN, 128], [1, WIN]]))
        v.drain()
        for k in range(8):
            v.tensor_copy(
                apx(sb_ap(f.prevc2), k, [[8, 128], [1, 1]]),
                apx(sb_ap(f.xfw), k * WIN + WIN - 1, [[8 * WIN, 128], [1, 1]]))
        v.drain()
        c.vv.tick(v)
    xfb2_ok = c.vv.n
    if _FFN_STOP == "i0a":
        return

    @blk.scalar
    def _(s):
        c.vv.wait(s, xfb2_ok)
        s.activation(f.sq[:], f.xfw[:], AF.Square, bias=0.0, scale=1.0)
        s.drain()
        c.sv.tick(s)
    sq_ok = c.sv.n

    @blk.tensor
    def _(t):
        c.sv.wait(t, sq_ok)
        c.vv.wait(t)
        last = None
        for half in range(2):
            for k in range(4):
                kk = half * 4 + k
                last = t.matmul(
                    apx(c.ps[2].ap(), half * 512, [[1024, 1], [1, WIN]]),
                    c.ones_f[:],
                    apx(sb_ap(f.xfw), kk * WIN, [[8 * WIN, 128], [1, WIN]]),
                    start=(k == 0), stop=(k == 3))
        for half in range(2):
            for k in range(4):
                kk = half * 4 + k
                last = t.matmul(
                    apx(c.ps[3].ap(), half * 512, [[1024, 1], [1, WIN]]),
                    c.ones_bf[:],
                    apx(sb_ap(f.sq), kk * WIN, [[8 * WIN, 128], [1, WIN]]),
                    start=(k == 0), stop=(k == 3))
        c.mm.inc(last)
    st_mm = c.mm.n

    @blk.vector
    def _(v):
        c.mm.wait(v, st_mm)
        v.tensor_copy(srow(R_S1H, 1), srow(R_S1H + WIN, 1))
        v.tensor_copy(srow(R_S2H, 1), srow(R_S2H + WIN, 1))
        v.drain()
        v.tensor_copy(srow(R_S1H + 1), apx(c.ps[2].ap(), 512, [[1024, 1], [1, WIN]]))
        v.tensor_copy(srow(R_S2H + 1), apx(c.ps[3].ap(), 512, [[1024, 1], [1, WIN]]))
        v.drain()
        v.tensor_add(srow(MU), apx(c.ps[2].ap(), 0, [[1024, 1], [1, WIN]]),
                     srow(R_S1H))
        v.tensor_add(srow(VA), apx(c.ps[3].ap(), 0, [[1024, 1], [1, WIN]]),
                     srow(R_S2H))
        v.drain()
        v.tensor_scalar_mul(srow(MU), srow(MU), 1.0 / D)
        v.tensor_scalar_mul(srow(VA), srow(VA), 1.0 / D)
        v.drain()
        v.tensor_mul(srow(RS), srow(MU), srow(MU))
        v.drain()
        v.tensor_sub(srow(VA), srow(VA), srow(RS))
        v.drain()
        c.vv.tick(v)
    rows1 = c.vv.n
    if _FFN_STOP == "i0c":
        return

    @blk.scalar
    def _(s):
        c.vv.wait(s, rows1)
        s.activation(srow(RS), srow(VA), AF.Ln, bias=1e-5, scale=1.0)
        s.drain()
        s.activation(srow(RS), srow(RS), AF.Exp, bias=0.0, scale=-0.5)
        s.drain()
        c.sv.tick(s)
    rows2_ok = c.sv.n

    @blk.vector
    def _(v):
        c.sv.wait(v, rows2_ok)
        v.tensor_mul(srow(MR), srow(MU), srow(RS))
        v.drain()
        c.vv.tick(v)
    rows3 = c.vv.n

    @blk.scalar
    def _(s):
        c.vv.wait(s, rows3)
        c.qG.wait(s)
        dmac(c, c.qS, s, out=apx(c.ds["row"], 0, [[WIN + 1, 2], [1, WIN]]),
             in_=apx(sb_ap(f.rows), RS, [[6 * (WIN + 1), 1], [WIN + 1, 2], [1, WIN]]))

    @blk.gpsimd
    def _(g):
        c.qS.wait(g)
        dmac(c, c.qG, g, out=sb_ap(f.rep).rearrange("p (a t) -> p a t", a=2),
             in_=apx(c.ds["row"], 0, [[0, 128], [WIN + 1, 2], [1, WIN]]))
    rep_ok = c.qG.n
    if _FFN_STOP == "i0d":
        return

    @blk.vector
    def _(v):
        c.qG.wait(v, rep_ok)
        rrep = apx(sb_ap(f.rep), 0, [[2 * WIN, 128], [1, WIN]])
        mrep = apx(sb_ap(f.rep), WIN, [[2 * WIN, 128], [1, WIN]])
        for k in range(8):
            off = k * (WIN + 1) + (1 if k < 4 else 0)
            xsh = apx(sb_ap(f.xfb2), off, [[8 * (WIN + 1), 128], [1, WIN]])
            tq0 = apx(sb_ap(f.tq), 0, [[2 * WIN, 128], [1, WIN]])
            xk = apx(sb_ap(f.xin), k * WIN, [[8 * WIN, 128], [1, WIN]])
            v.tensor_sub(tq0, xsh, mrep)
            v.drain()
            v.tensor_mul(tq0, tq0, rrep)
            v.drain()
            v.tensor_scalar(xk, tq0, f.lng[:, k:k + 1], f.lnb[:, k:k + 1],
                            ALU.mult, ALU.add)
            v.drain()
        c.vv.tick(v)
    xin_ok = c.vv.n
    if _FFN_STOP == "i1":
        return

    w1p = c.ds[f"wb_l{l}_w1p"]
    ring_load = [None] * 32
    ring_use = [None] * 32
    pf = {}
    gg_tick = {}
    h1_tick = {}
    NM = HIDP // 128
    morder = []
    for j in range(NM):
        morder += [NM + j, j]
    _fast = c.block.fast() if _SERIAL_P3 == True else None
    if _fast is not None:
        _fast.__enter__()
        @blk.vector
        def _(v):
            for sm in (c.mm, c.sv, c.gg, c.qG, c.qS, c.qV, c.qW, c.qWb,
                       c.qY, f.qw):
                sm.wait(v)
            v.wait_ge(c.vv.h, c.vv.n)
            c.vv.tick(v)
        xin_ok2 = c.vv.n
    else:
        xin_ok2 = xin_ok
    for wave in range(11):
        wm = morder[wave * 4:wave * 4 + 4]

        @blk.sync
        def _(sync, wm=wm, wave=wave):
            if wave == 0:
                c.mm.wait(sync)
                if w == 0:
                    c.gg.wait(sync, c.precast[w1p.tensor.name])
            for i, m in enumerate(wm):
                for k in range(8):
                    slot = (i * 8 + k) % 32
                    if ring_use[slot] is not None:
                        c.mm.wait(sync, ring_use[slot])
                    ring_load[slot] = dmac(
                        c, f.qw, sync,
                        out=f.wring[:, slot * 128:(slot + 1) * 128],
                        in_=apx(w1p, (k * 128) * (2 * HIDP) + m * 128,
                                [[2 * HIDP, 128], [1, 128]]))

        mmcs = {}

        @blk.tensor
        def _(t, wm=wm, wave=wave):
            f.qw.wait(t)
            if wave == 0:
                c.vv.wait(t, xin_ok)
            for i, m in enumerate(wm):
                sl = (wave * 4 + i) % 4
                if sl in pf:
                    eng, cnt = pf[sl]
                    (c.vv if eng == "v" else c.sv).wait(t, cnt)
                last = None
                for k in range(8):
                    slot = (i * 8 + k) % 32
                    last = t.matmul(psl(sl), f.wring[:, slot * 128:(slot + 1) * 128],
                                    apx(sb_ap(f.xin), k * WIN, [[8 * WIN, 128], [1, WIN]]),
                                    start=(k == 0), stop=(k == 7))
                    ring_use[slot] = c.mm.n + 1
                mmcs[m] = c.mm.inc(last)

        for i, m in enumerate(wm):
            sl = (wave * 4 + i) % 4
            if m >= NM:
                j = m - NM

                @blk.scalar
                def _(sx, j=j, sl=sl, m=m, mmcs=mmcs):
                    c.mm.wait(sx, mmcs[m])
                    if getattr(f, f"gg_used_{j % 2}", None) is not None:
                        c.vv.wait(sx, getattr(f, f"gg_used_{j % 2}"))
                    sx.activation(
                        apx(sb_ap(f.gg), (j % 2) * WIN, [[2 * WIN, 128], [1, WIN]]),
                        psl(sl), AF.Gelu, bias=f.b1c[:, NM + j:NM + j + 1],
                        scale=1.0)
                    sx.drain()
                    pf[sl] = ("s", c.sv.tick(sx))
                gg_tick[j] = c.sv.n
            else:
                j = m

                @blk.vector
                def _(vx, j=j, sl=sl, m=m, mmcs=mmcs):
                    c.mm.wait(vx, mmcs[m])
                    c.sv.wait(vx, gg_tick[j])
                    vx.scalar_tensor_tensor(
                        apx(sb_ap(f.h1), j * WIN, [[22 * WIN, 128], [1, WIN]]),
                        psl(sl), f.b1c[:, j:j + 1],
                        apx(sb_ap(f.gg), (j % 2) * WIN, [[2 * WIN, 128], [1, WIN]]),
                        ALU.add, ALU.mult)
                    vx.drain()
                    tick = c.vv.tick(vx)
                    pf[sl] = ("v", tick)
                    setattr(f, f"gg_used_{j % 2}", tick)
                h1_tick[j] = c.vv.n

    if _FFN_STOP == "i2":
        return
    w2p = c.ds[f"wb_l{l}_w2p"]
    ring_load2 = [None] * 32
    ring_use2 = [None] * 32
    pf2 = {}
    w2mm = {}
    for m in range(8):
        @blk.sync
        def _(sync, m=m):
            if m == 0:
                c.mm.wait(sync)
                if w == 0:
                    c.gg.wait(sync, c.precast[w2p.tensor.name])
            else:
                c.mm.wait(sync, w2mm[m - 1])
            for k in range(22):
                slot = (m * 22 + k) % 32
                if ring_use2[slot] is not None:
                    c.mm.wait(sync, ring_use2[slot])
                ring_load2[slot] = dmac(
                    c, f.qw, sync,
                    out=f.wring[:, slot * 128:(slot + 1) * 128],
                    in_=apx(w2p, (k * 128) * D + m * 128, [[D, 128], [1, 128]]))

        @blk.tensor
        def _(t, m=m):
            f.qw.wait(t)
            if m == 0:
                c.vv.wait(t, h1_tick[NM - 1])
            sl = m % 2
            if sl in pf2:
                c.vv.wait(t, pf2[sl])
            last = None
            for k in range(22):
                slot = (m * 22 + k) % 32
                last = t.matmul(psl(sl), f.wring[:, slot * 128:(slot + 1) * 128],
                                apx(sb_ap(f.h1), k * WIN, [[22 * WIN, 128], [1, WIN]]),
                                start=(k == 0), stop=(k == 21))
                ring_use2[slot] = c.mm.n + 1
            c.mm.inc(last)
        mmw = c.mm.n
        w2mm[m] = mmw

        @blk.vector
        def _(v, m=m, mmw=mmw):
            c.mm.wait(v, mmw)
            if m == 0 and getattr(st, "xa_tap", None) is not None:
                c.qG.wait(v)
            xk = apx(sb_ap(f.xfw), m * WIN, [[8 * WIN, 128], [1, WIN]])
            v.scalar_tensor_tensor(xk, psl(m % 2), f.b2c[:, m:m + 1], xk,
                                   ALU.add, ALU.add)
            v.drain()
            pf2[m % 2] = c.vv.tick(v)
    if _fast is not None:
        _fast.__exit__(None, None, None)
        @blk.vector
        def _(v):
            for sm in (c.mm, c.sv, f.qw, c.qW):
                sm.wait(v)
            c.vv.tick(v)


def phase4(c):
    nc, blk = c.nc, c.block
    p4 = ExitStack()
    sb = lambda n, sh, dt: p4.enter_context(nc.sbuf_tensor(n, list(sh), dt))
    xf = sb("p4_xf", [128, 8 * TC], F32)
    x2 = sb("p4_x2", [128, 8 * TC], F32)
    sq = sb("p4_sq", [128, 8 * TC], BF16)
    stat = sb("p4_stat", [1, 4 * TC], F32)
    rep = sb("p4_rep", [128, 2 * TC], F32)
    psl = c.psl
    MU, VA, RS, MR = 0, TC, 2 * TC, 3 * TC

    def srow(off):
        return apx(sb_ap(stat), off, [[4 * TC, 1], [1, TC]])

    marks = [(c.mm, c.mm.n), (c.vv, c.vv.n), (c.sv, c.sv.n), (c.qW, c.qW.n),
             (c.qWb, c.qWb.n), (c.qWc, c.qWc.n), (c.qWd, c.qWd.n),
             (c.qV, c.qV.n), (c.qS, c.qS.n), (c.qG, c.qG.n)]

    @blk.vector
    def _(v):
        for sm, n in marks:
            v.wait_ge(sm.h, n)
        c.vv.tick(v)
    p4_zero = c.vv.n

    for eng_name in ("scalar", "sync", "gpsimd", "tensor"):
        @getattr(blk, eng_name)
        def _(e):
            c.vv.wait(e, p4_zero)

    done = [None] * NCH
    for r in range(NCH):
        @blk.sync
        def _(sync, r=r):
            if r > 0:
                c.vv.wait(sync, done[r - 1])
            dmac(c, c.qV, sync,
                 out=sb_ap(xf).rearrange("p (k t) -> p k t", k=8),
                 in_=apx(c.ds["xf"], r * TC, [[T, 128], [128 * T, 8], [1, TC]]))
            dmac(c, c.qV, sync,
                 out=sb_ap(x2).rearrange("p (k t) -> p k t", k=8),
                 in_=apx(c.ds["x2"], r * TC, [[T, 128], [128 * T, 8], [1, TC]]))
        ld = c.qV.n

        @blk.scalar
        def _(s, r=r, ld=ld):
            c.qV.wait(s, ld)
            s.activation(sq[:], xf[:], AF.Square, bias=0.0, scale=1.0)
            s.drain()
            c.sv.tick(s)
        sq_ok = c.sv.n

        @blk.tensor
        def _(t, r=r, sq_ok=sq_ok):
            c.sv.wait(t, sq_ok)
            c.vv.wait(t)
            last = None
            for k in range(8):
                last = t.matmul(apx(c.ps[2].ap(), 0, [[1024, 1], [1, TC]]),
                                c.ones_f[:],
                                apx(sb_ap(xf), k * TC, [[8 * TC, 128], [1, TC]]),
                                start=(k == 0), stop=(k == 7))
            for k in range(8):
                last = t.matmul(apx(c.ps[2].ap(), 512, [[1024, 1], [1, TC]]),
                                c.ones_bf[:],
                                apx(sb_ap(sq), k * TC, [[8 * TC, 128], [1, TC]]),
                                start=(k == 0), stop=(k == 7))
            c.mm.inc(last)
        st_mm = c.mm.n

        @blk.vector
        def _(v, r=r, st_mm=st_mm):
            c.mm.wait(v, st_mm)
            v.tensor_scalar_mul(srow(MU), apx(c.ps[2].ap(), 0, [[1024, 1], [1, TC]]),
                                1.0 / D)
            v.tensor_scalar_mul(srow(VA), apx(c.ps[2].ap(), 512, [[1024, 1], [1, TC]]),
                                1.0 / D)
            v.drain()
            v.tensor_mul(srow(RS), srow(MU), srow(MU))
            v.drain()
            v.tensor_sub(srow(VA), srow(VA), srow(RS))
            v.drain()
            c.vv.tick(v)
        rows1 = c.vv.n

        @blk.scalar
        def _(s, rows1=rows1):
            c.vv.wait(s, rows1)
            s.activation(srow(RS), srow(VA), AF.Ln, bias=1e-5, scale=1.0)
            s.drain()
            s.activation(srow(RS), srow(RS), AF.Exp, bias=0.0, scale=-0.5)
            s.drain()
            c.sv.tick(s)
        rows2 = c.sv.n

        @blk.vector
        def _(v, rows2=rows2):
            c.sv.wait(v, rows2)
            v.tensor_mul(srow(MR), srow(MU), srow(RS))
            v.drain()
            c.vv.tick(v)
        rows3 = c.vv.n

        @blk.scalar
        def _(s, rows3=rows3):
            c.vv.wait(s, rows3)
            c.qG.wait(s)
            dmac(c, c.qS, s, out=apx(c.ds["row"], 0, [[TC, 2], [1, TC]]),
                 in_=apx(sb_ap(stat), RS, [[4 * TC, 1], [TC, 2], [1, TC]]))

        @blk.gpsimd
        def _(g):
            c.qS.wait(g)
            dmac(c, c.qG, g, out=sb_ap(rep).rearrange("p (a t) -> p a t", a=2),
                 in_=apx(c.ds["row"], 0, [[0, 128], [TC, 2], [1, TC]]))
        rep_ok = c.qG.n

        @blk.vector
        def _(v, r=r, rep_ok=rep_ok):
            c.qG.wait(v, rep_ok)
            rrep = apx(sb_ap(rep), 0, [[2 * TC, 128], [1, TC]])
            mrep = apx(sb_ap(rep), TC, [[2 * TC, 128], [1, TC]])
            for k in range(8):
                fk = apx(sb_ap(xf), k * TC, [[8 * TC, 128], [1, TC]])
                v.tensor_mul(fk, fk, rrep)
            v.drain()
            for k in range(8):
                fk = apx(sb_ap(xf), k * TC, [[8 * TC, 128], [1, TC]])
                v.tensor_sub(fk, fk, mrep)
            v.drain()
            for k in range(8):
                fk = apx(sb_ap(xf), k * TC, [[8 * TC, 128], [1, TC]])
                xk = apx(sb_ap(x2), k * TC, [[8 * TC, 128], [1, TC]])
                v.tensor_add(fk, fk, xk)
            v.drain()
            c.vv.tick(v)
        app = c.vv.n

        @blk.scalar
        def _(s, r=r, app=app):
            c.vv.wait(s, app)
            dmac(c, c.qS, s,
                 out=apx(c.out_t, r * TC, [[T, 128], [128 * T, 8], [1, TC]]),
                 in_=sb_ap(xf).rearrange("p (k t) -> p k t", k=8))
            s.drain()
            c.sv.tick(s)
        done[r] = c.vv.n

        @blk.vector
        def _(v, r=r):
            c.sv.wait(v)
            c.qS.wait(v)
            c.vv.tick(v)
        done[r] = c.vv.n

    p4.close()


def _finish(c):
    blk = c.block

    @blk.gpsimd
    def _(g):
        for sm in (c.qS, c.qG, c.qV, c.qW, c.qWb, c.qWc, c.qWd, c.qP, c.qY):
            g.wait_ge(sm.h, sm.n)
        c.gg.tick(g)

    @blk.vector
    def _(v):
        c.gg.wait(v)


def kernel(**inputs):
    per_core = _host_prep(inputs)
    nc = build_program()
    res = run_bass_kernel_spmd(nc, per_core, core_ids=list(range(len(per_core))))
    out = np.stack([np.ascontiguousarray(np.asarray(r["out_t"]).T)
                    for r in res.results])
    return out.astype(np.float32)


# version 05 (final)
# speedup vs baseline: 1.3705x; 1.2553x over previous
import numpy as np
import ml_dtypes
import concourse.bass as bass
import concourse.mybir as mybir
from concourse.bass_utils import run_bass_kernel_spmd
from contextlib import ExitStack

F32 = mybir.dt.float32
BF16 = mybir.dt.bfloat16
AF = mybir.ActivationFunctionType
ALU = mybir.AluOpType
AX = mybir.AxisListType

D = 1024
T = 2048
E = 2048
NST = 16
RK = 64
TC = 512
NCH = T // TC
SEP = 2
BLK = TC + SEP
SCW = NST * BLK
HEADS = 8
DH = 64
WIN = 512
INNER = 512
HID = 2730
HIDP = 2816
QKV_M = 2560

_TAPS = set()
_PHASE_LIMIT = 99
_FFN_LAYERS = 2
_FFN_WINDOWS = 99
_FFN_STOP = ""
_SERIAL_P3 = "stages"
_STAGES_FAST = {"qkv", "attn", "woinner"}
_QKV_PAIRS_SERIAL = False
_PAIR_SERIAL_FROM = 99
_QKV_VT_SERIAL = False


class S:

    def __init__(self, stack, nc, name):
        self.h = stack.enter_context(nc.semaphore(name))
        self.n = 0

    def inc(self, instr, k=1):
        instr.then_inc(self.h, k)
        self.n += k
        return self.n

    def dma(self, instr):
        return self.inc(instr, 16)

    def tick(self, eng, k=1):
        eng.sem_inc(self.h, k)
        self.n += k
        return self.n

    def wait(self, eng, at=None):
        eng.wait_ge(self.h, self.n if at is None else at)


def dmac(c, sem, eng, out, in_, **kw):
    n0 = c.nc.n_instructions()
    ins = eng.dma_start(out=out, in_=in_, **kw)
    d = c.nc.n_instructions() - n0
    return sem.inc(ins, 16 * d)


def apx(base, off, dims):
    return bass.AP(tensor=base.tensor, offset=base.offset + off,
                   ap=[list(d) for d in dims])


def sb_ap(t):
    return t.ap() if hasattr(t, "ap") else t


def _rotary_tables():
    half_idx = np.arange(0, DH, 2, dtype=np.float32)
    inv_freq = 1.0 / (10000.0 ** (half_idx / DH))
    tt = np.arange(T, dtype=np.float32)
    freqs = tt[:, None] * inv_freq[None, :]
    freqs = np.concatenate([freqs, freqs], axis=-1)
    base = (half_idx + 0.4 * DH) / (1.4 * DH)
    power = (tt - T // 2) / float(WIN // 2)
    scale = base[None, :] ** power[:, None]
    scale = np.concatenate([scale, scale], axis=-1)
    cos, sin = np.cos(freqs), np.sin(freqs)
    sign = np.where(np.arange(DH) < DH // 2, -1.0, 1.0).astype(np.float32)
    cs_q = (cos * scale).astype(np.float32)
    ss_q = (sin * scale * sign[None, :]).astype(np.float32)
    cs_k = (cos / scale).astype(np.float32)
    ss_k = (sin / scale * sign[None, :]).astype(np.float32)

    def fm(v):
        r = np.ascontiguousarray(v.T)
        return np.concatenate([r, r], axis=0)
    return fm(cs_q), fm(ss_q), fm(cs_k), fm(ss_k)


def _maskT_tiles():
    out = np.zeros((8, 128, WIN), np.float32)
    q = np.arange(WIN)
    for s in range(4):
        k_idx = (s * 128 + np.arange(128))[:, None]
        out[s] = (q[None, :] <= k_idx).astype(np.float32)
        out[4 + s] = (q[None, :] >= k_idx).astype(np.float32)
    return np.ascontiguousarray(out.reshape(8 * 128, WIN)).astype(ml_dtypes.bfloat16)


def _host_prep(inputs):
    x = inputs["L0_attn_x"]
    com = {}
    for pref in ("fwd", "bwd"):
        g = lambda n: np.ascontiguousarray(inputs[f"L0_attn_{pref}_{n}"])
        for n in ("in_w", "out_w", "xproj_w", "dt_w", "A_log"):
            com[f"{pref}_{n}"] = g(n)
        for n in ("conv_b", "dt_b", "Dp"):
            com[f"{pref}_{n}"] = np.ascontiguousarray(g(n).reshape(16, 128).T)
        com[f"{pref}_conv_w"] = np.ascontiguousarray(
            g("conv_w").reshape(16, 128, 4).transpose(1, 0, 2))
    perm = np.concatenate([np.arange(32) + 32, np.arange(32)])
    for l in range(2):
        wqkv = inputs["L0_ffn_wqkv"][l]
        wq, wk, wv = wqkv[:, 0:512], wqkv[:, 512:1024], wqkv[:, 1024:1536]
        pidx = (np.arange(INNER).reshape(HEADS, DH)[:, perm]).reshape(-1)
        com[f"l{l}_wqkv"] = np.ascontiguousarray(
            np.concatenate([wq, wq[:, pidx], wk, wk[:, pidx], wv], axis=1))
        com[f"l{l}_wo"] = np.ascontiguousarray(inputs["L0_ffn_wo"][l])
        w1 = inputs["L0_ffn_w1"][l]
        w1p = np.zeros((D, 2 * HIDP), np.float32)
        w1p[:, 0:HID] = w1[:, 0:HID]
        w1p[:, HIDP:HIDP + HID] = w1[:, HID:2 * HID]
        com[f"l{l}_w1p"] = w1p
        b1 = inputs["L0_ffn_b1"][l]
        b1p = np.zeros((2 * HIDP,), np.float32)
        b1p[0:HID] = b1[0:HID]
        b1p[HIDP:HIDP + HID] = b1[HID:2 * HID]
        com[f"l{l}_b1p"] = np.ascontiguousarray(b1p.reshape(44, 128).T)
        w2p = np.zeros((HIDP, D), np.float32)
        w2p[0:HID] = inputs["L0_ffn_w2"][l]
        com[f"l{l}_w2p"] = np.ascontiguousarray(w2p)
        for n in ("b2", "ln_g", "ln_b"):
            com[f"l{l}_{n}"] = np.ascontiguousarray(
                inputs[f"L0_ffn_{n}"][l].reshape(8, 128).T)
        for n in ("q_scale", "k_scale"):
            com[f"l{l}_{n}"] = np.ascontiguousarray(inputs[f"L0_ffn_{n}"][l])
    cs_q, ss_q, cs_k, ss_k = _rotary_tables()
    com["c_rot"] = np.ascontiguousarray(np.concatenate([cs_q, ss_q, cs_k, ss_k], axis=0))
    com["c_maskT"] = _maskT_tiles()
    ident = np.zeros((128, 128), np.float32)
    np.fill_diagonal(ident, 1.0)
    com["c_ident"] = ident.astype(ml_dtypes.bfloat16)
    com["c_ones"] = np.ones((128, 1), np.float32).astype(ml_dtypes.bfloat16)
    sel = np.zeros((128, 32), np.float32)
    sel[0:64, 0] = 1.0
    sel[64:128, 1] = 1.0
    com["c_sel2"] = sel.astype(ml_dtypes.bfloat16)
    com["c_sel2t"] = np.ascontiguousarray(sel.T).astype(ml_dtypes.bfloat16)
    scal = np.array([1e-5, -1.0, -0.5, 0.5, 2.0, -2.0, 64.0, 1.0 / 64], np.float32)
    com["c_scal"] = np.ascontiguousarray(np.broadcast_to(scal, (128, 8)))
    per_core = []
    for b in range(x.shape[0]):
        d = dict(com)
        d["x_t"] = np.ascontiguousarray(x[b].T)
        per_core.append(d)
    return per_core


class Ctx:
    pass


class BlkProxy:

    def __init__(self, c, blk):
        self._c = c
        self._blk = blk
        self.serial = False

    def fast(self):
        from contextlib import contextmanager

        @contextmanager
        def _cm():
            old = self.serial
            self.serial = False
            try:
                yield
            finally:
                self.serial = old
        return _cm()

    def _wrap(self, name):
        real = getattr(self._blk, name)
        c = self._c

        def deco(fn):
            if not self.serial:
                return real(fn)
            sems = [c.mm, c.vv, c.sv, c.gg, c.qW, c.qWb, c.qWc, c.qWd,
                    c.qG, c.qS, c.qV, c.qP, c.qY]
            marks = [(sm, sm.n) for sm in sems]

            def wrapped(eng):
                for sm, n in marks:
                    eng.wait_ge(sm.h, n)
                return fn(eng)
            return real(wrapped)
        return deco

    @property
    def sync(self):
        return self._wrap("sync")

    @property
    def vector(self):
        return self._wrap("vector")

    @property
    def scalar(self):
        return self._wrap("scalar")

    @property
    def tensor(self):
        return self._wrap("tensor")

    @property
    def gpsimd(self):
        return self._wrap("gpsimd")


def _tap_defs():
    return {
        "xln": ([D, T], BF16), "xln_rev": ([D, T], BF16),
        "fwd_xi": ([E, T], BF16), "fwd_z": ([E, T], BF16),
        "fwd_bc": ([32, T], F32), "fwd_yg": ([E, T], BF16),
        "bwd_yg": ([E, T], BF16), "x2": ([D, T], F32), "xf": ([D, T], F32),
        "fwd_delta": ([E, T], F32), "fwd_yr": ([E, T], F32),
        "l0_qrot": ([INNER, T], BF16), "l0_krot": ([INNER, T], BF16),
        "l0_ao": ([INNER, T], BF16), "l0_xa": ([D, T], F32),
        "l0_xin": ([D, T], BF16), "l0_xf": ([D, T], F32),
    }


def build_program():
    nc = bass.Bass("TRN2", target_bir_lowering=False)
    stack = ExitStack()
    c = Ctx()
    c.nc, c.stack = nc, stack
    di = {}

    def din(name, shape, dt=F32):
        di[name] = nc.dram_tensor(name, list(shape), dt, kind="ExternalInput").ap()
    din("x_t", [D, T])
    for pref in ("fwd", "bwd"):
        din(f"{pref}_in_w", [D, 2 * E])
        din(f"{pref}_out_w", [E, D])
        din(f"{pref}_xproj_w", [E, RK + 2 * NST])
        din(f"{pref}_dt_w", [RK, E])
        din(f"{pref}_conv_w", [128, 16, 4])
        din(f"{pref}_conv_b", [128, 16])
        din(f"{pref}_dt_b", [128, 16])
        din(f"{pref}_A_log", [E, NST])
        din(f"{pref}_Dp", [128, 16])
    for l in range(2):
        din(f"l{l}_wqkv", [D, QKV_M])
        din(f"l{l}_wo", [INNER, D])
        din(f"l{l}_w1p", [D, 2 * HIDP])
        din(f"l{l}_b1p", [128, 44])
        din(f"l{l}_w2p", [HIDP, D])
        din(f"l{l}_b2", [128, 8])
        din(f"l{l}_ln_g", [128, 8])
        din(f"l{l}_ln_b", [128, 8])
        din(f"l{l}_q_scale", [DH])
        din(f"l{l}_k_scale", [DH])
    din("c_rot", [4 * 128, T])
    din("c_maskT", [8 * 128, WIN], BF16)
    din("c_ident", [128, 128], BF16)
    din("c_ones", [128, 1], BF16)
    din("c_sel2", [128, 32], BF16)
    din("c_sel2t", [32, 128], BF16)
    din("c_scal", [128, 8])
    c.di = di

    c.out_t = nc.dram_tensor("out_t", [D, T], F32, kind="ExternalOutput").ap()
    c.taps = {}
    for tname, (shape, dt) in _tap_defs().items():
        if tname in _TAPS:
            c.taps[tname] = nc.dram_tensor("tap_" + tname, list(shape), dt,
                                           kind="ExternalOutput").ap()

    ds = {}

    def dscr(name, shape, dt):
        ds[name] = nc.dram_tensor("scr_" + name, list(shape), dt, kind="Internal").ap()
    dscr("xln", [D, T], BF16)
    dscr("xln_rev", [D, T], BF16)
    for pref in ("fwd", "bwd"):
        dscr(f"{pref}_bc", [32, T], F32)
        dscr(f"{pref}_yg", [E, T], BF16)
        if f"{pref}_delta" in _TAPS:
            dscr(f"{pref}_delta", [E, T], F32)
        if f"{pref}_yr" in _TAPS:
            dscr(f"{pref}_yr", [E, T], F32)
        if f"{pref}_xi" in _TAPS:
            dscr(f"{pref}_xi", [E, T], BF16)
        if f"{pref}_z" in _TAPS:
            dscr(f"{pref}_z", [E, T], BF16)
    dscr("x2", [D, T], F32)
    dscr("xf", [D, T], F32)
    for tn in ("l0_qrot", "l0_krot", "l0_ao", "l0_xa", "l0_xin", "l0_xf"):
        if tn in _TAPS:
            dscr(tn, list(_tap_defs()[tn][0]), _tap_defs()[tn][1])
    dscr("row", [64, T], F32)
    dscr("rmax", [HEADS * NCH * 4 * 128], F32)
    dscr("sumexp", [HEADS * NCH * WIN], F32)
    for pref in ("fwd", "bwd"):
        dscr(f"wb_{pref}_in_w", [D, 2 * E], BF16)
        dscr(f"wb_{pref}_out_w", [E, D], BF16)
    for l in range(2):
        dscr(f"wb_l{l}_w1p", [D, 2 * HIDP], BF16)
        dscr(f"wb_l{l}_w2p", [HIDP, D], BF16)
        dscr(f"wb_l{l}_wqkv", [D, QKV_M], BF16)
        dscr(f"wb_l{l}_wo", [INNER, D], BF16)
    c.ds = ds

    for s in ("qW", "qWb", "qWc", "qWd", "qG", "qS", "qV", "qP", "qY",
              "mm", "sv", "vv", "gg", "ws"):
        setattr(c, s, S(stack, nc, s))

    sb = {}

    def sbuf(name, shape, dt):
        sb[name] = stack.enter_context(nc.sbuf_tensor(name, list(shape), dt))
        return sb[name]
    c.sb, c.sbuf = sb, sbuf
    c.ps = [stack.enter_context(nc.psum_tensor(f"ps{i}", [128, 1024], F32))
            for i in range(4)]

    def psl(i):
        return apx(c.ps[i // 2].ap(), (i % 2) * 512, [[1024, 128], [1, 512]])
    c.psl = psl

    c.block = BlkProxy(c, stack.enter_context(nc.Block()))
    for i, val in enumerate([1e-5, -1.0, -0.5, 0.5, 2.0, -2.0, 64.0, 1.0 / 64]):
        pass

    phase0(c)
    if _PHASE_LIMIT >= 1:
        phase1(c)
    if _PHASE_LIMIT >= 2:
        phase2(c)
    if _PHASE_LIMIT >= 3:
        if _SERIAL_P3 is True or _SERIAL_P3 == "stages":
            c.block.serial = True
        phase3(c)
        c.block.serial = False
    if _PHASE_LIMIT >= 4:
        phase4(c)
    _finish(c)
    stack.close()
    return nc


def _emit_taps(c, names):
    todo = [n for n in names if n in c.taps]
    if not todo:
        return
    blk = c.block

    @blk.gpsimd
    def _(g):
        c.vv.wait(g)
        c.sv.wait(g)
        c.qS.wait(g)
        c.qW.wait(g)
        c.qWb.wait(g)
        c.qWc.wait(g)
        c.qWd.wait(g)
        c.qP.wait(g)
        c.qY.wait(g)
        c.qG.wait(g)
        for n in todo:
            dmac(c, c.qG, g, out=c.taps[n], in_=c.ds[n])


def phase0(c):
    nc, blk, sbuf = c.nc, c.block, c.sbuf

    c.ones_bf = sbuf("ones_bf", [128, 1], BF16)
    c.ones_f = sbuf("ones_f", [128, 1], F32)
    c.ident = sbuf("ident", [128, 128], BF16)
    c.sel2 = sbuf("sel2", [128, 32], BF16)
    c.sel2t = sbuf("sel2t", [32, 128], BF16)
    c.scal = sbuf("scal", [128, 8], F32)
    for i, val in enumerate([1e-5, -1.0, -0.5, 0.5, 2.0, -2.0, 64.0, 1.0 / 64]):
        nc.const_aps.register(F32, val, c.scal.ap()[:, i:i + 1])
    mp = {}
    for pref in ("fwd", "bwd"):
        mp[pref] = dict(
            conv_w=sbuf(f"{pref}_conv_w_sb", [128, 64], F32),
            conv_b=sbuf(f"{pref}_conv_b_sb", [128, 16], F32),
            dt_b=sbuf(f"{pref}_dt_b_sb", [128, 16], F32),
            Dp=sbuf(f"{pref}_Dp_sb", [128, 16], F32),
            A_neg=sbuf(f"{pref}_A_neg_sb", [128, 256], F32),
            xproj=sbuf(f"{pref}_xproj_sb", [128, 16 * 96], BF16),
            dt_w=sbuf(f"{pref}_dtw_sb", [64, E], BF16),
        )
    c.mp = mp

    p0 = ExitStack()
    x_f = p0.enter_context(nc.sbuf_tensor("p0_x", [128, 8 * T], F32))
    xsq = p0.enter_context(nc.sbuf_tensor("p0_xsq", [128, 8 * T], BF16))
    stat = p0.enter_context(nc.sbuf_tensor("p0_stat", [1, 4 * T], F32))
    rep = p0.enter_context(nc.sbuf_tensor("p0_rep", [128, 2 * T], F32))
    xln_sb = p0.enter_context(nc.sbuf_tensor("p0_xln", [128, 8 * T], BF16))

    @blk.sync
    def _(sync):
        dmac(c, c.qW, sync, out=sb_ap(x_f).rearrange("p (c t) -> p c t", c=8),
             in_=c.di["x_t"].rearrange("(c p) t -> p c t", p=128))
        dmac(c, c.qW, sync, out=c.ones_bf[:], in_=c.di["c_ones"])
        dmac(c, c.qW, sync, out=c.ident[:], in_=c.di["c_ident"])
        dmac(c, c.qW, sync, out=c.sel2[:], in_=c.di["c_sel2"])
        dmac(c, c.qW, sync, out=c.sel2t[:], in_=c.di["c_sel2t"])
        dmac(c, c.qW, sync, out=c.scal[:], in_=c.di["c_scal"])
        for pref in ("fwd", "bwd"):
            m = mp[pref]
            dmac(c, c.qW, sync,
                 out=sb_ap(m["conv_w"]).rearrange("p (c k) -> p c k", c=16),
                 in_=c.di[f"{pref}_conv_w"])
            dmac(c, c.qW, sync, out=m["conv_b"][:], in_=c.di[f"{pref}_conv_b"])
            dmac(c, c.qW, sync, out=m["dt_b"][:], in_=c.di[f"{pref}_dt_b"])
            dmac(c, c.qW, sync, out=m["Dp"][:], in_=c.di[f"{pref}_Dp"])
            dmac(c, c.qW, sync,
                 out=sb_ap(m["A_neg"]).rearrange("p (c n) -> p c n", c=16),
                 in_=c.di[f"{pref}_A_log"].rearrange("(c p) n -> p c n", p=128))
    p0_loads = c.qW.n

    @blk.gpsimd
    def _(g):
        for pref in ("fwd", "bwd"):
            m = mp[pref]
            dmac(c, c.qG, g,
                 out=sb_ap(m["xproj"]).rearrange("p (c r) -> p c r", c=16),
                 in_=c.di[f"{pref}_xproj_w"].rearrange("(c p) r -> p c r", p=128))
            dmac(c, c.qG, g, out=m["dt_w"][:], in_=c.di[f"{pref}_dt_w"])
    c.small_bf = c.qG.n

    @blk.vector
    def _(v):
        c.qW.wait(v, p0_loads)
        v.tensor_copy(c.ones_f[:], c.ones_bf[:])
        v.drain()
        c.vv.tick(v)
    onesf_ok = c.vv.n

    @blk.scalar
    def _(s):
        c.qW.wait(s, p0_loads)
        for pref in ("fwd", "bwd"):
            s.activation(mp[pref]["A_neg"][:], mp[pref]["A_neg"][:], AF.Exp,
                         bias=0.0, scale=1.0)
            s.drain()
        s.activation(xsq[:], x_f[:], AF.Square, bias=0.0, scale=1.0)
        s.drain()
        c.sv.tick(s)
    sq_ok = c.sv.n

    @blk.vector
    def _(v):
        c.sv.wait(v, sq_ok)
        for pref in ("fwd", "bwd"):
            v.tensor_scalar_mul(mp[pref]["A_neg"][:], mp[pref]["A_neg"][:], -1.0)
        v.drain()
        c.vv.tick(v)
    c.aneg_ok = c.vv.n

    @blk.tensor
    def _(t):
        c.vv.wait(t, onesf_ok)
        c.sv.wait(t, sq_ok)
        last = None
        for half in range(2):
            for tc in range(2):
                for k in range(8):
                    last = t.matmul(
                        apx(c.ps[half].ap(), tc * 512, [[1024, 1], [1, 512]]),
                        c.ones_f[:],
                        apx(sb_ap(x_f), k * T + half * 1024 + tc * 512,
                            [[8 * T, 128], [1, 512]]),
                        start=(k == 0), stop=(k == 7))
        for half in range(2):
            for tc in range(2):
                for k in range(8):
                    last = t.matmul(
                        apx(c.ps[2 + half].ap(), tc * 512, [[1024, 1], [1, 512]]),
                        c.ones_bf[:],
                        apx(sb_ap(xsq), k * T + half * 1024 + tc * 512,
                            [[8 * T, 128], [1, 512]]),
                        start=(k == 0), stop=(k == 7))
        c.mm.inc(last)
    stats_mm = c.mm.n

    MU, VAR, RSTD, MUR = 0, T, 2 * T, 3 * T

    def srow(off):
        return apx(sb_ap(stat), off, [[4 * T, 1], [1, T]])

    @blk.vector
    def _(v):
        c.mm.wait(v, stats_mm)
        for half in range(2):
            v.tensor_scalar_mul(apx(sb_ap(stat), MU + half * 1024, [[4 * T, 1], [1, 1024]]),
                                apx(c.ps[half].ap(), 0, [[1024, 1], [1, 1024]]), 1.0 / D)
            v.tensor_scalar_mul(apx(sb_ap(stat), VAR + half * 1024, [[4 * T, 1], [1, 1024]]),
                                apx(c.ps[2 + half].ap(), 0, [[1024, 1], [1, 1024]]), 1.0 / D)
        v.drain()
        v.tensor_mul(srow(RSTD), srow(MU), srow(MU))
        v.drain()
        v.tensor_sub(srow(VAR), srow(VAR), srow(RSTD))
        v.drain()
        c.vv.tick(v)
    var_ok = c.vv.n

    @blk.scalar
    def _(s):
        c.vv.wait(s, var_ok)
        s.activation(srow(RSTD), srow(VAR), AF.Ln, bias=1e-5, scale=1.0)
        s.drain()
        s.activation(srow(RSTD), srow(RSTD), AF.Exp, bias=0.0, scale=-0.5)
        s.drain()
        c.sv.tick(s)
    rstd_ok = c.sv.n

    @blk.vector
    def _(v):
        c.sv.wait(v, rstd_ok)
        v.tensor_mul(srow(MUR), srow(MU), srow(RSTD))
        v.drain()
        c.vv.tick(v)
    mur_ok = c.vv.n

    @blk.scalar
    def _(s):
        c.vv.wait(s, mur_ok)
        dmac(c, c.qS, s, out=apx(c.ds["row"], 0, [[T, 2], [1, T]]),
             in_=apx(sb_ap(stat), RSTD, [[4 * T, 1], [T, 2], [1, T]]))
    rows_stored = c.qS.n

    @blk.gpsimd
    def _(g):
        c.qS.wait(g, rows_stored)
        dmac(c, c.qG, g, out=sb_ap(rep).rearrange("p (c t) -> p c t", c=2),
             in_=apx(c.ds["row"], 0, [[0, 128], [T, 2], [1, T]]))
    rep_ok = c.qG.n

    @blk.vector
    def _(v):
        c.qG.wait(v, rep_ok)
        rrep = apx(sb_ap(rep), 0, [[2 * T, 128], [1, T]])
        mrep = apx(sb_ap(rep), T, [[2 * T, 128], [1, T]])
        for k in range(8):
            xk = apx(sb_ap(x_f), k * T, [[8 * T, 128], [1, T]])
            v.tensor_mul(xk, xk, rrep)
        v.drain()
        for k in range(8):
            xk = apx(sb_ap(x_f), k * T, [[8 * T, 128], [1, T]])
            ok = apx(sb_ap(xln_sb), k * T, [[8 * T, 128], [1, T]])
            v.tensor_sub(ok, xk, mrep)
        v.drain()
        for k in range(8):
            src = apx(sb_ap(xln_sb), k * T + (T - 1), [[8 * T, 128], [-1, T]])
            dst = apx(sb_ap(xsq), k * T, [[8 * T, 128], [1, T]])
            v.tensor_copy(dst, src)
        v.drain()
        c.vv.tick(v)
    xln_applied = c.vv.n

    @blk.scalar
    def _(s):
        c.vv.wait(s, xln_applied)
        dmac(c, c.qS, s, out=c.ds["xln"].rearrange("(c p) t -> p c t", p=128),
             in_=sb_ap(xln_sb).rearrange("p (c t) -> p c t", c=8))
        dmac(c, c.qS, s, out=c.ds["xln_rev"].rearrange("(c p) t -> p c t", p=128),
             in_=sb_ap(xsq).rearrange("p (c t) -> p c t", c=8))
    c.xln_ok = c.qS.n

    plan = []
    for pref in ("fwd", "bwd"):
        plan.append((c.di[f"{pref}_in_w"], c.ds[f"wb_{pref}_in_w"], D, 2 * E))
    for pref in ("fwd", "bwd"):
        plan.append((c.di[f"{pref}_out_w"], c.ds[f"wb_{pref}_out_w"], E, D))
    for l in range(2):
        plan.append((c.di[f"l{l}_wqkv"], c.ds[f"wb_l{l}_wqkv"], D, QKV_M))
        plan.append((c.di[f"l{l}_wo"], c.ds[f"wb_l{l}_wo"], INNER, D))
        plan.append((c.di[f"l{l}_w1p"], c.ds[f"wb_l{l}_w1p"], D, 2 * HIDP))
        plan.append((c.di[f"l{l}_w2p"], c.ds[f"wb_l{l}_w2p"], HIDP, D))
    c.precast = {}

    @blk.gpsimd
    def _(g):
        for src, dst, rws, cols in plan:
            nrt = rws // 128
            step = max(1, nrt // 4)
            for r0 in range(0, nrt, step):
                r1 = min(nrt, r0 + step)
                dmac(c, c.qP, g, out=dst[r0 * 128:r1 * 128, :],
                     in_=src[r0 * 128:r1 * 128, :])
            c.qP.wait(g)
            c.precast[dst.tensor.name] = c.gg.tick(g)

    p0.close()
    _emit_taps(c, ["xln", "xln_rev"])


def phase1(c):
    nc, blk, sbuf = c.nc, c.block, c.sbuf
    p1 = ExitStack()
    sb = lambda n, sh, dt: p1.enter_context(nc.sbuf_tensor(n, list(sh), dt))
    b = Ctx()
    b.xln = [sb("m_xln0", [128, 8 * TC], BF16), sb("m_xln1", [128, 8 * TC], BF16)]
    b.xiw = sb("m_xiw", [128, 16 * (TC + 3)], BF16)
    b.xis = sb("m_xis", [128, 16 * TC], BF16)
    b.zs = sb("m_zs", [128, 16 * TC], BF16)
    b.acc = sb("m_acc", [128, 2 * TC], F32)
    b.dt = sb("m_dt", [64, TC], BF16)
    b.bc = sb("m_bc", [64, TC], F32)
    b.brep = sb("m_brep", [128, NST * TC], BF16)
    b.crep = sb("m_crep", [128, NST * TC], BF16)
    b.delta = sb("m_delta", [128, 2 * TC], F32)
    b.sp = sb("m_sp", [128, 2 * TC], F32)
    b.du = sb("m_du", [128, 2 * TC], BF16)
    b.a = sb("m_a", [128, 2 * SCW], BF16)
    b.bt = sb("m_bt", [128, SCW], BF16)
    b.h = b.bt
    b.carry = sb("m_carry", [128, 16 * NST], BF16)
    b.yr = sb("m_yr", [128, TC], F32)
    b.yg = sb("m_yg", [128, 2 * TC], BF16)
    b.wring = sb("m_wring", [128, 32 * 128], BF16)

    @blk.vector
    def _(v):
        for sm in (c.qS, c.qW, c.qG, c.mm, c.sv):
            sm.wait(v)
        v.memset(b.a[:], 0.0)
        v.memset(b.bt[:], 0.0)
        v.drain()
        c.vv.tick(v)
    b.scan_zeroed = c.vv.n

    for pref in ("fwd", "bwd"):
        _mamba_dir(c, b, pref)
    p1.close()
    _emit_taps(c, ["fwd_xi", "fwd_z", "fwd_bc", "fwd_yg", "bwd_yg",
                   "fwd_delta", "fwd_yr"])


def _mamba_dir(c, b, pref):
    nc, blk = c.nc, c.block
    m = c.mp[pref]
    psl = c.psl
    xsrc = c.ds["xln"] if pref == "fwd" else c.ds["xln_rev"]
    wb_in = c.ds[f"wb_{pref}_in_w"]
    yg_dst = c.ds[f"{pref}_yg"]

    marks = [(c.mm, c.mm.n), (c.vv, c.vv.n), (c.sv, c.sv.n),
             (c.qW, c.qW.n), (c.qV, c.qV.n), (c.qS, c.qS.n),
             (c.qG, c.qG.n), (c.qY, c.qY.n)]

    @blk.vector
    def _(v):
        for s, n in marks:
            v.wait_ge(s.h, n)
        v.memset(b.carry[:], 0.0)
        v.memset(b.xiw[:], 0.0)
        v.drain()
        c.vv.tick(v)
    dir_zero = c.vv.n

    @blk.scalar
    def _(s_):
        for s, n in marks:
            s_.wait_ge(s.h, n)
        c.sv.tick(s_)

    @blk.tensor
    def _(t):
        c.vv.wait(t, dir_zero)

    @blk.sync
    def _(sync):
        c.vv.wait(sync, dir_zero)

    @blk.gpsimd
    def _(g):
        c.vv.wait(g, dir_zero)

    xln_ld = [None] * NCH
    stA_mm = {}
    evac_done = {}
    conv_done = [None] * 16
    silu_done = [None] * 16
    xproj_mm = [None] * NCH
    dt_ready = [None] * NCH
    bc_stored = [None] * NCH
    rep_ready = [None] * NCH
    dtp_mm = {}
    sp_done = {}
    delta_done = {}
    exps_done = {}
    scan_done = {}
    btt_done = {}
    ymult_done = {}
    gate_done = {}
    yasm_done = {}
    yg_dma = {}
    yred_done = {}
    ring_use = [None] * 32
    ring_load = [None] * 32

    precast_cnt = c.precast[wb_in.tensor.name]

    for ch in range(NCH):
        @blk.sync
        def _(sync, ch=ch):
            if ch >= 2:
                c.mm.wait(sync, xproj_mm[ch - 2])
            if ch >= 1:
                c.mm.wait(sync, stA_mm[(ch - 1, 28)])
            if ch == 0:
                c.qS.wait(sync, c.xln_ok)
            xln_ld[ch] = dmac(c, c.qW, sync,
                out=sb_ap(b.xln[ch % 2]).rearrange("p (k t) -> p k t", k=8),
                in_=apx(xsrc, ch * TC, [[T, 128], [128 * T, 8], [1, TC]]))

        for wave in range(8):
            @blk.sync
            def _(sync, ch=ch, wave=wave):
                if ch == 0 and wave == 0:
                    c.gg.wait(sync, precast_cnt)
                for mi in range(wave * 4, wave * 4 + 4):
                    for k in range(8):
                        slot = (mi * 8 + k) % 32
                        if ring_use[slot] is not None:
                            c.mm.wait(sync, ring_use[slot])
                        ring_load[slot] = dmac(c, c.qW, sync,
                            out=b.wring[:, slot * 128:(slot + 1) * 128],
                            in_=apx(wb_in, (k * 128) * (2 * E) + mi * 128,
                                    [[2 * E, 128], [1, 128]]))

            @blk.tensor
            def _(t, ch=ch, wave=wave):
                c.qW.wait(t)
                for mi in range(wave * 4, wave * 4 + 4):
                    slot_ps = mi % 4
                    prev = (ch, mi - 4) if mi >= 4 else (ch - 1, mi + 28)
                    if prev in evac_done:
                        eng, cnt = evac_done[prev]
                        (c.vv if eng == "vv" else c.sv).wait(t, cnt)
                    last = None
                    for k in range(8):
                        slot = (mi * 8 + k) % 32
                        last = t.matmul(
                            psl(slot_ps),
                            b.wring[:, slot * 128:(slot + 1) * 128],
                            apx(sb_ap(b.xln[ch % 2]), k * TC, [[8 * TC, 128], [1, TC]]),
                            start=(k == 0), stop=(k == 7))
                        ring_use[slot] = c.mm.n + 1
                    stA_mm[(ch, mi)] = c.mm.inc(last)

            if wave < 4:
                @blk.vector
                def _(v, ch=ch, wave=wave):
                    if wave == 0 and ch == 0:
                        c.vv.wait(v, dir_zero)
                    for mi in range(wave * 4, wave * 4 + 4):
                        c.mm.wait(v, stA_mm[(ch, mi)])
                        if ch > 0:
                            v.tensor_copy(
                                apx(sb_ap(b.xiw), mi * (TC + 3), [[16 * (TC + 3), 128], [1, 3]]),
                                apx(sb_ap(b.xiw), mi * (TC + 3) + TC, [[16 * (TC + 3), 128], [1, 3]]))
                            v.drain()
                        v.tensor_copy(
                            apx(sb_ap(b.xiw), mi * (TC + 3) + 3, [[16 * (TC + 3), 128], [1, TC]]),
                            psl(mi % 4))
                        v.drain()
                        evac_done[(ch, mi)] = ("vv", c.vv.tick(v))
            else:
                @blk.scalar
                def _(s, ch=ch, wave=wave):
                    if (f"{pref}_z" in c.taps or f"{pref}_xi" in c.taps) and ch > 0:
                        c.qG.wait(s)
                    for mi in range(wave * 4, wave * 4 + 4):
                        j = mi - 16
                        c.mm.wait(s, stA_mm[(ch, mi)])
                        if ch > 0:
                            c.vv.wait(s, gate_done[(ch - 1, j)])
                        ins = s.activation(
                            apx(sb_ap(b.zs), j * TC, [[16 * TC, 128], [1, TC]]),
                            psl(mi % 4), AF.Silu, bias=0.0, scale=1.0)
                        s.drain()
                        evac_done[(ch, mi)] = ("sv", c.sv.tick(s))

        for j in range(16):
            @blk.vector
            def _(v, ch=ch, j=j):
                base = j * (TC + 3)
                xw = lambda off: apx(sb_ap(b.xiw), base + off,
                                     [[16 * (TC + 3), 128], [1, TC]])
                accp = apx(sb_ap(b.acc), (j % 2) * TC, [[2 * TC, 128], [1, TC]])
                if silu_done[(j - 2) % 16] is not None:
                    c.sv.wait(v, silu_done[(j - 2) % 16])
                v.tensor_scalar(accp, xw(0), m["conv_w"][:, j * 4:j * 4 + 1],
                                m["conv_b"][:, j:j + 1], ALU.mult, ALU.add)
                v.drain()
                for k in range(1, 4):
                    v.scalar_tensor_tensor(accp, xw(k), m["conv_w"][:, j * 4 + k:j * 4 + k + 1],
                                           accp, ALU.mult, ALU.add)
                    v.drain()
                conv_done[j] = c.vv.tick(v)

            @blk.scalar
            def _(s, ch=ch, j=j):
                c.vv.wait(s, conv_done[j])
                if (f"{pref}_z" in c.taps or f"{pref}_xi" in c.taps) and ch > 0:
                    c.qG.wait(s)
                if ch > 0:
                    c.vv.wait(s, yasm_done[(ch - 1, j)])
                accp = apx(sb_ap(b.acc), (j % 2) * TC, [[2 * TC, 128], [1, TC]])
                ins = s.activation(apx(sb_ap(b.xis), j * TC, [[16 * TC, 128], [1, TC]]),
                                   accp, AF.Silu, bias=0.0, scale=1.0)
                s.drain()
                silu_done[j] = c.sv.tick(s)

        @blk.tensor
        def _(t, ch=ch):
            for j in range(16):
                c.sv.wait(t, silu_done[j])
            if ch > 0:
                c.vv.wait(t, dt_ready[ch - 1])
            last = None
            for k in range(16):
                last = t.matmul(
                    apx(c.ps[2].ap(), 0, [[1024, 96], [1, 512]]),
                    m["xproj"][:, k * 96:(k + 1) * 96],
                    apx(sb_ap(b.xis), k * TC, [[16 * TC, 128], [1, TC]]),
                    start=(k == 0), stop=(k == 15))
            xproj_mm[ch] = c.mm.inc(last)

        @blk.vector
        def _(v, ch=ch):
            c.mm.wait(v, xproj_mm[ch])
            if ch > 0:
                c.qG.wait(v)
            v.tensor_copy(b.dt[:], apx(c.ps[2].ap(), 0, [[1024, 64], [1, TC]]))
            v.tensor_copy(apx(sb_ap(b.bc), 0, [[TC, 32], [1, TC]]),
                          apx(c.ps[2].ap(), 64 * 1024, [[1024, 32], [1, TC]]))
            v.drain()
            dt_ready[ch] = c.vv.tick(v)

        @blk.gpsimd
        def _(g, ch=ch):
            c.vv.wait(g, dt_ready[ch])
            bc_stored[ch] = dmac(c, c.qG, g,
                out=apx(c.ds[f"{pref}_bc"], ch * TC, [[T, 32], [1, TC]]),
                in_=apx(sb_ap(b.bc), 0, [[TC, 32], [1, TC]]))
            c.qG.wait(g, bc_stored[ch])
            if ch > 0:
                c.vv.wait(g, btt_done[(ch - 1, 15)])
            dmac(c, c.qG, g,
                 out=sb_ap(b.brep).rearrange("p (n t) -> p n t", n=NST),
                 in_=apx(c.ds[f"{pref}_bc"], ch * TC, [[0, 128], [T, NST], [1, TC]]))
            if ch > 0:
                c.vv.wait(g, ymult_done[(ch - 1, 15)])
            rep_ready[ch] = dmac(c, c.qG, g,
                out=sb_ap(b.crep).rearrange("p (n t) -> p n t", n=NST),
                in_=apx(c.ds[f"{pref}_bc"], NST * T + ch * TC,
                        [[0, 128], [T, NST], [1, TC]]))

        for j in range(16):
            dslot = 5 if j % 2 == 0 else 6

            @blk.tensor
            def _(t, ch=ch, j=j, dslot=dslot):
                c.vv.wait(t, dt_ready[ch])
                prev = (ch, j - 2) if j >= 2 else ((ch - 1, j + 14) if ch > 0 else None)
                if prev is not None and prev in sp_done:
                    c.sv.wait(t, sp_done[prev])
                ins = t.matmul(psl(dslot), m["dt_w"][:, j * 128:(j + 1) * 128],
                               b.dt[:], start=True, stop=True)
                dtp_mm[(ch, j)] = c.mm.inc(ins)

            @blk.scalar
            def _(s, ch=ch, j=j, dslot=dslot):
                c.mm.wait(s, dtp_mm[(ch, j)])
                prev = (ch, j - 1) if j >= 1 else ((ch - 1, 15) if ch > 0 else None)
                if prev is not None and prev in delta_done:
                    c.vv.wait(s, delta_done[prev])
                sp0 = apx(sb_ap(b.sp), 0, [[2 * TC, 128], [1, TC]])
                sp1 = apx(sb_ap(b.sp), TC, [[2 * TC, 128], [1, TC]])
                dtb = m["dt_b"][:, j:j + 1]
                s.activation(sp0, psl(dslot), AF.Abs, bias=dtb, scale=1.0)
                s.drain()
                s.activation(sp1, sp0, AF.Exp, bias=0.0, scale=-1.0)
                s.drain()
                s.activation(sp0, sp1, AF.Ln, bias=1.0, scale=1.0)
                s.drain()
                s.activation(sp1, psl(dslot), AF.Relu, bias=dtb, scale=1.0)
                s.drain()
                sp_done[(ch, j)] = c.sv.tick(s)

            @blk.vector
            def _(v, ch=ch, j=j):
                c.sv.wait(v, sp_done[(ch, j)])
                prev = (ch, j - 2) if j >= 2 else ((ch - 1, j + 14) if ch > 0 else None)
                if prev is not None and prev in exps_done:
                    c.sv.wait(v, exps_done[prev])
                sp0 = apx(sb_ap(b.sp), 0, [[2 * TC, 128], [1, TC]])
                sp1 = apx(sb_ap(b.sp), TC, [[2 * TC, 128], [1, TC]])
                if f"{pref}_delta" in c.taps:
                    c.qG.wait(v)
                dj = apx(sb_ap(b.delta), (j % 2) * TC, [[2 * TC, 128], [1, TC]])
                v.tensor_add(dj, sp0, sp1)
                v.drain()
                delta_done[(ch, j)] = c.vv.tick(v)

            @blk.scalar
            def _(s, ch=ch, j=j):
                c.vv.wait(s, delta_done[(ch, j)])
                if ch == 0 and j == 0:
                    c.vv.wait(s, b.scan_zeroed)
                    c.vv.wait(s, c.aneg_ok)
                prev = (ch, j - 2) if j >= 2 else ((ch - 1, j + 14) if ch > 0 else None)
                if prev is not None and prev in yred_done:
                    c.vv.wait(s, yred_done[prev])
                dj = apx(sb_ap(b.delta), (j % 2) * TC, [[2 * TC, 128], [1, TC]])
                abase = (j % 2) * SCW
                last = None
                for n in range(NST):
                    last = s.activation(
                        apx(sb_ap(b.a), abase + n * BLK + SEP, [[2 * SCW, 128], [1, TC]]),
                        dj, AF.Exp, bias=0.0,
                        scale=m["A_neg"][:, j * NST + n:j * NST + n + 1])
                s.drain()
                exps_done[(ch, j)] = c.sv.tick(s)

            @blk.vector
            def _(v, ch=ch, j=j):
                dj = apx(sb_ap(b.delta), (j % 2) * TC, [[2 * TC, 128], [1, TC]])
                duj = apx(sb_ap(b.du), (j % 2) * TC, [[2 * TC, 128], [1, TC]])
                xisj = apx(sb_ap(b.xis), j * TC, [[16 * TC, 128], [1, TC]])
                v.tensor_mul(duj, dj, xisj)
                v.drain()
                v.tensor_copy(
                    apx(sb_ap(b.bt), 1, [[SCW, 128], [BLK, NST]]),
                    apx(sb_ap(b.carry), j * NST, [[16 * NST, 128], [1, NST]]))
                v.drain()
                c.qG.wait(v, rep_ready[ch])
                v.tensor_tensor(
                    apx(sb_ap(b.bt), SEP, [[SCW, 128], [BLK, NST], [1, TC]]),
                    apx(sb_ap(b.du), (j % 2) * TC, [[2 * TC, 128], [0, NST], [1, TC]]),
                    sb_ap(b.brep).rearrange("p (n t) -> p n t", n=NST),
                    ALU.mult)
                v.drain()
                btt_done[(ch, j)] = c.vv.tick(v)
                c.sv.wait(v, exps_done[(ch, j)])
                abase = (j % 2) * SCW
                v.tensor_tensor_scan(
                    sb_ap(b.bt)[:, :],
                    apx(sb_ap(b.a), abase, [[2 * SCW, 128], [1, SCW]]),
                    b.bt[:], 0.0, ALU.mult, ALU.add)
                v.drain()
                scan_done[(ch, j)] = c.vv.tick(v)
                v.tensor_copy(
                    apx(sb_ap(b.carry), j * NST, [[16 * NST, 128], [1, NST]]),
                    apx(sb_ap(b.bt), SEP + TC - 1, [[SCW, 128], [BLK, NST]]))
                v.drain()
                abase = (j % 2) * SCW
                v.tensor_tensor(
                    apx(sb_ap(b.a), abase + SEP, [[2 * SCW, 128], [BLK, NST], [1, TC]]),
                    apx(sb_ap(b.bt), SEP, [[SCW, 128], [BLK, NST], [1, TC]]),
                    sb_ap(b.crep).rearrange("p (n t) -> p n t", n=NST),
                    ALU.mult)
                v.drain()
                ymult_done[(ch, j)] = c.vv.tick(v)
                if f"{pref}_yr" in c.taps or f"{pref}_delta" in c.taps:
                    c.qG.wait(v)
                yrj = apx(sb_ap(b.yr), 0, [[TC, 128], [1, TC], [1, 1]])
                v.tensor_reduce(yrj,
                                apx(sb_ap(b.a), abase + SEP, [[2 * SCW, 128], [1, TC], [BLK, NST]]),
                                AX.X, ALU.add)
                v.drain()
                yred_done[(ch, j)] = c.vv.tick(v)
                yrf = apx(sb_ap(b.yr), 0, [[TC, 128], [1, TC]])
                v.scalar_tensor_tensor(yrf, xisj, m["Dp"][:, j:j + 1], yrf,
                                       ALU.mult, ALU.add)
                v.drain()
                yasm_done[(ch, j)] = c.vv.tick(v)
                prev = (ch, j - 2) if j >= 2 else (ch - 1, j + 14)
                if prev in yg_dma:
                    c.qY.wait(v)
                zsj = apx(sb_ap(b.zs), j * TC, [[16 * TC, 128], [1, TC]])
                if pref == "fwd":
                    ygj = apx(sb_ap(b.yg), (j % 2) * TC, [[2 * TC, 128], [1, TC]])
                else:
                    ygj = apx(sb_ap(b.yg), (j % 2) * TC + TC - 1, [[2 * TC, 128], [-1, TC]])
                v.tensor_mul(ygj, yrf, zsj)
                v.drain()
                gate_done[(ch, j)] = c.vv.tick(v)

            @blk.gpsimd
            def _(g, ch=ch, j=j):
                c.vv.wait(g, gate_done[(ch, j)])
                gcol = ch * TC if pref == "fwd" else T - (ch + 1) * TC
                yg_dma[(ch, j)] = dmac(c, c.qY, g,
                    out=apx(yg_dst, (j * 128) * T + gcol, [[T, 128], [1, TC]]),
                    in_=apx(sb_ap(b.yg), (j % 2) * TC, [[2 * TC, 128], [1, TC]]))
                if f"{pref}_delta" in c.taps:
                    dmac(c, c.qG, g,
                         out=apx(c.ds[f"{pref}_delta"], (j * 128) * T + ch * TC,
                                 [[T, 128], [1, TC]]),
                         in_=apx(sb_ap(b.delta), (j % 2) * TC, [[2 * TC, 128], [1, TC]]))
                if f"{pref}_yr" in c.taps:
                    yrf2 = apx(sb_ap(b.yr), 0, [[TC, 128], [1, TC]])
                    dmac(c, c.qG, g,
                         out=apx(c.ds[f"{pref}_yr"], (j * 128) * T + ch * TC,
                                 [[T, 128], [1, TC]]),
                         in_=yrf2)

        if f"{pref}_xi" in c.taps or f"{pref}_z" in c.taps:
            @blk.gpsimd
            def _(g, ch=ch):
                c.sv.wait(g)
                c.vv.wait(g)
                if f"{pref}_xi" in c.taps:
                    dmac(c, c.qG, g,
                         out=apx(c.ds[f"{pref}_xi"], ch * TC,
                                 [[T, 128], [128 * T, 16], [1, TC]]),
                         in_=sb_ap(b.xis).rearrange("p (j t) -> p j t", j=16))
                if f"{pref}_z" in c.taps:
                    dmac(c, c.qG, g,
                         out=apx(c.ds[f"{pref}_z"], ch * TC,
                                 [[T, 128], [128 * T, 16], [1, TC]]),
                         in_=sb_ap(b.zs).rearrange("p (j t) -> p j t", j=16))


def phase2(c):
    nc, blk, sbuf = c.nc, c.block, c.sbuf
    p2 = ExitStack()
    sb = lambda n, sh, dt: p2.enter_context(nc.sbuf_tensor(n, list(sh), dt))
    ygf = sb("p2_ygf", [128, 16 * TC], BF16)
    ygb = sb("p2_ygb", [128, 16 * TC], BF16)
    ao = sb("p2_ao", [128, 8 * TC], F32)
    aosq = sb("p2_aosq", [128, 8 * TC], BF16)
    xt = sb("p2_x", [128, 8 * TC], F32)
    xf0 = sb("p2_xf", [128, 8 * TC], F32)
    stat = sb("p2_stat", [1, 4 * TC], F32)
    rep = sb("p2_rep", [128, 2 * TC], F32)
    wring = sb("p2_wring", [128, 32 * 128], BF16)
    psl = c.psl

    marks = [(c.mm, c.mm.n), (c.vv, c.vv.n), (c.sv, c.sv.n), (c.qWb, c.qWb.n),
             (c.qV, c.qV.n), (c.qS, c.qS.n), (c.qG, c.qG.n), (c.qY, c.qY.n)]

    @blk.vector
    def _(v):
        for sm, n in marks:
            v.wait_ge(sm.h, n)
        c.vv.tick(v)
    p2_zero = c.vv.n

    for eng_name in ("scalar", "sync", "gpsimd", "tensor"):
        @getattr(blk, eng_name)
        def _(e):
            c.vv.wait(e, p2_zero)

    wb_f = c.ds["wb_fwd_out_w"]
    wb_b = c.ds["wb_bwd_out_w"]
    ring_load = [None] * 32
    ring_use = [None] * 32
    ow_mm = {}
    evacV = {}
    evacS = {}
    ygld = [None] * NCH
    s1_mm = [None] * NCH
    apply_done = [None] * NCH
    xf_done = [None] * NCH
    x2_stored = [None] * NCH
    MU, VA, RS, MR = 0, TC, 2 * TC, 3 * TC

    def srow(off, n=TC):
        return apx(sb_ap(stat), off, [[4 * TC, 1], [1, n]])

    for r in range(NCH):
        @blk.sync
        def _(sync, r=r):
            if r == 0:
                c.qY.wait(sync)
                c.gg.wait(sync, c.precast[wb_f.tensor.name])
                c.gg.wait(sync, c.precast[wb_b.tensor.name])
            else:
                c.mm.wait(sync, ow_mm[(r - 1, 7)])
                c.vv.wait(sync, xf_done[r - 1])
            for src, dstt in ((c.ds["fwd_yg"], ygf), (c.ds["bwd_yg"], ygb)):
                dmac(c, c.qV, sync,
                     out=sb_ap(dstt).rearrange("p (k t) -> p k t", k=16),
                     in_=apx(src, r * TC, [[T, 128], [128 * T, 16], [1, TC]]))
            ygld[r] = dmac(c, c.qV, sync,
                           out=sb_ap(xt).rearrange("p (k t) -> p k t", k=8),
                           in_=apx(c.di["x_t"], r * TC, [[T, 128], [128 * T, 8], [1, TC]]))

        for m in range(8):
            @blk.sync
            def _(sync, r=r, m=m):
                for di_, wsrc in enumerate((wb_f, wb_b)):
                    for k in range(16):
                        slot = (di_ * 16 + k) % 32
                        if ring_use[slot] is not None:
                            c.mm.wait(sync, ring_use[slot])
                        ring_load[slot] = dmac(
                            c, c.qWb, sync,
                            out=wring[:, slot * 128:(slot + 1) * 128],
                            in_=apx(wsrc, (k * 128) * D + m * 128,
                                    [[D, 128], [1, 128]]))

            @blk.tensor
            def _(t, r=r, m=m):
                c.qWb.wait(t)
                c.qV.wait(t, ygld[r])
                prev = (r, m - 4) if m >= 4 else (r - 1, m + 4)
                if prev in evacV:
                    c.vv.wait(t, evacV[prev])
                    c.sv.wait(t, evacS[prev])
                last = None
                i = 0
                for di_, ygt in enumerate((ygf, ygb)):
                    for k in range(16):
                        slot = (di_ * 16 + k) % 32
                        last = t.matmul(
                            psl(m % 4),
                            wring[:, slot * 128:(slot + 1) * 128],
                            apx(sb_ap(ygt), k * TC, [[16 * TC, 128], [1, TC]]),
                            start=(i == 0), stop=(i == 31))
                        ring_use[slot] = c.mm.n + 1
                        i += 1
                ow_mm[(r, m)] = c.mm.inc(last)

            @blk.vector
            def _(v, r=r, m=m):
                c.mm.wait(v, ow_mm[(r, m)])
                v.tensor_copy(apx(sb_ap(ao), m * TC, [[8 * TC, 128], [1, TC]]),
                              psl(m % 4))
                v.drain()
                evacV[(r, m)] = c.vv.tick(v)

            @blk.scalar
            def _(s, r=r, m=m):
                c.mm.wait(s, ow_mm[(r, m)])
                s.activation(apx(sb_ap(aosq), m * TC, [[8 * TC, 128], [1, TC]]),
                             psl(m % 4), AF.Square, bias=0.0, scale=1.0)
                s.drain()
                evacS[(r, m)] = c.sv.tick(s)

        @blk.tensor
        def _(t, r=r):
            for m in range(8):
                c.vv.wait(t, evacV[(r, m)])
                c.sv.wait(t, evacS[(r, m)])
            last = None
            for k in range(8):
                last = t.matmul(apx(c.ps[2].ap(), 0, [[1024, 1], [1, TC]]),
                                c.ones_f[:],
                                apx(sb_ap(ao), k * TC, [[8 * TC, 128], [1, TC]]),
                                start=(k == 0), stop=(k == 7))
            for k in range(8):
                last = t.matmul(apx(c.ps[2].ap(), 512, [[1024, 1], [1, TC]]),
                                c.ones_bf[:],
                                apx(sb_ap(aosq), k * TC, [[8 * TC, 128], [1, TC]]),
                                start=(k == 0), stop=(k == 7))
            s1_mm[r] = c.mm.inc(last)

        def emit_ln_rows(r, mm_cnt, which):
            @blk.vector
            def _(v, r=r, mm_cnt=mm_cnt):
                c.mm.wait(v, mm_cnt)
                v.tensor_scalar_mul(srow(MU), apx(c.ps[2].ap(), 0, [[1024, 1], [1, TC]]),
                                    1.0 / D)
                v.tensor_scalar_mul(srow(VA), apx(c.ps[2].ap(), 512, [[1024, 1], [1, TC]]),
                                    1.0 / D)
                v.drain()
                v.tensor_mul(srow(RS), srow(MU), srow(MU))
                v.drain()
                v.tensor_sub(srow(VA), srow(VA), srow(RS))
                v.drain()
                c.vv.tick(v)
            rows1 = c.vv.n

            @blk.scalar
            def _(s, rows1=rows1):
                c.vv.wait(s, rows1)
                s.activation(srow(RS), srow(VA), AF.Ln, bias=1e-5, scale=1.0)
                s.drain()
                s.activation(srow(RS), srow(RS), AF.Exp, bias=0.0, scale=-0.5)
                s.drain()
                c.sv.tick(s)
            rows2 = c.sv.n

            @blk.vector
            def _(v, rows2=rows2):
                c.sv.wait(v, rows2)
                v.tensor_mul(srow(MR), srow(MU), srow(RS))
                v.drain()
                c.vv.tick(v)
            rows3 = c.vv.n

            @blk.scalar
            def _(s, rows3=rows3):
                c.vv.wait(s, rows3)
                c.qG.wait(s)
                dmac(c, c.qS, s, out=apx(c.ds["row"], 0, [[TC, 2], [1, TC]]),
                     in_=apx(sb_ap(stat), RS, [[4 * TC, 1], [TC, 2], [1, TC]]))

            @blk.gpsimd
            def _(g):
                c.qS.wait(g)
                dmac(c, c.qG, g, out=sb_ap(rep).rearrange("p (a t) -> p a t", a=2),
                     in_=apx(c.ds["row"], 0, [[0, 128], [TC, 2], [1, TC]]))
            return c.qG.n

        rep1 = emit_ln_rows(r, s1_mm[r], "ao")

        @blk.vector
        def _(v, r=r, rep1=rep1):
            c.qG.wait(v, rep1)
            rrep = apx(sb_ap(rep), 0, [[2 * TC, 128], [1, TC]])
            mrep = apx(sb_ap(rep), TC, [[2 * TC, 128], [1, TC]])
            for k in range(8):
                aok = apx(sb_ap(ao), k * TC, [[8 * TC, 128], [1, TC]])
                v.tensor_mul(aok, aok, rrep)
            v.drain()
            for k in range(8):
                aok = apx(sb_ap(ao), k * TC, [[8 * TC, 128], [1, TC]])
                xk = apx(sb_ap(xt), k * TC, [[8 * TC, 128], [1, TC]])
                v.tensor_sub(aok, aok, mrep)
            v.drain()
            for k in range(8):
                aok = apx(sb_ap(ao), k * TC, [[8 * TC, 128], [1, TC]])
                xk = apx(sb_ap(xt), k * TC, [[8 * TC, 128], [1, TC]])
                v.tensor_add(xk, xk, aok)
            v.drain()
            apply_done[r] = c.vv.tick(v)

        @blk.scalar
        def _(s, r=r):
            c.vv.wait(s, apply_done[r])
            x2_stored[r] = dmac(
                c, c.qS, s,
                out=apx(c.ds["x2"], r * TC, [[T, 128], [128 * T, 8], [1, TC]]),
                in_=sb_ap(xt).rearrange("p (k t) -> p k t", k=8))
            s.activation(aosq[:], xt[:], AF.Square, bias=0.0, scale=1.0)
            s.drain()
            c.sv.tick(s)
        x2sq = c.sv.n

        @blk.tensor
        def _(t, r=r, x2sq=x2sq):
            c.sv.wait(t, x2sq)
            last = None
            for k in range(8):
                last = t.matmul(apx(c.ps[2].ap(), 0, [[1024, 1], [1, TC]]),
                                c.ones_f[:],
                                apx(sb_ap(xt), k * TC, [[8 * TC, 128], [1, TC]]),
                                start=(k == 0), stop=(k == 7))
            for k in range(8):
                last = t.matmul(apx(c.ps[2].ap(), 512, [[1024, 1], [1, TC]]),
                                c.ones_bf[:],
                                apx(sb_ap(aosq), k * TC, [[8 * TC, 128], [1, TC]]),
                                start=(k == 0), stop=(k == 7))
            c.mm.inc(last)
        rep2 = emit_ln_rows(r, c.mm.n, "x2")

        @blk.vector
        def _(v, r=r, rep2=rep2):
            c.qG.wait(v, rep2)
            rrep = apx(sb_ap(rep), 0, [[2 * TC, 128], [1, TC]])
            mrep = apx(sb_ap(rep), TC, [[2 * TC, 128], [1, TC]])
            for k in range(8):
                xk = apx(sb_ap(xt), k * TC, [[8 * TC, 128], [1, TC]])
                fk = apx(sb_ap(xf0), k * TC, [[8 * TC, 128], [1, TC]])
                v.tensor_sub(fk, xk, mrep)
            v.drain()
            for k in range(8):
                fk = apx(sb_ap(xf0), k * TC, [[8 * TC, 128], [1, TC]])
                v.tensor_mul(fk, fk, rrep)
            v.drain()
            xf_done[r] = c.vv.tick(v)

        @blk.scalar
        def _(s, r=r):
            c.vv.wait(s, xf_done[r])
            dmac(c, c.qS, s,
                 out=apx(c.ds["xf"], r * TC, [[T, 128], [128 * T, 8], [1, TC]]),
                 in_=sb_ap(xf0).rearrange("p (k t) -> p k t", k=8))
            s.drain()
            c.sv.tick(s)

    p2.close()
    _emit_taps(c, ["x2"])


def phase3(c):
    nc, blk, sbuf = c.nc, c.block, c.sbuf
    p3 = ExitStack()
    sb = lambda n, sh, dt: p3.enter_context(nc.sbuf_tensor(n, list(sh), dt))
    f = Ctx()
    f.xfw = sb("f_xfw", [128, 8 * WIN], F32)
    f.xfb = sb("f_xfb", [128, 8 * (WIN + 1)], BF16)
    f.xfb2 = sb("f_xfb2", [128, 8 * (WIN + 1)], BF16)
    f.sq = sb("f_sq", [128, 8 * WIN], BF16)
    f.qro = sb("f_qro", [128, 4 * WIN], BF16)
    f.kro = [sb("f_kro0", [128, 4 * WIN], BF16), sb("f_kro1", [128, 4 * WIN], BF16)]
    f.vT = [sb("f_vT0", [128, 16 * 130], BF16), sb("f_vT1", [128, 16 * 130], BF16)]
    f.tq = sb("f_tq", [128, 2 * WIN], F32)
    f.vs = sb("f_vs", [128, WIN], BF16)
    f.rot = sb("f_rot", [128, 4 * WIN], F32)
    f.maskT = sb("f_maskT", [128, 8 * WIN], BF16)
    f.qa = sb("f_qa", [65, WIN], BF16)
    f.ka = [sb("f_ka0", [65, WIN], BF16), sb("f_ka1", [65, WIN], BF16)]
    f.pT = sb("f_pT", [128, 8 * WIN], BF16)
    f.srep = sb("f_srep", [64, WIN], F32)
    f.serow = sb("f_serow", [1, WIN], F32)
    f.ato = sb("f_ato", [128, 4 * WIN], BF16)
    f.xin = sb("f_xin", [128, 8 * WIN], BF16)
    f.h1 = sb("f_h1", [128, 22 * WIN], BF16)
    f.gg = sb("f_gg", [128, 2 * WIN], BF16)
    f.wring = sb("f_wring", [128, 32 * 128], BF16)
    f.wo = sb("f_wo", [128, 4 * 1024], BF16)
    f.rstdb = sb("f_rstdb", [32, WIN], BF16)
    f.nrm = sb("f_nrm", [32, WIN], F32)
    f.rmneg = sb("f_rmneg", [128, 8], F32)
    f.rows = sb("f_rows", [1, 6 * (WIN + 1)], F32)
    f.rep = sb("f_rep", [128, 2 * WIN], F32)
    f.prevc1 = sb("f_prevc1", [128, 8], BF16)
    f.prevc2 = sb("f_prevc2", [128, 8], BF16)
    f.qs8 = sb("f_qs8", [128, 4], F32)
    f.b1c = sb("f_b1c", [128, 44], F32)
    f.b1cs = sb("f_b1cs", [128, 22], F32)
    f.gb = sb("f_gb", [128, 2 * WIN], F32)
    f.b2c = sb("f_b2c", [128, 8], F32)
    f.lng = sb("f_lng", [128, 8], F32)
    f.lnb = sb("f_lnb", [128, 8], F32)
    psl = c.psl

    marks = [(c.mm, c.mm.n), (c.vv, c.vv.n), (c.sv, c.sv.n), (c.qW, c.qW.n),
             (c.qWb, c.qWb.n), (c.qWc, c.qWc.n), (c.qWd, c.qWd.n),
             (c.qV, c.qV.n), (c.qS, c.qS.n), (c.qG, c.qG.n), (c.qY, c.qY.n)]

    @blk.vector
    def _(v):
        for sm, n in marks:
            v.wait_ge(sm.h, n)
        v.memset(f.ka[0][:], 0.0)
        v.memset(f.ka[1][:], 0.0)
        v.memset(f.vT[0][:], 0.0)
        v.memset(f.vT[1][:], 0.0)
        v.memset(f.rstdb[:], 0.0)
        v.memset(f.nrm[:], 0.0)
        v.drain()
        v.memset(f.ka[0][64:65, :], 1.0)
        v.memset(f.ka[1][64:65, :], 1.0)
        for i in range(2):
            v.memset(apx(sb_ap(f.vT[i]), 64, [[16 * 130, 128], [130, 16], [1, 1]]), 1.0)
            v.memset(apx(sb_ap(f.vT[i]), 129, [[16 * 130, 128], [130, 16], [1, 1]]), 1.0)
        v.drain()
        c.vv.tick(v)
    p3_zero = c.vv.n

    for eng_name in ("scalar", "sync", "gpsimd", "tensor"):
        @getattr(blk, eng_name)
        def _(e):
            c.vv.wait(e, p3_zero)

    @blk.sync
    def _(sync):
        dmac(c, c.qWb, sync, out=f.maskT.ap().rearrange("p (i t) -> p i t", i=8),
             in_=c.di["c_maskT"].rearrange("(i p) t -> p i t", p=128))
        c.qWb.wait(sync)
        c.maskT_ok = c.gg.tick(sync)

    for l in range(min(2, _FFN_LAYERS)):
        _ffn_layer(c, f, l)
    p3.close()
    _emit_taps(c, ["l0_qrot", "l0_krot", "l0_ao", "l0_xa", "l0_xin", "l0_xf", "xf"])


def _ffn_layer(c, f, l):
    nc, blk = c.nc, c.block
    psl = c.psl
    f.qw = c.qWc if l == 0 else c.qWd
    wqkv = c.ds[f"wb_l{l}_wqkv"]
    w1p = c.ds[f"wb_l{l}_w1p"]
    w2p = c.ds[f"wb_l{l}_w2p"]
    wo_src = c.ds[f"wb_l{l}_wo"]

    marks = [(c.mm, c.mm.n), (c.vv, c.vv.n), (c.sv, c.sv.n), (c.qW, f.qw.n),
             (c.qV, c.qV.n), (c.qS, c.qS.n), (c.qG, c.qG.n)]

    @blk.vector
    def _(v):
        for sm, n in marks:
            v.wait_ge(sm.h, n)
        v.memset(f.prevc1[:], 0.0)
        v.memset(f.prevc2[:], 0.0)
        v.memset(f.rows[:], 0.0)
        v.drain()
        c.vv.tick(v)
    l_zero = c.vv.n

    @blk.sync
    def _(sync):
        c.vv.wait(sync, l_zero)
        c.gg.wait(sync, c.precast[wo_src.tensor.name])
        dmac(c, f.qw, sync, out=f.wo.ap().rearrange("p (k m) -> p k m", k=4),
             in_=wo_src.rearrange("(k p) m -> p k m", p=128))
        for col, name in ((0, f"l{l}_q_scale"), (2, f"l{l}_k_scale")):
            src = c.di[name]
            for hb in range(2):
                dmac(c, f.qw, sync, out=f.qs8[hb * 64:hb * 64 + 64, col:col + 1],
                     in_=apx(src, 0, [[1, 64], [1, 1]]))
                dmac(c, f.qw, sync,
                     out=f.qs8[hb * 64:hb * 64 + 32, col + 1:col + 2],
                     in_=apx(src, 32, [[1, 32], [1, 1]]))
                dmac(c, f.qw, sync,
                     out=f.qs8[hb * 64 + 32:hb * 64 + 64, col + 1:col + 2],
                     in_=apx(src, 0, [[1, 32], [1, 1]]))
        dmac(c, f.qw, sync, out=f.b1c[:, 0:44], in_=c.di[f"l{l}_b1p"])
        dmac(c, f.qw, sync, out=f.b2c[:, 0:8], in_=c.di[f"l{l}_b2"])
        dmac(c, f.qw, sync, out=f.lng[:, 0:8], in_=c.di[f"l{l}_ln_g"])
        dmac(c, f.qw, sync, out=f.lnb[:, 0:8], in_=c.di[f"l{l}_ln_b"])
    l_small = f.qw.n

    @blk.vector
    def _(v):
        f.qw.wait(v, l_small)
        v.tensor_scalar_mul(f.qs8[:, 0:2], f.qs8[:, 0:2], 8.0)
        v.drain()
        v.tensor_scalar_mul(f.b1cs[:], f.b1c[:, 22:44], 0.70710678)
        v.drain()
        c.vv.tick(v)
    l_qs = c.vv.n

    for w in range(min(NCH, _FFN_WINDOWS)):
        _ffn_window(c, f, l, w, l_qs)


def _stage_fast_begin(c, f):
    blk = c.block
    cm = blk.fast()
    cm.__enter__()

    @blk.vector
    def _(v):
        for sm in (c.mm, c.sv, c.gg, c.qG, c.qS, c.qV, c.qW, c.qWb,
                   c.qWc, c.qWd, c.qP, c.qY):
            sm.wait(v)
        c.vv.tick(v)
    gate = c.vv.n
    for en in ("sync", "scalar", "tensor", "gpsimd"):
        @getattr(blk, en)
        def _(e):
            c.vv.wait(e, gate)
    return cm


def _stage_fast_end(c, cm):
    cm.__exit__(None, None, None)


def _ffn_window(c, f, l, w, l_qs):
    nc, blk = c.nc, c.block
    psl = c.psl
    wqkv = c.ds[f"wb_l{l}_wqkv"]
    w1p = c.ds[f"wb_l{l}_w1p"]
    w2p = c.ds[f"wb_l{l}_w2p"]
    cur = w % 2
    prv = 1 - cur
    st = Ctx()

    @blk.sync
    def _(sync):
        if w > 0:
            c.vv.wait(sync, c._f_xfw_free)
        dmac(c, c.qV, sync,
             out=sb_ap(f.xfw).rearrange("p (k t) -> p k t", k=8),
             in_=apx(c.ds["xf"], w * WIN, [[T, 128], [128 * T, 8], [1, WIN]]))
        dmac(c, c.qV, sync,
             out=sb_ap(f.rot).rearrange("p (a t) -> p a t", a=4),
             in_=apx(c.di["c_rot"], w * WIN, [[T, 128], [128 * T, 4], [1, WIN]]))
    st.xf_ld = c.qV.n
    if _FFN_STOP == "loads":
        c._f_xfw_free = c.vv.n
        return

    @blk.vector
    def _(v):
        c.qV.wait(v, st.xf_ld)
        c.vv.wait(v, l_qs)
        for k in range(8):
            v.tensor_copy(
                apx(sb_ap(f.xfb), k * (WIN + 1), [[8 * (WIN + 1), 128], [1, 1]]),
                apx(sb_ap(f.prevc1), k, [[8, 128], [1, 1]]))
        v.drain()
        for k in range(8):
            v.tensor_copy(
                apx(sb_ap(f.xfb), k * (WIN + 1) + 1, [[8 * (WIN + 1), 128], [1, WIN]]),
                apx(sb_ap(f.xfw), k * WIN, [[8 * WIN, 128], [1, WIN]]))
        v.drain()
        for k in range(8):
            v.tensor_copy(
                apx(sb_ap(f.prevc1), k, [[8, 128], [1, 1]]),
                apx(sb_ap(f.xfw), k * WIN + WIN - 1, [[8 * WIN, 128], [1, 1]]))
        v.drain()
        st.xfb_ok = c.vv.tick(v)
    if _FFN_STOP == "xfb":
        c._f_xfw_free = c.vv.n
        return

    st.l_qs = l_qs
    if _SERIAL_P3 == "qkv":
        c.block.serial = True
    _cm = (_stage_fast_begin(c, f)
           if _SERIAL_P3 == "stages" and "qkv" in _STAGES_FAST else None)
    _qkv_stage(c, f, l, w, st)
    if _cm is not None:
        _stage_fast_end(c, _cm)
    if _SERIAL_P3 == "qkv":
        c.block.serial = False
    if _FFN_STOP == "":
        if _SERIAL_P3 == "attn":
            c.block.serial = True
        _cm = (_stage_fast_begin(c, f)
               if _SERIAL_P3 == "stages" and "attn" in _STAGES_FAST else None)
        _attn_stage(c, f, l, w, st)
        if _cm is not None:
            _stage_fast_end(c, _cm)
        if _SERIAL_P3 == "attn":
            c.block.serial = False
        if _FFN_STOP != "attn":
            _cm = (_stage_fast_begin(c, f)
                   if _SERIAL_P3 == "stages" and "woinner" in _STAGES_FAST else None)
            _wo_stage(c, f, l, w, st)
            if _FFN_STOP != "wo":
                if _SERIAL_P3 == "inner":
                    c.block.serial = True
                _inner_stage(c, f, l, w, st)
                if _SERIAL_P3 == "inner":
                    c.block.serial = False
            if _cm is not None:
                _stage_fast_end(c, _cm)
            else:
                st.wo_done = c.vv.n
        else:
            st.wo_done = c.vv.n
    else:
        st.wo_done = c.vv.n

    if l == 0 and any(t in c.taps for t in ("l0_xa", "l0_xin", "l0_xf")):
        @blk.gpsimd
        def _(g):
            c.vv.wait(g)
            c.sv.wait(g)
            if "l0_xin" in c.taps:
                dmac(c, c.qG, g,
                     out=apx(c.ds["l0_xin"], w * WIN, [[T, 128], [128 * T, 8], [1, WIN]]),
                     in_=sb_ap(f.xin).rearrange("p (k t) -> p k t", k=8))
            if "l0_xf" in c.taps:
                dmac(c, c.qG, g,
                     out=apx(c.ds["l0_xf"], w * WIN, [[T, 128], [128 * T, 8], [1, WIN]]),
                     in_=sb_ap(f.xfw).rearrange("p (k t) -> p k t", k=8))

    @blk.scalar
    def _(s):
        c.vv.wait(s)
        st.xf_st = dmac(
            c, c.qS, s,
            out=apx(c.ds["xf"], w * WIN, [[T, 128], [128 * T, 8], [1, WIN]]),
            in_=sb_ap(f.xfw).rearrange("p (k t) -> p k t", k=8))
        s.drain()
        c.sv.tick(s)
    @blk.vector
    def _(v):
        c.qS.wait(v)
        c.qG.wait(v)
        c.vv.tick(v)
    c._f_xfw_free = c.vv.n


def _qkv_stage(c, f, l, w, st):
    nc, blk = c.nc, c.block
    psl = c.psl
    wqkv = c.ds[f"wb_l{l}_wqkv"]
    cur = w % 2
    morder = [0, 4, 1, 5, 2, 6, 3, 7, 8, 12, 9, 13, 10, 14, 11, 15,
              16, 17, 18, 19]
    ring_load = [None] * 32
    ring_use = [None] * 32
    mmc = {}
    freed = {}

    def rhs_ap(k):
        off = k * (WIN + 1) + (1 if k < 4 else 0)
        return apx(sb_ap(f.xfb), off, [[8 * (WIN + 1), 128], [1, WIN]])

    for wave in range(5):
        wm = morder[wave * 4:wave * 4 + 4]

        @blk.sync
        def _(sync, wm=wm, wave=wave):
            if wave == 0:
                c.mm.wait(sync)
                c.vv.wait(sync, st.l_qs)
                if w == 0:
                    c.gg.wait(sync, c.precast[wqkv.tensor.name])
            for i, m in enumerate(wm):
                for k in range(8):
                    slot = (i * 8 + k) % 32
                    if ring_use[slot] is not None:
                        c.mm.wait(sync, ring_use[slot])
                    ring_load[slot] = dmac(
                        c, f.qw, sync,
                        out=f.wring[:, slot * 128:(slot + 1) * 128],
                        in_=apx(wqkv, (k * 128) * QKV_M + m * 128,
                                [[QKV_M, 128], [1, 128]]))

        @blk.tensor
        def _(t, wm=wm, wave=wave):
            f.qw.wait(t)
            if wave == 0:
                c.vv.wait(t, st.xfb_ok)
            for i, m in enumerate(wm):
                sl = (wave * 4 + i) % 4
                for eng, cnt in freed.get(sl, []):
                    (c.vv if eng == "v" else c.sv).wait(t, cnt)
                last = None
                for k in range(8):
                    slot = (i * 8 + k) % 32
                    last = t.matmul(psl(sl), f.wring[:, slot * 128:(slot + 1) * 128],
                                    rhs_ap(k), start=(k == 0), stop=(k == 7))
                    ring_use[slot] = c.mm.n + 1
                mmc[m] = c.mm.inc(last)

        if _FFN_STOP == "qkvmm":
            continue
        if wave < 4:
            is_k = wm[0] >= 8
            for pi in range(2):
                j = wm[pi * 2] if not is_k else wm[pi * 2] - 8
                mq, mp = wm[pi * 2], wm[pi * 2 + 1]
                _tog = (_QKV_PAIRS_SERIAL and not c.block.serial)
                if _tog:
                    c.block.serial = True
                _norm_rotary_pair(c, f, l, w, st, j, is_k, mq, mp, mmc, freed, psl)
                if _tog:
                    c.block.serial = False
        else:
            if _FFN_STOP == "pairs":
                continue
            for i, m in enumerate(wm):
                _tog = (_QKV_VT_SERIAL and not c.block.serial)
                if _tog:
                    c.block.serial = True
                _v_transpose(c, f, l, w, st, m - 16, m, mmc, freed, psl)
                if _tog:
                    c.block.serial = False

    st.qkv_done = (c.vv.n, c.sv.n)
    if l == 0 and ("l0_qrot" in c.taps or "l0_krot" in c.taps):
        @blk.gpsimd
        def _(g):
            c.vv.wait(g)
            c.sv.wait(g)
            if "l0_qrot" in c.taps:
                dmac(c, c.qG, g,
                     out=apx(c.ds["l0_qrot"], w * WIN, [[T, 128], [128 * T, 4], [1, WIN]]),
                     in_=sb_ap(f.qro).rearrange("p (j t) -> p j t", j=4))
            if "l0_krot" in c.taps:
                dmac(c, c.qG, g,
                     out=apx(c.ds["l0_krot"], w * WIN, [[T, 128], [128 * T, 4], [1, WIN]]),
                     in_=sb_ap(f.kro[cur]).rearrange("p (j t) -> p j t", j=4))


def _norm_rotary_pair(c, f, l, w, st, j, is_k, mq, mp, mmc, freed, psl):
    nc, blk = c.nc, c.block
    cur = w % 2
    morder = [0, 4, 1, 5, 2, 6, 3, 7, 8, 12, 9, 13, 10, 14, 11, 15, 16, 17, 18, 19]
    slq = morder.index(mq) % 4
    slp = morder.index(mp) % 4
    tq0 = apx(sb_ap(f.tq), 0, [[2 * WIN, 128], [1, WIN]])
    tq1 = apx(sb_ap(f.tq), WIN, [[2 * WIN, 128], [1, WIN]])
    sqt = apx(sb_ap(f.sq), 0, [[8 * WIN, 128], [1, WIN]])
    rows2 = apx(c.ps[2].ap(), 512, [[1024, 32], [1, WIN]])
    repp = apx(c.ps[3].ap(), 0, [[1024, 128], [1, WIN]])

    def emit(idx, eng, fn):
        tog = (idx >= _PAIR_SERIAL_FROM) and not c.block.serial
        if tog:
            c.block.serial = True
        getattr(blk, eng)(fn)
        if tog:
            c.block.serial = False

    def s0(v):
        c.mm.wait(v, mmc[mq])
        c.mm.wait(v, mmc[mp])
        v.tensor_copy(tq0, psl(slq))
        v.tensor_copy(tq1, psl(slp))
        v.drain()
        tick = c.vv.tick(v)
        freed[slp] = [("v", tick)]
        freed[slq] = [("v", tick)]
    emit(0, "vector", s0)
    cp_ok = c.vv.n

    def s1(sx):
        c.vv.wait(sx, cp_ok)
        sx.activation(sqt, tq0, AF.Square, bias=0.0, scale=1.0)
        sx.drain()
        c.sv.tick(sx)
    emit(1, "scalar", s1)
    sq_ok = c.sv.n
    if _FFN_STOP == "nr1":
        return

    def s2(t):
        c.sv.wait(t, sq_ok)
        if getattr(f, "last_nrm", None) is not None:
            c.vv.wait(t, f.last_nrm)
        if getattr(f, "last_rep_used", None) is not None:
            c.vv.wait(t, f.last_rep_used)
        ins = t.matmul(rows2, c.sel2[:], sqt, start=True, stop=True)
        c.mm.inc(ins)
    emit(2, "tensor", s2)
    ss_mm = c.mm.n

    def s3(v):
        c.mm.wait(v, ss_mm)
        v.tensor_scalar_max(f.nrm[0:2, :], apx(c.ps[2].ap(), 512, [[1024, 2], [1, WIN]]),
                            1e-24)
        v.drain()
        c.vv.tick(v)
    emit(3, "vector", s3)
    nrm_ok0 = c.vv.n
    f.last_nrm = nrm_ok0

    def s4(sx):
        c.vv.wait(sx, nrm_ok0)
        sx.activation(f.nrm[0:2, :], f.nrm[0:2, :], AF.Ln, bias=0.0, scale=1.0)
        sx.drain()
        sx.activation(f.nrm[0:2, :], f.nrm[0:2, :], AF.Exp, bias=0.0, scale=-0.5)
        sx.drain()
        c.sv.tick(sx)
    emit(4, "scalar", s4)
    nrm_ok = c.sv.n
    if _FFN_STOP == "nr2":
        return

    def s5(v):
        c.sv.wait(v, nrm_ok)
        v.tensor_copy(f.rstdb[0:2, :], f.nrm[0:2, :])
        v.drain()
        c.vv.tick(v)
    emit(5, "vector", s5)
    rstd_ok = c.vv.n
    if _FFN_STOP == "nr3":
        return

    def s6(t):
        c.vv.wait(t, rstd_ok)
        ins = t.matmul(repp, c.sel2t[:], f.rstdb[:], start=True, stop=True)
        c.mm.inc(ins)
    emit(6, "tensor", s6)
    rep_mm = c.mm.n
    if _FFN_STOP == "nr4":
        def s6b(v):
            c.mm.wait(v, rep_mm)
            f.last_rep_used = c.vv.tick(v)
        emit(7, "vector", s6b)
        return

    scol = 2 if is_k else 0
    rota = 2 if is_k else 0

    def s7(v):
        c.mm.wait(v, rep_mm)
        c.qV.wait(v, st.xf_ld)
        v.scalar_tensor_tensor(tq0, tq0, f.qs8[:, scol:scol + 1], repp,
                               ALU.mult, ALU.mult)
        v.drain()
        v.scalar_tensor_tensor(tq1, tq1, f.qs8[:, scol + 1:scol + 2], repp,
                               ALU.mult, ALU.mult)
        v.drain()
        f.last_rep_used = c.vv.tick(v)
        cs = apx(sb_ap(f.rot), rota * WIN, [[4 * WIN, 128], [1, WIN]])
        ss = apx(sb_ap(f.rot), (rota + 1) * WIN, [[4 * WIN, 128], [1, WIN]])
        v.tensor_mul(tq0, tq0, cs)
        v.tensor_mul(tq1, tq1, ss)
        v.drain()
        dst = f.kro[cur] if is_k else f.qro
        v.tensor_add(apx(sb_ap(dst), j * WIN, [[4 * WIN, 128], [1, WIN]]),
                     tq0, tq1)
        v.drain()
        c.vv.tick(v)
    emit(7, "vector", s7)


def _v_transpose(c, f, l, w, st, j, m, mmc, freed, psl):
    nc, blk = c.nc, c.block
    cur = w % 2
    morder = [0, 4, 1, 5, 2, 6, 3, 7, 8, 12, 9, 13, 10, 14, 11, 15, 16, 17, 18, 19]
    sl = morder.index(m) % 4
    ptr128 = apx(c.ps[3].ap().bitcast(BF16), 1024, [[2048, 128], [1, 128]])

    @blk.vector
    def _(v):
        c.mm.wait(v, mmc[m])
        v.tensor_copy(f.vs[:], psl(sl))
        v.drain()
        freed[sl] = [("v", c.vv.tick(v))]
    vs_ok = c.vv.n

    for tsub in range(4):
        @blk.tensor
        def _(t, tsub=tsub):
            c.vv.wait(t, vs_ok)
            if tsub > 0:
                c.vv.wait(t)
            ins = t.transpose(ptr128, f.vs[:, tsub * 128:(tsub + 1) * 128],
                              c.ident[:])
            c.mm.inc(ins)
        tmm = c.mm.n

        @blk.vector
        def _(v, tsub=tsub):
            c.mm.wait(v, tmm)
            tile = (tsub * 4 + j) * 130
            v.tensor_copy(
                apx(sb_ap(f.vT[cur]), tile, [[16 * 130, 128], [65, 2], [1, 64]]),
                apx(ptr128, 0, [[2048, 128], [64, 2], [1, 64]]))
            v.drain()
            c.vv.tick(v)


def _attn_stage(c, f, l, w, st):
    nc, blk = c.nc, c.block
    psl = c.psl
    cur = w % 2
    prv = 1 - cur
    halves = ([0, 1] if w > 0 else [1])
    rmax = c.ds["rmax"]
    sume = c.ds["sumexp"]
    pf = {}
    st.ato_ticks = []

    for h in range(HEADS):
        jt = h // 2
        hoff = (h % 2) * 64
        poff = hoff * 4 * WIN

        def qro_lhs(qs_):
            return apx(sb_ap(f.qro), poff + jt * WIN + qs_ * 128,
                       [[4 * WIN, 64], [1, 128]])

        def kro_rhs(hb):
            buf = f.kro[cur] if hb == 1 else f.kro[prv]
            return apx(sb_ap(buf), poff + jt * WIN, [[4 * WIN, 64], [1, WIN]])

        for qs_ in range(4):
            @blk.tensor
            def _(t, h=h, qs_=qs_):
                if h == 0 and qs_ == 0:
                    c.vv.wait(t)
                    c.sv.wait(t)
                for sl in (0, 1):
                    if sl in pf:
                        eng, cnt = pf[sl]
                        (c.vv if eng == "v" else c.sv).wait(t, cnt)
                last = None
                for hi, hb in enumerate(halves):
                    last = t.matmul(psl(hi if len(halves) == 2 else 1),
                                    qro_lhs(qs_), kro_rhs(hb),
                                    start=True, stop=True)
                c.mm.inc(last)
            mm1 = c.mm.n

            @blk.vector
            def _(v, h=h, qs_=qs_, mm1=mm1):
                c.mm.wait(v, mm1)
                if qs_ == 0:
                    c.qS.wait(v)
                if len(halves) == 2:
                    v.reduce_max(f.rmneg[:, 6:7], psl(0), AX.X)
                    v.reduce_max(f.rmneg[:, 7:8], psl(1), AX.X)
                    v.drain()
                    v.tensor_max(f.rmneg[:, qs_:qs_ + 1], f.rmneg[:, 6:7],
                                 f.rmneg[:, 7:8])
                else:
                    v.reduce_max(f.rmneg[:, qs_:qs_ + 1], psl(1), AX.X)
                v.drain()
                v.tensor_scalar_mul(f.rmneg[:, qs_:qs_ + 1],
                                    f.rmneg[:, qs_:qs_ + 1], -1.0)
                v.drain()
                tick = c.vv.tick(v)
                pf[0] = ("v", tick)
                pf[1] = ("v", tick)

        @blk.scalar
        def _(s, h=h):
            c.vv.wait(s)
            for qs_ in range(4):
                dmac(c, c.qS, s,
                     out=apx(rmax, ((w * 8 + h) * 4 + qs_) * 128,
                             [[1, 128], [1, 1]]),
                     in_=f.rmneg[:, qs_:qs_ + 1])
        rm_st = c.qS.n

        @blk.vector
        def _(v, h=h):
            if getattr(f, "prev_simT", None) is not None:
                c.mm.wait(v, f.prev_simT)
            v.tensor_copy(f.qa[0:64, :],
                          apx(sb_ap(f.qro), poff + jt * WIN, [[4 * WIN, 64], [1, WIN]]))
            for hb in halves:
                buf = f.kro[cur] if hb == 1 else f.kro[prv]
                v.tensor_copy(f.ka[hb][0:64, :],
                              apx(sb_ap(buf), poff + jt * WIN, [[4 * WIN, 64], [1, WIN]]))
            v.drain()
            c.vv.tick(v)
        qa_rows = c.vv.n

        @blk.gpsimd
        def _(g, h=h):
            c.qS.wait(g, rm_st)
            c.vv.wait(g, qa_rows)
            if getattr(f, "prev_simT", None) is not None:
                c.mm.wait(g, f.prev_simT)
            dmac(c, c.qG, g, out=f.qa[64:65, :],
                 in_=apx(rmax, (w * 8 + h) * 512, [[512, 1], [1, 512]]))
        qa_ok = c.qG.n

        mask_ticks = []
        for hb in halves:
            for ks_ in range(4):
                sl = 4 + ((ks_ + hb * 4) % 2)
                @blk.tensor
                def _(t, h=h, hb=hb, ks_=ks_, sl=sl):
                    c.qG.wait(t, qa_ok)
                    if sl in pf:
                        eng, cnt = pf[sl]
                        (c.vv if eng == "v" else c.sv).wait(t, cnt)
                    ins = t.matmul(psl(sl), f.ka[hb][:, ks_ * 128:(ks_ + 1) * 128],
                                   f.qa[:], start=True, stop=True)
                    c.mm.inc(ins)
                mmT = c.mm.n

                @blk.scalar
                def _(s, hb=hb, ks_=ks_, sl=sl, mmT=mmT):
                    c.mm.wait(s, mmT)
                    if getattr(f, "prev_av", None) is not None:
                        c.mm.wait(s, f.prev_av)
                    idx = hb * 4 + ks_
                    s.activation(
                        apx(sb_ap(f.pT), idx * WIN, [[8 * WIN, 128], [1, WIN]]),
                        psl(sl), AF.Exp, bias=0.0, scale=1.0)
                    s.drain()
                    pf[sl] = ("s", c.sv.tick(s))
                exp_ok = c.sv.n

                @blk.vector
                def _(v, hb=hb, ks_=ks_, exp_ok=exp_ok, h=h):
                    c.sv.wait(v, exp_ok)
                    if h == 0 and not mask_ticks:
                        c.gg.wait(v, c.maskT_ok)
                    idx = hb * 4 + ks_
                    pt = apx(sb_ap(f.pT), idx * WIN, [[8 * WIN, 128], [1, WIN]])
                    mk = apx(sb_ap(f.maskT), idx * WIN, [[8 * WIN, 128], [1, WIN]])
                    v.tensor_mul(pt, pt, mk)
                    v.drain()
                    mask_ticks.append(c.vv.tick(v))

        f.prev_simT = c.mm.n

        @blk.tensor
        def _(t, h=h):
            c.vv.wait(t, mask_ticks[-1])
            if 2 in pf:
                eng, cnt = pf[2]
                (c.vv if eng == "v" else c.sv).wait(t, cnt)
            last = None
            first = True
            for hb in halves:
                buf = f.vT[cur] if hb == 1 else f.vT[prv]
                for ks_ in range(4):
                    idx = hb * 4 + ks_
                    lhsT = apx(sb_ap(buf), (ks_ * 4 + jt) * 130 + (h % 2) * 65,
                               [[16 * 130, 128], [1, 65]])
                    rhs = apx(sb_ap(f.pT), idx * WIN, [[8 * WIN, 128], [1, WIN]])
                    last = t.matmul(apx(c.ps[1].ap(), 0, [[1024, 65], [1, WIN]]),
                                    lhsT, rhs, start=first,
                                    stop=(hb == halves[-1] and ks_ == 3))
                    first = False
            c.mm.inc(last)
        av_mm = c.mm.n
        f.prev_av = av_mm

        @blk.vector
        def _(v, h=h, av_mm=av_mm):
            c.mm.wait(v, av_mm)
            v.tensor_copy(f.serow[:], apx(c.ps[1].ap(), 64 * 1024, [[1024, 1], [1, WIN]]))
            v.drain()
            c.vv.tick(v)
        se_cp = c.vv.n

        @blk.scalar
        def _(s, h=h, se_cp=se_cp):
            c.vv.wait(s, se_cp)
            dmac(c, c.qS, s,
                 out=apx(sume, (w * 8 + h) * 512, [[512, 1], [1, 512]]),
                 in_=f.serow[:])
        se_st = c.qS.n

        @blk.gpsimd
        def _(g, h=h, se_st=se_st):
            c.qS.wait(g, se_st)
            if getattr(f, "srep_used", None) is not None:
                c.vv.wait(g, f.srep_used)
            dmac(c, c.qG, g, out=f.srep[:],
                 in_=apx(sume, (w * 8 + h) * 512, [[0, 64], [1, 512]]))
        srep_ok = c.qG.n

        @blk.vector
        def _(v, h=h, srep_ok=srep_ok):
            c.qG.wait(v, srep_ok)
            v.reciprocal(f.srep[:], f.srep[:])
            v.drain()
            out = apx(sb_ap(f.ato), poff + jt * WIN, [[4 * WIN, 64], [1, WIN]])
            v.tensor_mul(out, apx(c.ps[1].ap(), 0, [[1024, 64], [1, WIN]]),
                         f.srep[:])
            v.drain()
            tick = c.vv.tick(v)
            f.srep_used = tick
            pf[2] = ("v", tick)
            st.ato_ticks.append(tick)


def _wo_stage(c, f, l, w, st):
    nc, blk = c.nc, c.block
    psl = c.psl
    pf = {}
    evt = {}

    for m in range(8):
        @blk.tensor
        def _(t, m=m):
            if m == 0:
                c.vv.wait(t, st.ato_ticks[-1])
            if m - 2 in evt:
                c.vv.wait(t, evt[m - 2])
            last = None
            for k in range(4):
                last = t.matmul(
                    psl(m % 2),
                    f.wo[:, k * 1024 + m * 128:k * 1024 + m * 128 + 128],
                    apx(sb_ap(f.ato), k * WIN, [[4 * WIN, 128], [1, WIN]]),
                    start=(k == 0), stop=(k == 3))
            c.mm.inc(last)
        mmw = c.mm.n

        @blk.vector
        def _(v, m=m, mmw=mmw):
            c.mm.wait(v, mmw)
            xk = apx(sb_ap(f.xfw), m * WIN, [[8 * WIN, 128], [1, WIN]])
            v.tensor_add(xk, xk, psl(m % 2))
            v.drain()
            evt[m] = c.vv.tick(v)

    st.wo_done = evt[7]
    if l == 0 and "l0_xa" in c.taps:
        @blk.gpsimd
        def _(g):
            c.vv.wait(g, st.wo_done)
            dmac(c, c.qG, g,
                 out=apx(c.ds["l0_xa"], w * WIN, [[T, 128], [128 * T, 8], [1, WIN]]),
                 in_=sb_ap(f.xfw).rearrange("p (k t) -> p k t", k=8))
        st.xa_tap = c.qG.n


def _inner_stage(c, f, l, w, st):
    nc, blk = c.nc, c.block
    psl = c.psl
    R_S1H, R_S2H = 4 * (WIN + 1), 5 * (WIN + 1)
    MU, VA, RS, MR = 0, WIN + 1, 2 * (WIN + 1), 3 * (WIN + 1)

    def srow(off, n=WIN):
        return apx(sb_ap(f.rows), off, [[6 * (WIN + 1), 1], [1, n]])

    @blk.vector
    def _(v):
        c.vv.wait(v, st.wo_done)
        if getattr(st, "xa_tap", None) is not None:
            c.qG.wait(v)
        for k in range(8):
            v.tensor_copy(
                apx(sb_ap(f.xfb2), k * (WIN + 1), [[8 * (WIN + 1), 128], [1, 1]]),
                apx(sb_ap(f.prevc2), k, [[8, 128], [1, 1]]))
        v.drain()
        for k in range(8):
            v.tensor_copy(
                apx(sb_ap(f.xfb2), k * (WIN + 1) + 1, [[8 * (WIN + 1), 128], [1, WIN]]),
                apx(sb_ap(f.xfw), k * WIN, [[8 * WIN, 128], [1, WIN]]))
        v.drain()
        for k in range(8):
            v.tensor_copy(
                apx(sb_ap(f.prevc2), k, [[8, 128], [1, 1]]),
                apx(sb_ap(f.xfw), k * WIN + WIN - 1, [[8 * WIN, 128], [1, 1]]))
        v.drain()
        c.vv.tick(v)
    xfb2_ok = c.vv.n
    if _FFN_STOP == "i0a":
        return

    @blk.scalar
    def _(s):
        c.vv.wait(s, xfb2_ok)
        s.activation(f.sq[:], f.xfw[:], AF.Square, bias=0.0, scale=1.0)
        s.drain()
        c.sv.tick(s)
    sq_ok = c.sv.n

    @blk.tensor
    def _(t):
        c.sv.wait(t, sq_ok)
        c.vv.wait(t)
        last = None
        for half in range(2):
            for k in range(4):
                kk = half * 4 + k
                last = t.matmul(
                    apx(c.ps[2].ap(), half * 512, [[1024, 1], [1, WIN]]),
                    c.ones_f[:],
                    apx(sb_ap(f.xfw), kk * WIN, [[8 * WIN, 128], [1, WIN]]),
                    start=(k == 0), stop=(k == 3))
        for half in range(2):
            for k in range(4):
                kk = half * 4 + k
                last = t.matmul(
                    apx(c.ps[3].ap(), half * 512, [[1024, 1], [1, WIN]]),
                    c.ones_bf[:],
                    apx(sb_ap(f.sq), kk * WIN, [[8 * WIN, 128], [1, WIN]]),
                    start=(k == 0), stop=(k == 3))
        c.mm.inc(last)
    st_mm = c.mm.n

    @blk.vector
    def _(v):
        c.mm.wait(v, st_mm)
        v.tensor_copy(srow(R_S1H, 1), srow(R_S1H + WIN, 1))
        v.tensor_copy(srow(R_S2H, 1), srow(R_S2H + WIN, 1))
        v.drain()
        v.tensor_copy(srow(R_S1H + 1), apx(c.ps[2].ap(), 512, [[1024, 1], [1, WIN]]))
        v.tensor_copy(srow(R_S2H + 1), apx(c.ps[3].ap(), 512, [[1024, 1], [1, WIN]]))
        v.drain()
        v.tensor_add(srow(MU), apx(c.ps[2].ap(), 0, [[1024, 1], [1, WIN]]),
                     srow(R_S1H))
        v.tensor_add(srow(VA), apx(c.ps[3].ap(), 0, [[1024, 1], [1, WIN]]),
                     srow(R_S2H))
        v.drain()
        v.tensor_scalar_mul(srow(MU), srow(MU), 1.0 / D)
        v.tensor_scalar_mul(srow(VA), srow(VA), 1.0 / D)
        v.drain()
        v.tensor_mul(srow(RS), srow(MU), srow(MU))
        v.drain()
        v.tensor_sub(srow(VA), srow(VA), srow(RS))
        v.drain()
        c.vv.tick(v)
    rows1 = c.vv.n
    if _FFN_STOP == "i0c":
        return

    @blk.scalar
    def _(s):
        c.vv.wait(s, rows1)
        s.activation(srow(RS), srow(VA), AF.Ln, bias=1e-5, scale=1.0)
        s.drain()
        s.activation(srow(RS), srow(RS), AF.Exp, bias=0.0, scale=-0.5)
        s.drain()
        c.sv.tick(s)
    rows2_ok = c.sv.n

    @blk.vector
    def _(v):
        c.sv.wait(v, rows2_ok)
        v.tensor_mul(srow(MR), srow(MU), srow(RS))
        v.drain()
        c.vv.tick(v)
    rows3 = c.vv.n

    @blk.scalar
    def _(s):
        c.vv.wait(s, rows3)
        c.qG.wait(s)
        dmac(c, c.qS, s, out=apx(c.ds["row"], 0, [[WIN + 1, 2], [1, WIN]]),
             in_=apx(sb_ap(f.rows), RS, [[6 * (WIN + 1), 1], [WIN + 1, 2], [1, WIN]]))

    @blk.gpsimd
    def _(g):
        c.qS.wait(g)
        dmac(c, c.qG, g, out=sb_ap(f.rep).rearrange("p (a t) -> p a t", a=2),
             in_=apx(c.ds["row"], 0, [[0, 128], [WIN + 1, 2], [1, WIN]]))
    rep_ok = c.qG.n
    if _FFN_STOP == "i0d":
        return

    @blk.vector
    def _(v):
        c.qG.wait(v, rep_ok)
        rrep = apx(sb_ap(f.rep), 0, [[2 * WIN, 128], [1, WIN]])
        mrep = apx(sb_ap(f.rep), WIN, [[2 * WIN, 128], [1, WIN]])
        for k in range(8):
            off = k * (WIN + 1) + (1 if k < 4 else 0)
            xsh = apx(sb_ap(f.xfb2), off, [[8 * (WIN + 1), 128], [1, WIN]])
            tq0 = apx(sb_ap(f.tq), 0, [[2 * WIN, 128], [1, WIN]])
            xk = apx(sb_ap(f.xin), k * WIN, [[8 * WIN, 128], [1, WIN]])
            v.tensor_sub(tq0, xsh, mrep)
            v.drain()
            v.tensor_mul(tq0, tq0, rrep)
            v.drain()
            v.tensor_scalar(xk, tq0, f.lng[:, k:k + 1], f.lnb[:, k:k + 1],
                            ALU.mult, ALU.add)
            v.drain()
        c.vv.tick(v)
    xin_ok = c.vv.n
    if _FFN_STOP == "i1":
        return

    w1p = c.ds[f"wb_l{l}_w1p"]
    ring_load = [None] * 32
    ring_use = [None] * 32
    pf = {}
    gg_tick = {}
    h1_tick = {}
    NM = HIDP // 128
    morder = []
    for j in range(NM):
        morder += [NM + j, j]
    _fast = c.block.fast() if _SERIAL_P3 == True else None
    if _fast is not None:
        _fast.__enter__()
        @blk.vector
        def _(v):
            for sm in (c.mm, c.sv, c.gg, c.qG, c.qS, c.qV, c.qW, c.qWb,
                       c.qY, f.qw):
                sm.wait(v)
            v.wait_ge(c.vv.h, c.vv.n)
            c.vv.tick(v)
        xin_ok2 = c.vv.n
    else:
        xin_ok2 = xin_ok
    for wave in range(11):
        wm = morder[wave * 4:wave * 4 + 4]

        @blk.sync
        def _(sync, wm=wm, wave=wave):
            if wave == 0:
                c.mm.wait(sync)
                if w == 0:
                    c.gg.wait(sync, c.precast[w1p.tensor.name])
            for i, m in enumerate(wm):
                for k in range(8):
                    slot = (i * 8 + k) % 32
                    if ring_use[slot] is not None:
                        c.mm.wait(sync, ring_use[slot])
                    ring_load[slot] = dmac(
                        c, f.qw, sync,
                        out=f.wring[:, slot * 128:(slot + 1) * 128],
                        in_=apx(w1p, (k * 128) * (2 * HIDP) + m * 128,
                                [[2 * HIDP, 128], [1, 128]]))

        mmcs = {}

        @blk.tensor
        def _(t, wm=wm, wave=wave):
            f.qw.wait(t)
            if wave == 0:
                c.vv.wait(t, xin_ok)
            for i, m in enumerate(wm):
                sl = (wave * 4 + i) % 4
                if sl in pf:
                    eng, cnt = pf[sl]
                    (c.vv if eng == "v" else c.sv).wait(t, cnt)
                last = None
                for k in range(8):
                    slot = (i * 8 + k) % 32
                    last = t.matmul(psl(sl), f.wring[:, slot * 128:(slot + 1) * 128],
                                    apx(sb_ap(f.xin), k * WIN, [[8 * WIN, 128], [1, WIN]]),
                                    start=(k == 0), stop=(k == 7))
                    ring_use[slot] = c.mm.n + 1
                mmcs[m] = c.mm.inc(last)

        for i, m in enumerate(wm):
            sl = (wave * 4 + i) % 4
            if m >= NM:
                j = m - NM

                @blk.scalar
                def _(sx, j=j, sl=sl, m=m, mmcs=mmcs):
                    c.mm.wait(sx, mmcs[m])
                    if getattr(f, f"gg_used_{j % 2}", None) is not None:
                        c.vv.wait(sx, getattr(f, f"gg_used_{j % 2}"))
                    sx.activation(
                        apx(sb_ap(f.gg), (j % 2) * WIN, [[2 * WIN, 128], [1, WIN]]),
                        psl(sl), AF.Gelu, bias=f.b1c[:, NM + j:NM + j + 1],
                        scale=1.0)
                    sx.drain()
                    pf[sl] = ("s", c.sv.tick(sx))
                gg_tick[j] = c.sv.n
            else:
                j = m

                @blk.vector
                def _(vx, j=j, sl=sl, m=m, mmcs=mmcs):
                    c.mm.wait(vx, mmcs[m])
                    c.sv.wait(vx, gg_tick[j])
                    vx.scalar_tensor_tensor(
                        apx(sb_ap(f.h1), j * WIN, [[22 * WIN, 128], [1, WIN]]),
                        psl(sl), f.b1c[:, j:j + 1],
                        apx(sb_ap(f.gg), (j % 2) * WIN, [[2 * WIN, 128], [1, WIN]]),
                        ALU.add, ALU.mult)
                    vx.drain()
                    tick = c.vv.tick(vx)
                    pf[sl] = ("v", tick)
                    setattr(f, f"gg_used_{j % 2}", tick)
                h1_tick[j] = c.vv.n

    if _FFN_STOP == "i2":
        return
    w2p = c.ds[f"wb_l{l}_w2p"]
    ring_load2 = [None] * 32
    ring_use2 = [None] * 32
    pf2 = {}
    w2mm = {}
    for m in range(8):
        @blk.sync
        def _(sync, m=m):
            if m == 0:
                c.mm.wait(sync)
                if w == 0:
                    c.gg.wait(sync, c.precast[w2p.tensor.name])
            else:
                c.mm.wait(sync, w2mm[m - 1])
            for k in range(22):
                slot = (m * 22 + k) % 32
                if ring_use2[slot] is not None:
                    c.mm.wait(sync, ring_use2[slot])
                ring_load2[slot] = dmac(
                    c, f.qw, sync,
                    out=f.wring[:, slot * 128:(slot + 1) * 128],
                    in_=apx(w2p, (k * 128) * D + m * 128, [[D, 128], [1, 128]]))

        @blk.tensor
        def _(t, m=m):
            f.qw.wait(t)
            if m == 0:
                c.vv.wait(t, h1_tick[NM - 1])
            sl = m % 2
            if sl in pf2:
                c.vv.wait(t, pf2[sl])
            last = None
            for k in range(22):
                slot = (m * 22 + k) % 32
                last = t.matmul(psl(sl), f.wring[:, slot * 128:(slot + 1) * 128],
                                apx(sb_ap(f.h1), k * WIN, [[22 * WIN, 128], [1, WIN]]),
                                start=(k == 0), stop=(k == 21))
                ring_use2[slot] = c.mm.n + 1
            c.mm.inc(last)
        mmw = c.mm.n
        w2mm[m] = mmw

        @blk.vector
        def _(v, m=m, mmw=mmw):
            c.mm.wait(v, mmw)
            if m == 0 and getattr(st, "xa_tap", None) is not None:
                c.qG.wait(v)
            xk = apx(sb_ap(f.xfw), m * WIN, [[8 * WIN, 128], [1, WIN]])
            v.scalar_tensor_tensor(xk, psl(m % 2), f.b2c[:, m:m + 1], xk,
                                   ALU.add, ALU.add)
            v.drain()
            pf2[m % 2] = c.vv.tick(v)
    if _fast is not None:
        _fast.__exit__(None, None, None)
        @blk.vector
        def _(v):
            for sm in (c.mm, c.sv, f.qw, c.qW):
                sm.wait(v)
            c.vv.tick(v)


def phase4(c):
    nc, blk = c.nc, c.block
    p4 = ExitStack()
    sb = lambda n, sh, dt: p4.enter_context(nc.sbuf_tensor(n, list(sh), dt))
    xf = sb("p4_xf", [128, 8 * TC], F32)
    x2 = sb("p4_x2", [128, 8 * TC], F32)
    sq = sb("p4_sq", [128, 8 * TC], BF16)
    stat = sb("p4_stat", [1, 4 * TC], F32)
    rep = sb("p4_rep", [128, 2 * TC], F32)
    psl = c.psl
    MU, VA, RS, MR = 0, TC, 2 * TC, 3 * TC

    def srow(off):
        return apx(sb_ap(stat), off, [[4 * TC, 1], [1, TC]])

    marks = [(c.mm, c.mm.n), (c.vv, c.vv.n), (c.sv, c.sv.n), (c.qW, c.qW.n),
             (c.qWb, c.qWb.n), (c.qWc, c.qWc.n), (c.qWd, c.qWd.n),
             (c.qV, c.qV.n), (c.qS, c.qS.n), (c.qG, c.qG.n)]

    @blk.vector
    def _(v):
        for sm, n in marks:
            v.wait_ge(sm.h, n)
        c.vv.tick(v)
    p4_zero = c.vv.n

    for eng_name in ("scalar", "sync", "gpsimd", "tensor"):
        @getattr(blk, eng_name)
        def _(e):
            c.vv.wait(e, p4_zero)

    done = [None] * NCH
    for r in range(NCH):
        @blk.sync
        def _(sync, r=r):
            if r > 0:
                c.vv.wait(sync, done[r - 1])
            dmac(c, c.qV, sync,
                 out=sb_ap(xf).rearrange("p (k t) -> p k t", k=8),
                 in_=apx(c.ds["xf"], r * TC, [[T, 128], [128 * T, 8], [1, TC]]))
            dmac(c, c.qV, sync,
                 out=sb_ap(x2).rearrange("p (k t) -> p k t", k=8),
                 in_=apx(c.ds["x2"], r * TC, [[T, 128], [128 * T, 8], [1, TC]]))
        ld = c.qV.n

        @blk.scalar
        def _(s, r=r, ld=ld):
            c.qV.wait(s, ld)
            s.activation(sq[:], xf[:], AF.Square, bias=0.0, scale=1.0)
            s.drain()
            c.sv.tick(s)
        sq_ok = c.sv.n

        @blk.tensor
        def _(t, r=r, sq_ok=sq_ok):
            c.sv.wait(t, sq_ok)
            c.vv.wait(t)
            last = None
            for k in range(8):
                last = t.matmul(apx(c.ps[2].ap(), 0, [[1024, 1], [1, TC]]),
                                c.ones_f[:],
                                apx(sb_ap(xf), k * TC, [[8 * TC, 128], [1, TC]]),
                                start=(k == 0), stop=(k == 7))
            for k in range(8):
                last = t.matmul(apx(c.ps[2].ap(), 512, [[1024, 1], [1, TC]]),
                                c.ones_bf[:],
                                apx(sb_ap(sq), k * TC, [[8 * TC, 128], [1, TC]]),
                                start=(k == 0), stop=(k == 7))
            c.mm.inc(last)
        st_mm = c.mm.n

        @blk.vector
        def _(v, r=r, st_mm=st_mm):
            c.mm.wait(v, st_mm)
            v.tensor_scalar_mul(srow(MU), apx(c.ps[2].ap(), 0, [[1024, 1], [1, TC]]),
                                1.0 / D)
            v.tensor_scalar_mul(srow(VA), apx(c.ps[2].ap(), 512, [[1024, 1], [1, TC]]),
                                1.0 / D)
            v.drain()
            v.tensor_mul(srow(RS), srow(MU), srow(MU))
            v.drain()
            v.tensor_sub(srow(VA), srow(VA), srow(RS))
            v.drain()
            c.vv.tick(v)
        rows1 = c.vv.n

        @blk.scalar
        def _(s, rows1=rows1):
            c.vv.wait(s, rows1)
            s.activation(srow(RS), srow(VA), AF.Ln, bias=1e-5, scale=1.0)
            s.drain()
            s.activation(srow(RS), srow(RS), AF.Exp, bias=0.0, scale=-0.5)
            s.drain()
            c.sv.tick(s)
        rows2 = c.sv.n

        @blk.vector
        def _(v, rows2=rows2):
            c.sv.wait(v, rows2)
            v.tensor_mul(srow(MR), srow(MU), srow(RS))
            v.drain()
            c.vv.tick(v)
        rows3 = c.vv.n

        @blk.scalar
        def _(s, rows3=rows3):
            c.vv.wait(s, rows3)
            c.qG.wait(s)
            dmac(c, c.qS, s, out=apx(c.ds["row"], 0, [[TC, 2], [1, TC]]),
                 in_=apx(sb_ap(stat), RS, [[4 * TC, 1], [TC, 2], [1, TC]]))

        @blk.gpsimd
        def _(g):
            c.qS.wait(g)
            dmac(c, c.qG, g, out=sb_ap(rep).rearrange("p (a t) -> p a t", a=2),
                 in_=apx(c.ds["row"], 0, [[0, 128], [TC, 2], [1, TC]]))
        rep_ok = c.qG.n

        @blk.vector
        def _(v, r=r, rep_ok=rep_ok):
            c.qG.wait(v, rep_ok)
            rrep = apx(sb_ap(rep), 0, [[2 * TC, 128], [1, TC]])
            mrep = apx(sb_ap(rep), TC, [[2 * TC, 128], [1, TC]])
            for k in range(8):
                fk = apx(sb_ap(xf), k * TC, [[8 * TC, 128], [1, TC]])
                v.tensor_mul(fk, fk, rrep)
            v.drain()
            for k in range(8):
                fk = apx(sb_ap(xf), k * TC, [[8 * TC, 128], [1, TC]])
                v.tensor_sub(fk, fk, mrep)
            v.drain()
            for k in range(8):
                fk = apx(sb_ap(xf), k * TC, [[8 * TC, 128], [1, TC]])
                xk = apx(sb_ap(x2), k * TC, [[8 * TC, 128], [1, TC]])
                v.tensor_add(fk, fk, xk)
            v.drain()
            c.vv.tick(v)
        app = c.vv.n

        @blk.scalar
        def _(s, r=r, app=app):
            c.vv.wait(s, app)
            dmac(c, c.qS, s,
                 out=apx(c.out_t, r * TC, [[T, 128], [128 * T, 8], [1, TC]]),
                 in_=sb_ap(xf).rearrange("p (k t) -> p k t", k=8))
            s.drain()
            c.sv.tick(s)
        done[r] = c.vv.n

        @blk.vector
        def _(v, r=r):
            c.sv.wait(v)
            c.qS.wait(v)
            c.vv.tick(v)
        done[r] = c.vv.n

    p4.close()


def _finish(c):
    blk = c.block

    @blk.gpsimd
    def _(g):
        for sm in (c.qS, c.qG, c.qV, c.qW, c.qWb, c.qWc, c.qWd, c.qP, c.qY):
            g.wait_ge(sm.h, sm.n)
        c.gg.tick(g)

    @blk.vector
    def _(v):
        c.gg.wait(v)


def kernel(**inputs):
    per_core = _host_prep(inputs)
    nc = build_program()
    res = run_bass_kernel_spmd(nc, per_core, core_ids=list(range(len(per_core))))
    out = np.stack([np.ascontiguousarray(np.asarray(r["out_t"]).T)
                    for r in res.results])
    return out.astype(np.float32)
